# version 02
# baseline (speedup 1.0000x reference)
import math
import numpy as np
import concourse.bass as bass
import concourse.mybir as mybir
from concourse.bass_utils import run_bass_kernel_spmd

F32 = mybir.dt.float32
BF16 = mybir.dt.bfloat16
AX = mybir.AxisListType.X
ALU = mybir.AluOpType
ACT = mybir.ActivationFunctionType

H = 2048
SEQ = 4096
NT_SEQ = 32
NT_OWN = 8
NE = 6
EX = 1024
C_SHIFT = 10.0
SCALE = 1.0 / math.sqrt(128.0)
EPS = 1e-5

CFG = dict(heads=16, dump=False, phases="ABCDEF")
ENGINES = ("sync", "tensor", "vector", "scalar", "gpsimd")


class PB:
    def __init__(self, nc):
        self.nc = nc
        self.q = {e: [] for e in ENGINES}
        self.sem = {}
        self.cnt = {}

    def new_sem(self, name):
        self.sem[name] = self.nc.alloc_semaphore(name)
        self.cnt[name] = 0

    def emit(self, eng, fn, waits=(), inc=None, inc_n=1):
        tgt = None
        if inc is not None:
            self.cnt[inc] += inc_n
            tgt = self.cnt[inc]
        self.q[eng].append((fn, tuple(waits), inc, inc_n))
        return tgt

    def wait(self, eng, waits):
        self.q[eng].append((None, tuple(waits), None, 0))

    def run(self):
        sems = self.sem

        def body(eng_name):
            def _(eng_obj):
                for fn, waits, inc, inc_n in self.q[eng_name]:
                    for s, v in waits:
                        if v and v > 0:
                            eng_obj.wait_ge(sems[s], v)
                    if fn is not None:
                        inst = fn(eng_obj)
                        if inc is not None:
                            inst.then_inc(sems[inc], inc_n)
            return _

        with self.nc.Block() as blk:
            blk.sync(body("sync"))
            blk.tensor(body("tensor"))
            blk.vector(body("vector"))
            blk.scalar(body("scalar"))
            blk.gpsimd(body("gpsimd"))


def build():
    heads = CFG["heads"]
    dump = CFG["dump"]
    phases = CFG["phases"]

    nc = bass.Bass("TRN2", target_bir_lowering=False,
                   detect_race_conditions=False,
                   dynamic_dma_scratch_size=16384)

    di = {}

    def inp(name, shape):
        di[name] = nc.dram_tensor(name, list(shape), F32, kind="ExternalInput").ap()

    inp("x_own", (1024, H))
    inp("x_seq", (SEQ, H))
    inp("wq_r", (16, 128, 2048)); inp("wk_r", (128, 32768)); inp("wv_r", (128, 32768))
    inp("wo_r", (4, 128, 8192))
    inp("wg_in", (128, 256)); inp("bg_in", (1, 16))
    inp("wup_r", (NE, 8, 128, 2048)); inp("wsw_r", (NE, 8, 128, 1024))
    inp("wdn_r", (NE, 4, 128, 4096))
    inp("wsu_r2", (16, 128, 2048)); inp("wsa_r2", (128, 8192))
    inp("wsd_r2", (4, 128, 8192))
    inp("bup_r", (128, 48)); inp("bsw_r", (128, 48)); inp("bsu_r", (128, 16))
    inp("bsa_r", (128, 8)); inp("bdnsd", (7, H))
    inp("cmask", (2, 128, 512))

    out_own = nc.dram_tensor("out_own", [1024, H], F32, kind="ExternalOutput").ap()
    skind = "ExternalOutput" if dump else "Internal"
    kT_scr = nc.dram_tensor("kT_scr", [H, SEQ], BF16, kind=skind).ap()
    vT_scr = nc.dram_tensor("vT_scr", [SEQ, H], BF16, kind=skind).ap()
    xlt_scr = nc.dram_tensor("xlt_scr", [NT_SEQ, 128, H], BF16, kind=skind).ap()
    res_scr = nc.dram_tensor("res_scr", [1024, H], F32, kind=skind).ap()
    x2_scr = nc.dram_tensor("x2_scr", [1024, H], F32, kind=skind).ap()
    if dump:
        q_dump = nc.dram_tensor("q_dump", [128, 16384], BF16, kind="ExternalOutput").ap()
        at_dump = nc.dram_tensor("at_dump", [128, 16384], BF16, kind="ExternalOutput").ap()
        accd = [nc.dram_tensor(f"acc_dump{k}", [128, 16384], F32,
                               kind="ExternalOutput").ap() for k in range(3)]
        w_dump = nc.dram_tensor("w_dump", [128, 64], F32, kind="ExternalOutput").ap()
        sh_dump = nc.dram_tensor("sh_dump", [128, 16384], BF16, kind="ExternalOutput").ap()
        z_dump = nc.dram_tensor("z_dump", [128, 16384], BF16, kind="ExternalOutput").ap()
        sml_dump = nc.dram_tensor("sml_dump", [128, 128], F32, kind="ExternalOutput").ap()
        x_dump = nc.dram_tensor("x_dump", [4, 128, 2048], F32, kind="ExternalOutput").ap()
        xltb_dump = nc.dram_tensor("xltb_dump", [128, 16384], BF16, kind="ExternalOutput").ap()
        id_dump = nc.dram_tensor("id_dump", [128, 256], F32, kind="ExternalOutput").ap()
        xtt_dump = nc.dram_tensor("xtt_dump", [128, 16384], BF16, kind="ExternalOutput").ap()
        id_dump2 = nc.dram_tensor("id_dump2", [128, 128], F32, kind="ExternalOutput").ap()
        x_dump_v = [x_dump[k] for k in range(4)]

    sb = nc.alloc_sbuf_tensor
    ident_bf = sb("ident_bf", [128, 128], BF16)
    ident_f32 = sb("ident_f32", [128, 128], F32)
    onescol_bf = sb("onescol_bf", [128, 1], BF16)
    onesrow_f32 = sb("onesrow_f32", [1, 128], F32)
    negC = sb("negC", [128, 1], F32)
    c1e30 = sb("c1e30", [128, 1], F32)
    c1e4 = sb("c1e4", [128, 1], F32)
    wg_bf = sb("wg_bf", [128, 256], BF16)
    bg_sb = sb("bg_sb", [1, 16], F32)
    bup_sb = sb("bup_sb", [128, 48], F32)
    bsw_sb = sb("bsw_sb", [128, 48], F32)
    bsu_sb = sb("bsu_sb", [128, 16], F32)
    bsa_sb = sb("bsa_sb", [128, 8], F32)
    bdnsd_sb = sb("bdnsd_sb", [8, H], BF16)
    wcomb = sb("wcomb", [8, 1024], BF16)
    w_all = sb("w_all", [128, 64], F32)
    srl = sb("srl", [1, 128], F32)
    srlb = sb("srlb", [128, 128], F32)
    sml = sb("sml", [128, 160], F32)
    smlsnap = sb("smlsnap", [128, 128], F32) if dump else None
    tmp_bf = sb("tmp_bf", [128, 512], BF16)

    NBF = 59392
    BB = sb("BB", [128, NBF], BF16)
    NF = 20480
    FB = sb("FB", [128, NF], F32)
    assert nc.sbuf_bytes_remaining >= 0

    def bbv(off, n):
        return BB[:, off:off + n]

    def fbv(off, n):
        return FB[:, off:off + n]

    OFF_WKT = 0
    OFF_XLTA = 32768
    OFF_XLTR = 40960
    OFF_KVO = 49152
    OFF_Q = 0
    OFF_XLTB = 16384
    OFF_ATT = 16384
    OFF_KVH = 32768
    OFF_PT = 50176
    OFF_CMASK = 54272
    OFF_WSLAB = 50176
    OFF_XTT = 0
    OFF_WOQ = 32768
    OFF_R2 = 16384
    OFF_R3 = 32768
    OFF_R4 = 49152

    OFF_XLN_AB = 45056
    OFF_XLN_DE = 53248
    OFF_ACC = 0
    OFF_XF = 16384
    OFF_XA4 = 8192

    P = [nc.alloc_psum_tensor(f"P{i}", [128, 512], F32) for i in range(7)]
    PB7 = nc.alloc_psum_tensor("PB7", [128, 1024], BF16)

    pb = PB(nc)
    SYNC_SEMS = ["LB_H", "LX0", "LX1", "LX2", "LX3", "LK0", "LK1",
                 "SKV0", "SKV1", "SX0", "SX1", "SX2", "SX3", "SO"]
    GPS_SEMS = ["LB_G", "LWa0", "LWa1", "LWb0", "LWb1", "LWc0", "LWc1",
                "LWd0", "LWd1"]
    for s in SYNC_SEMS + GPS_SEMS + ["T", "S", "V", "G"]:
        pb.new_sem(s)
    sem_nums = [h.num for h in pb.sem.values()]
    clear_range = range(min(sem_nums), max(sem_nums) + 1)
    if CFG.get("preamble", True):
        nc.gpsimd.sem_clear(clear_range)
        nc.all_engine_barrier()

    def dma_sync(sem, dst, src, waits=()):
        return pb.emit("sync", lambda e, d=dst, s=src: e.dma_start(out=d, in_=s),
                       waits, sem, 16)

    def dma_cast(sem, dst, src, waits=()):
        return pb.emit("gpsimd", lambda e, d=dst, s=src: e.dma_start(out=d, in_=s),
                       waits, sem, 16)

    def dump_barrier():
        v = pb.cnt["SO"]
        for eng in ENGINES:
            pb.wait(eng, [("SO", v)])

    TEN, VEC, SCA = "tensor", "vector", "scalar"
    GPS = "gpsimd"

    pb.emit(GPS, lambda e: e.memset(ident_bf[:], 0.0))
    pb.emit(GPS, lambda e: e.affine_select(
        out=ident_bf[:], in_=ident_bf[:], compare_op=ALU.not_equal, fill=1.0,
        base=0, pattern=[[-1, 128]], channel_multiplier=1))
    pb.emit(GPS, lambda e: e.memset(ident_f32[:], 0.0))
    pb.emit(GPS, lambda e: e.affine_select(
        out=ident_f32[:], in_=ident_f32[:], compare_op=ALU.not_equal, fill=1.0,
        base=0, pattern=[[-1, 128]], channel_multiplier=1))
    pb.emit(GPS, lambda e: e.memset(onescol_bf[:], 1.0))
    pb.emit(GPS, lambda e: e.memset(onesrow_f32[:], 1.0))
    pb.emit(GPS, lambda e: e.memset(negC[:], -C_SHIFT))
    pb.emit(GPS, lambda e: e.memset(c1e30[:], 1.0e30))
    pb.emit(GPS, lambda e: e.memset(c1e4[:], 1.0e4))
    G_CONST = pb.emit(GPS, lambda e: e.memset(wcomb[:], 1.0), inc="G")

    cmask_bf = BB[:, OFF_CMASK:OFF_CMASK + 1024]
    dma_cast("LB_G", cmask_bf.rearrange("p (i j) -> p i j", i=2),
             di["cmask"].rearrange("i p j -> p i j"))
    LG_WGC = dma_cast("LB_G", wg_bf[:], di["wg_in"])
    LH_BIAS = 0
    for name, dst in (("bg_in", bg_sb[:]), ("bup_r", bup_sb[:]),
                      ("bsw_r", bsw_sb[:]), ("bsu_r", bsu_sb[:]),
                      ("bsa_r", bsa_sb[:])):
        LH_BIAS = dma_sync("LB_H", dst, di[name])
    LG_BDN = dma_cast("LB_G", bdnsd_sb[0:7, :], di["bdnsd"])

    wkT_sb = bbv(OFF_WKT, 32768)
    LG_WKT = 0
    if "A" in phases:
        for q4 in range(4):
            LG_WKT = dma_cast(
                "LB_G", bbv(OFF_WKT + q4 * 8192, 8192),
                di["wk_r"][:, q4 * 8192:(q4 + 1) * 8192])

    state = dict(pb7_free=0)

    LNDBG = dict(tile=None)
    fill_state = dict(k=0)

    def vfill(n=2):
        for _ in range(n):
            k = fill_state["k"] = (fill_state["k"] + 1) % 4
            pb.emit(VEC, lambda e, k=k: e.tensor_copy(
                sml[:, 151 + k:152 + k], sml[:, 150:151]))

    def emit_ln(xap, out_bf_ap, out_f32_ap, waits_v, par):
        o = par * 64
        for c in range(4):
            pb.emit(VEC, lambda e, c=c, o=o: e.bn_stats(
                sml[:, o + 6 * c:o + 6 * c + 6], xap[:, 512 * c:512 * (c + 1)]),
                waits_v if c == 0 else ())
        for c in range(4):
            pb.emit(VEC, lambda e, c=c, o=o: e.bn_aggr(
                sml[:, o + 24 + 2 * c:o + 26 + 2 * c], sml[:, o + 6 * c:o + 6 * c + 6]))
        mview = sml[:, o + 24:o + 32].rearrange("p (c t) -> p c t", t=2)[:, :, 0]
        vview = sml[:, o + 24:o + 32].rearrange("p (c t) -> p c t", t=2)[:, :, 1]
        vfill()
        pb.emit(VEC, lambda e: e.reduce_sum(sml[:, o + 32:o + 33], mview, AX))
        pb.emit(VEC, lambda e: e.tensor_mul(sml[:, o + 36:o + 40], mview, mview))
        pb.emit(VEC, lambda e: e.reduce_sum(sml[:, o + 41:o + 42], vview, AX))
        pb.emit(VEC, lambda e: e.tensor_scalar_mul(
            sml[:, o + 33:o + 34], sml[:, o + 32:o + 33], -0.25))
        vfill(1)
        pb.emit(VEC, lambda e: e.reduce_sum(
            sml[:, o + 40:o + 41], sml[:, o + 36:o + 40], AX))
        pb.emit(VEC, lambda e: e.tensor_mul(
            sml[:, o + 44:o + 45], sml[:, o + 33:o + 34], sml[:, o + 33:o + 34]))
        vfill(1)
        pb.emit(VEC, lambda e: e.tensor_add(
            sml[:, o + 42:o + 43], sml[:, o + 40:o + 41], sml[:, o + 41:o + 42]))
        vfill(2)
        pb.emit(VEC, lambda e: e.tensor_scalar_mul(
            sml[:, o + 43:o + 44], sml[:, o + 42:o + 43], 0.25))
        vfill(2)
        pb.emit(VEC, lambda e: e.tensor_sub(
            sml[:, o + 45:o + 46], sml[:, o + 43:o + 44], sml[:, o + 44:o + 45]))
        vfill(2)
        v_stats = pb.emit(VEC, lambda e: e.tensor_scalar_add(
            sml[:, o + 46:o + 47], sml[:, o + 45:o + 46], EPS), inc="V")
        pb.emit(SCA, lambda e: e.activation(
            sml[:, o + 47:o + 48], sml[:, o + 46:o + 47], ACT.Ln), [("V", v_stats)])
        s_r = pb.emit(SCA, lambda e: e.activation(
            sml[:, o + 48:o + 49], sml[:, o + 47:o + 48], ACT.Exp, scale=-0.5),
            inc="S")
        v_negmr = pb.emit(VEC, lambda e: e.tensor_mul(
            sml[:, o + 49:o + 50], sml[:, o + 33:o + 34],
            sml[:, o + 48:o + 49]), [("S", s_r)], inc="V")
        if dump and LNDBG["tile"] is not None:
            tdbg = LNDBG["tile"]
            vfill(2)
            pb.emit(VEC, lambda e, tdbg=tdbg: e.tensor_copy(
                smlsnap[:, tdbg * 4:tdbg * 4 + 2], sml[:, o + 48:o + 50]))
            pb.emit(VEC, lambda e, tdbg=tdbg: e.tensor_copy(
                smlsnap[:, tdbg * 4 + 2:tdbg * 4 + 4], sml[:, o + 45:o + 47]))
            if 2 <= tdbg <= 5:
                dma_sync("SO", x_dump_v[tdbg - 2], xap, [("V", pb.cnt["V"])])
            LNDBG["tile"] = None
        s_cast = pb.emit(SCA, lambda e: e.activation(
            out_bf_ap, xap, ACT.Identity, bias=sml[:, o + 49:o + 50],
            scale=sml[:, o + 48:o + 49]), [("V", v_negmr)], inc="S")
        if out_f32_ap is not None:
            s_cast = pb.emit(SCA, lambda e: e.activation(
                out_f32_ap, xap, ACT.Identity, bias=sml[:, o + 49:o + 50],
                scale=sml[:, o + 48:o + 49]), inc="S")
        return s_cast

    def emit_transpose16(src_fn, dst_off, waits_t):
        vlast = None
        for r in range(2):
            for k in range(8):
                fc = r * 8 + k
                w = []
                if k == 0:
                    w = list(waits_t) if r == 0 else []
                    w.append(("V", state["pb7_free"] if r == 0 else vlast))
                pb.emit(TEN, lambda e, fc=fc, k=k: e.transpose(
                    PB7[:, k * 128:(k + 1) * 128], src_fn(fc), ident_bf[:]),
                    w, inc="T" if k == 7 else None)
            t_done = pb.cnt["T"]
            vlast = pb.emit(VEC, lambda e, r=r: e.tensor_copy(
                bbv(dst_off + r * 1024, 1024), PB7[:]), [("T", t_done)], inc="V")
        state["pb7_free"] = vlast
        return vlast

    ST_XLT = {}
    A1_T_END = A2_T_END = 0
    if "A" in phases:
        xcast = {}
        ttrans = {}
        vcopy = {}
        kevac = {}
        kst = {}
        kmm_last = {}
        A_ST = CFG.get("a_st", 8)
        for st in range(A_ST):
            for tt in range(4):
                t = 4 * st + tt
                xslot = fbv(OFF_XA4 + (t % 4) * 2048, 2048)
                w = [("S", xcast[t - 4])] if t >= 4 else []
                if dump and t >= 4:
                    w.append(("SO", pb.cnt["SO"]))
                lh_x = dma_sync(f"LX{t % 4}", xslot, di["x_seq"].rearrange(
                    "(t p) h -> t p h", p=128)[t], w)
                wv = [(f"LX{t % 4}", lh_x)]
                if t >= 2:
                    wv.append(("T", ttrans[t - 2]))
                    wv.append(("S", xcast[t - 2]))
                xlslot = bbv(OFF_XLN_AB + (t % 2) * 2048, 2048)
                LNDBG["tile"] = t
                xcast[t] = emit_ln(xslot, xlslot, None, wv, t % 2)
                wt = [("S", xcast[t]), ("G", G_CONST)]
                if st >= 1:
                    wt.append((f"SX{tt}", ST_XLT[4 * (st - 1) + tt]))
                vcopy[t] = emit_transpose16(
                    lambda fc, xl=xlslot: xl[:, fc * 128:(fc + 1) * 128],
                    OFF_XLTA + tt * 2048, wt)
                ttrans[t] = pb.cnt["T"]
                ST_XLT[t] = dma_sync(f"SX{tt}", xlt_scr[t],
                                     bbv(OFF_XLTA + tt * 2048, 2048),
                                     [("V", vcopy[t])])
            xlt_view = bbv(OFF_XLTA, 8192).rearrange(
                "p (tile fc tl) -> p tile fc tl", tile=4, fc=16)
            for dc in range(16):
                bank = P[dc % 2]
                if dc < 2:
                    w = [("LB_G", LG_WKT), ("V", vcopy[4 * st + 3])]
                else:
                    w = [("V", kevac[(st, dc - 2)])]
                for fc in range(16):
                    pb.emit(TEN, lambda e, dc=dc, fc=fc, bank=bank, xv=xlt_view:
                            e.matmul(
                        bank[:], wkT_sb[:, fc * 2048 + dc * 128:fc * 2048 + (dc + 1) * 128],
                        xv[:, :, fc, :], start=(fc == 0), stop=(fc == 15)),
                        w if fc == 0 else (), inc="T" if fc == 15 else None)
                kmm = pb.cnt["T"]
                kmm_last[st] = kmm
                kv_slot = bbv(OFF_KVO + (dc % 2) * 512, 512)
                w2 = [("T", kmm)]
                prevkey = (st, dc - 2) if dc >= 2 else (st - 1, dc + 14)
                if prevkey in kst:
                    w2.append((f"SKV{dc % 2}", kst[prevkey]))
                kevac[(st, dc)] = pb.emit(
                    VEC, lambda e, bank=bank, kv=kv_slot: e.tensor_copy(kv, bank[:]),
                    w2, inc="V")
                kst[(st, dc)] = dma_sync(
                    f"SKV{dc % 2}",
                    kT_scr[dc * 128:(dc + 1) * 128, st * 512:(st + 1) * 512],
                    kv_slot, [("V", kevac[(st, dc)])])
        if dump:
            dma_sync("SO", sml_dump, smlsnap[:], [("V", pb.cnt["V"])])
            dump_barrier()
        A1_T_END = pb.cnt["T"]

        LG_WVT = 0
        pb.wait(GPS, [("T", A1_T_END)])
        for q4 in range(4):
            LG_WVT = dma_cast(
                "LB_G", bbv(OFF_WKT + q4 * 8192, 8192),
                di["wv_r"][:, q4 * 8192:(q4 + 1) * 8192])
        wvT_sb = bbv(OFF_WKT, 32768)
        vevac = {}
        vst = {}
        vmm_last = {}
        for t in range(4 * A_ST):
            slot = bbv(OFF_XLTR + (t % 2) * 2048, 2048)
            w = [(f"SX{t % 4}", ST_XLT[t])]
            if t >= 2:
                w.append(("T", vmm_last[t - 2]))
            xltld = dma_sync(f"LX{t % 2}", slot, xlt_scr[t], w)
            for dvg in range(4):
                bank = P[dvg % 2]
                if dvg < 2:
                    w = [(f"LX{t % 2}", xltld), ("LB_G", LG_WVT)]
                    prevkey = (t - 1, dvg + 2)
                else:
                    w = []
                    prevkey = (t, dvg - 2)
                if prevkey in vevac:
                    w.append(("V", vevac[prevkey]))
                for fc in range(16):
                    pb.emit(TEN, lambda e, fc=fc, dvg=dvg, bank=bank, sl=slot:
                            e.matmul(
                        bank[:], sl[:, fc * 128:(fc + 1) * 128],
                        wvT_sb[:, fc * 2048 + dvg * 512:fc * 2048 + (dvg + 1) * 512],
                        start=(fc == 0), stop=(fc == 15)),
                        w if fc == 0 else (), inc="T" if fc == 15 else None)
                vmm = pb.cnt["T"]
                vmm_last[t] = vmm
                kv_slot = bbv(OFF_KVO + (dvg % 2) * 512, 512)
                w2 = [("T", vmm)]
                prevkey = (t, dvg - 2) if dvg >= 2 else (t - 1, dvg + 2)
                if prevkey in vst:
                    w2.append((f"SKV{dvg % 2}", vst[prevkey]))
                vevac[(t, dvg)] = pb.emit(
                    VEC, lambda e, bank=bank, kv=kv_slot: e.tensor_copy(kv, bank[:]),
                    w2, inc="V")
                vst[(t, dvg)] = dma_sync(
                    f"SKV{dvg % 2}",
                    vT_scr[t * 128:(t + 1) * 128, dvg * 512:(dvg + 1) * 512],
                    kv_slot, [("V", vevac[(t, dvg)])])
        A2_T_END = pb.cnt["T"]

    B_V_END = 0
    if "B" in phases:
        if dump:
            _vid = pb.emit(VEC, lambda e: e.tensor_copy(
                srlb[:], ident_bf[:]), [("T", A2_T_END)], inc="V")
            dma_sync("SO", id_dump[:, 0:128], srlb[:], [("V", _vid)])
            dump_barrier()
        bcast = {}
        btrans = {}
        for ot in range(NT_OWN):
            xslot = fbv(OFF_XF + (ot % 2) * 2048, 2048)
            w = [("S", bcast[ot - 2])] if ot >= 2 else []
            lh_x = dma_sync(f"LX{ot % 2}", xslot, di["x_own"].rearrange(
                "(t p) h -> t p h", p=128)[ot], w)
            wv = [(f"LX{ot % 2}", lh_x)]
            if ot == 0:
                wv.append(("T", A2_T_END))
            if ot >= 2:
                wv.append(("T", btrans[ot - 2]))
                wv.append(("S", bcast[ot - 2]))
            xlslot = bbv(OFF_XLN_AB + (ot % 2) * 2048, 2048)
            if dump and "A" not in phases:
                LNDBG["tile"] = ot
            bcast[ot] = emit_ln(xslot, xlslot, None, wv, ot % 2)
            wt = [("S", bcast[ot]), ("G", G_CONST), ("T", A2_T_END)]
            emit_transpose16(lambda fc, xl=xlslot: xl[:, fc * 128:(fc + 1) * 128],
                             OFF_XLTB + ot * 2048, wt)
            btrans[ot] = pb.cnt["T"]
        v_xltb = pb.cnt["V"]
        XLTB_SO = None
        if dump:
            XLTB_SO = dma_sync("SO", xltb_dump, bbv(OFF_XLTB, 16384),
                               [("V", v_xltb)])
            dump_barrier()
        skip_q = bool(CFG.get("noq"))
        xltb_view = bbv(OFF_XLTB, 16384).rearrange(
            "p (tile fc tl) -> p tile fc tl", tile=8, fc=16)
        qevac = {}
        qmm_done = {}
        lg_wq = {}

        def load_wq(dc, w):
            lg_wq[dc] = dma_cast(
                f"LWd{dc % 2}",
                bbv(OFF_WSLAB + (dc % 2) * 2048, 2048), di["wq_r"][dc], w)

        if not skip_q:
            wq0w = [("T", A2_T_END)]
            if CFG.get("serial_wq"):
                wq0w.append(("V", v_xltb))
            load_wq(0, wq0w)
            load_wq(1, [])
        for dc in range(16 if not skip_q else 0):
            slab = bbv(OFF_WSLAB + (dc % 2) * 2048, 2048)
            for tq in range(2):
                bank = P[tq]
                w = []
                if tq == 0:
                    w = [(f"LWd{dc % 2}", lg_wq[dc]), ("V", v_xltb)]
                if (dc - 1, tq) in qevac:
                    w.append(("V", qevac[(dc - 1, tq)]))
                for fc in range(16):
                    pb.emit(TEN, lambda e, fc=fc, tq=tq, bank=bank, sl=slab: e.matmul(
                        bank[:], sl[:, fc * 128:(fc + 1) * 128],
                        xltb_view[:, tq * 4:(tq + 1) * 4, fc, :],
                        start=(fc == 0), stop=(fc == 15)),
                        w if fc == 0 else (), inc="T" if fc == 15 else None)
                qmm_done[dc] = pb.cnt["T"]
                qevac[(dc, tq)] = pb.emit(
                    VEC, lambda e, dc=dc, tq=tq, bank=bank: e.tensor_copy(
                        bbv(OFF_Q + dc * 1024 + tq * 512, 512), bank[:]),
                    [("T", qmm_done[dc])], inc="V")
            if dc + 2 < 16:
                load_wq(dc + 2, [("T", qmm_done[dc])])
        if dump and "A" not in phases:
            dma_sync("SO", sml_dump, smlsnap[:], [("V", pb.cnt["V"])])
        B_V_END = pb.cnt["V"]
        QDUMP_SO = None
        if dump:
            QDUMP_SO = dma_sync("SO", q_dump, bbv(OFF_Q, 16384), [("V", B_V_END)])
            dump_barrier()

    C_V_END = C_T_END = 0
    if "C" in phases:
        kvload = {}
        head_done = {}
        prev_evac = B_V_END
        prev_l_user = 0
        prev_rlb_user = 0
        bank_free_S = {0: 0, 1: 0}
        for h in range(heads):
            buf = h % 2
            koff = OFF_KVH + buf * 8192
            voff = OFF_KVH + buf * 8192 + 4096
            w = [("V", B_V_END)] if h < 2 else [("T", head_done[h - 2])]
            dma_sync(f"LK{h % 2}", bbv(koff, 4096),
                     kT_scr[h * 128:(h + 1) * 128, :], w)
            for vh in range(2):
                kvload[h] = dma_sync(
                    f"LK{h % 2}",
                    bbv(voff + vh * 2048, 2048).rearrange("p (kt d) -> p kt d", kt=16),
                    vT_scr.rearrange("(kt p) d -> p kt d", p=128)[
                        :, vh * 16:(vh + 1) * 16, h * 128:(h + 1) * 128])
            for s in range(NT_OWN):
                ngr = s + 1
                exp_done = {}
                av_done = {}
                for g in range(ngr):
                    bank = P[g % 2]
                    w = []
                    if g == 0:
                        w = [(f"LK{h % 2}", kvload[h])]
                        if h == 0 and s == 0:
                            w.append(("G", G_CONST))
                    if g >= 2:
                        w.append(("S", exp_done[g - 2]))
                    elif bank_free_S[g % 2]:
                        w.append(("S", bank_free_S[g % 2]))
                    for jj in range(4):
                        kt = 4 * g + jj
                        pb.emit(TEN, lambda e, kt=kt, jj=jj, bank=bank, ko=koff,
                                h=h, s=s: e.matmul(
                            bank[:, jj * 128:(jj + 1) * 128],
                            bbv(ko + kt * 128, 128),
                            bbv(OFF_Q + h * 1024 + s * 128, 128),
                            start=True, stop=True),
                            w if jj == 0 else (), inc="T" if jj == 3 else None)
                    smm = pb.cnt["T"]
                    wexp = [("T", smm)]
                    if g >= 2:
                        wexp.append(("T", av_done[g - 2]))
                    ptap = bbv(OFF_PT + (g % 2) * 512, 512)
                    exp_done[g] = pb.emit(SCA, lambda e, bank=bank, pt=ptap:
                                          e.activation(
                        pt, bank[:], ACT.Exp, bias=negC[:, 0:1], scale=SCALE),
                        wexp, inc="S")
                    bank_free_S[g % 2] = exp_done[g]
                    gate_sem = ("S", exp_done[g])
                    if g == s:
                        side = 0 if s < 4 else 1
                        vm = pb.emit(VEC, lambda e, pt=ptap, side=side: e.tensor_mul(
                            pt, pt, BB[:, OFF_CMASK + side * 512:
                                       OFF_CMASK + (side + 1) * 512]),
                            [("S", exp_done[g]), ("LB_G", LG_WGC)], inc="V")
                        gate_sem = ("V", vm)
                    w = [gate_sem]
                    if g == 0:
                        w.append(("V", prev_evac))
                        if prev_l_user:
                            w.append(("V", prev_l_user))
                    first = (g == 0)
                    last = (g == ngr - 1)
                    for jj in range(4):
                        kt = 4 * g + jj
                        pb.emit(TEN, lambda e, kt=kt, jj=jj, pt=ptap, vo=voff,
                                fi=first, la=last: e.matmul(
                            P[2][:, 0:128], bbv(vo + kt * 128, 128),
                            pt[:, jj * 128:(jj + 1) * 128],
                            start=(fi and jj == 0), stop=(la and jj == 3)),
                            w if jj == 0 else ())
                    for jj in range(4):
                        pb.emit(TEN, lambda e, jj=jj, pt=ptap, fi=first, la=last:
                                e.matmul(
                            P[3][0:1, 0:128], onescol_bf[:],
                            pt[:, jj * 128:(jj + 1) * 128],
                            start=(fi and jj == 0), stop=(la and jj == 3)),
                            (), inc="T" if jj == 3 else None)
                    av_done[g] = pb.cnt["T"]
                v_rl = pb.emit(VEC, lambda e: e.reciprocal(srl[:], P[3][0:1, 0:128]),
                               [("T", av_done[ngr - 1])], inc="V")
                prev_l_user = v_rl
                w = [("V", v_rl)]
                if prev_rlb_user:
                    w.append(("V", prev_rlb_user))
                t_rlb = pb.emit(TEN, lambda e: e.matmul(
                    P[4][:, 0:128], onesrow_f32[:], srl[:], start=True, stop=True),
                    w, inc="T")
                v_cp = pb.emit(VEC, lambda e: e.tensor_copy(srlb[:], P[4][:, 0:128]),
                               [("T", t_rlb)], inc="V")
                vfill(2)
                wev = [("V", v_cp)]
                if h == 0 and s == 0 and dump and "B" in phases and XLTB_SO:
                    wev.append(("SO", XLTB_SO))
                prev_evac = pb.emit(VEC, lambda e, h=h, s=s: e.tensor_mul(
                    bbv(OFF_ATT + h * 1024 + s * 128, 128), P[2][:, 0:128], srlb[:]),
                    wev, inc="V")
                prev_rlb_user = prev_evac
            head_done[h] = pb.cnt["T"]
        C_V_END = pb.cnt["V"]
        C_T_END = pb.cnt["T"]
        ATD_SO = None
        if dump:
            ATD_SO = dma_sync("SO", at_dump, bbv(OFF_ATT, 16384), [("V", C_V_END)])
            dump_barrier()

    D_T_END = D_V_END = D_SO_END = 0
    if "D" in phases:
        att_view = bbv(OFF_ATT, 16384)
        d_evac = {}
        oq_done = {}
        lg_woq = {}

        def load_woq(oq, w):
            lg_woq[oq] = dma_cast(
                f"LWd{oq % 2}",
                bbv(OFF_WOQ + (oq % 2) * 8192, 8192), di["wo_r"][oq], w)

        if dump:
            _vid3 = pb.emit(VEC, lambda e: e.tensor_copy(
                srlb[:], ident_bf[:]), [("T", pb.cnt["T"]), ("V", pb.cnt["V"])],
                inc="V")
            dma_sync("SO", id_dump2, srlb[:], [("V", _vid3)])
            dump_barrier()
        load_woq(0, [("T", C_T_END)])
        load_woq(1, [])
        for oq in range(4 if not CFG.get("skip_dmm") else 0):
            woq = bbv(OFF_WOQ + (oq % 2) * 8192, 8192)
            for quad in range(2):
                for dc in range(16):
                    for tl in range(4):
                        tile = quad * 4 + tl
                        w = []
                        if dc == 0 and tl == 0:
                            w = [(f"LWd{oq % 2}", lg_woq[oq]), ("V", C_V_END)]
                            pk = (oq, 0, 3) if quad == 1 else (oq - 1, 1, 3)
                            if pk in d_evac:
                                w.append(("V", d_evac[pk]))
                        pb.emit(TEN, lambda e, dc=dc, tile=tile, tl=tl, wq=woq:
                                e.matmul(
                            P[tl][:], att_view[:, dc * 1024 + tile * 128:
                                               dc * 1024 + (tile + 1) * 128],
                            wq[:, dc * 512:(dc + 1) * 512],
                            start=(dc == 0), stop=(dc == 15)),
                            w, inc="T" if dc == 15 else None)
                dmm = pb.cnt["T"]
                oq_done[oq] = dmm
                for tl in range(4):
                    tile = quad * 4 + tl
                    if CFG.get("evac_alt"):
                        d_evac[(oq, quad, tl)] = pb.emit(
                            VEC, lambda e, tl=tl: e.tensor_copy(
                                srlb[:], P[tl][:, 0:128]),
                            [("T", dmm)], inc="V")
                    else:
                        d_evac[(oq, quad, tl)] = pb.emit(
                            VEC, lambda e, tile=tile, oq=oq, tl=tl: e.tensor_copy(
                                fbv(OFF_ACC + tile * 2048 + oq * 512, 512), P[tl][:]),
                            [("T", dmm)], inc="V")
            if oq + 2 < 4:
                load_woq(oq + 2, [("T", oq_done[oq])])
        if dump:
            _vid2 = pb.emit(VEC, lambda e: e.tensor_copy(
                srlb[:], ident_bf[:]), [("T", pb.cnt["T"]), ("V", pb.cnt["V"])],
                inc="V")
            dma_sync("SO", id_dump[:, 128:256], srlb[:], [("V", _vid2)])
            dump_barrier()
        dcast = {}
        dtrans = {}
        st_res = {}
        for tile in range(NT_OWN if not CFG.get("skip_dmm") else 0):
            xslot = fbv(OFF_XF + (tile % 2) * 2048, 2048)
            w = [("V", d_evac[(3, tile // 4, tile % 4)])]
            if tile >= 2:
                w.append(("S", dcast[tile - 2]))
                w.append(("SO", pb.cnt["SO"]))
            lh_x = dma_sync(f"LX{tile % 2}", xslot, di["x_own"].rearrange(
                "(t p) h -> t p h", p=128)[tile], w)
            x2ap = fbv(OFF_ACC + tile * 2048, 2048)
            v_x2 = pb.emit(VEC, lambda e, a=x2ap, xs=xslot: e.tensor_add(a, a, xs),
                           [(f"LX{tile % 2}", lh_x)], inc="V")
            dma_sync("SO", x2_scr.rearrange("(t p) h -> t p h", p=128)[tile], x2ap,
                     [("V", v_x2)])
            xlslot = bbv(OFF_XLN_DE + (tile % 2) * 2048, 2048)
            wv = [("V", v_x2)]
            if tile >= 2:
                wv.append(("T", dtrans[tile - 2]))
                wv.append(("S", dcast[tile - 2]))
            dcast[tile] = emit_ln(x2ap, xlslot, xslot, wv, tile % 2)
            st_res[tile] = dma_sync(
                "SO", res_scr.rearrange("(t p) h -> t p h", p=128)[tile], xslot,
                [("S", dcast[tile])])
            wtr = [("S", dcast[tile])]
            if tile == 0 and dump and "B" in phases and QDUMP_SO:
                wtr.append(("SO", QDUMP_SO))
            emit_transpose16(lambda fc, xl=xlslot: xl[:, fc * 128:(fc + 1) * 128],
                             OFF_XTT + tile * 2048, wtr)
            dtrans[tile] = pb.cnt["T"]
        if dump and not CFG.get("skip_dmm"):
            dma_sync("SO", xtt_dump, bbv(OFF_XTT, 16384),
                     [("T", pb.cnt["T"]), ("V", pb.cnt["V"])])
            dump_barrier()
        D_T_END = pb.cnt["T"]
        D_V_END = pb.cnt["V"]
        D_SO_END = pb.cnt["SO"]

    if "E" in phases:
        xtt_view = bbv(OFF_XTT, 16384).rearrange(
            "p (tile fc tl) -> p tile fc tl", tile=8, fc=16)

        def xtt_rhs(fc, th):
            return xtt_view[:, th * 4:(th + 1) * 4, fc, :]

        prev_iter_T = D_T_END
        prev_iter_V = D_V_END
        prev_iter_SO = D_SO_END
        for j in range(2):
            w_gate_done = {}
            for tile in range(NT_OWN):
                if tile == 0:
                    w = [("T", prev_iter_T), ("V", prev_iter_V), ("LB_G", LG_WGC)]
                else:
                    w = [("V", w_gate_done[tile - 1])]
                for fc in range(16):
                    pb.emit(TEN, lambda e, fc=fc, tile=tile, j=j: e.matmul(
                        P[4][:, 0:8], xtt_view[:, tile, fc, :],
                        wg_bf[:, j * 128 + fc * 8:j * 128 + (fc + 1) * 8],
                        start=(fc == 0), stop=False),
                        w if fc == 0 else ())
                t_g = pb.emit(TEN, lambda e, j=j: e.matmul(
                    P[4][:, 0:8], onesrow_f32[:], bg_sb[0:1, j * 8:(j + 1) * 8],
                    start=False, stop=True), [("LB_H", LH_BIAS)], inc="T")
                o = 100
                pb.emit(VEC, lambda e: e.tensor_copy(sml[:, o:o + 8], P[4][:, 0:8]),
                        [("T", t_g)])
                vfill(2)
                pb.emit(VEC, lambda e: e.tensor_reduce(
                    sml[:, o + 8:o + 9], sml[:, o:o + 8], AX, ALU.min))
                vfill(2)
                pb.emit(VEC, lambda e: e.tensor_scalar(
                    sml[:, o + 10:o + 18], sml[:, o:o + 8], sml[:, o + 8:o + 9],
                    None, ALU.is_equal))
                vfill(2)
                pb.emit(VEC, lambda e: e.scalar_tensor_tensor(
                    sml[:, o + 18:o + 26], sml[:, o + 10:o + 18], c1e30[:, 0:1],
                    sml[:, o:o + 8], ALU.mult, ALU.add))
                vfill(2)
                pb.emit(VEC, lambda e: e.tensor_reduce(
                    sml[:, o + 9:o + 10], sml[:, o + 18:o + 26], AX, ALU.min))
                vfill(2)
                pb.emit(VEC, lambda e: e.tensor_scalar(
                    sml[:, o + 10:o + 18], sml[:, o:o + 8], sml[:, o + 9:o + 10],
                    None, ALU.is_gt))
                vfill(2)
                pb.emit(VEC, lambda e: e.scalar_tensor_tensor(
                    sml[:, o + 18:o + 26], sml[:, o:o + 8], c1e4[:, 0:1],
                    sml[:, o + 10:o + 18], ALU.add, ALU.mult))
                vfill(2)
                v_m = pb.emit(VEC, lambda e: e.reduce_max(
                    sml[:, o + 8:o + 9], sml[:, o + 18:o + 26], AX, negate=True),
                    inc="V")
                s_e = pb.emit(SCA, lambda e: e.activation(
                    sml[:, o + 26:o + 34], sml[:, o + 18:o + 26], ACT.Exp,
                    bias=sml[:, o + 8:o + 9]), [("V", v_m)], inc="S")
                pb.emit(VEC, lambda e: e.reduce_sum(
                    sml[:, o + 9:o + 10], sml[:, o + 26:o + 34], AX), [("S", s_e)])
                vfill(2)
                pb.emit(VEC, lambda e: e.reciprocal(
                    sml[:, o + 8:o + 9], sml[:, o + 9:o + 10]))
                vfill(2)
                v_w = pb.emit(VEC, lambda e, tile=tile: e.tensor_scalar_mul(
                    w_all[:, tile * 8:(tile + 1) * 8], sml[:, o + 26:o + 34],
                    sml[:, o + 8:o + 9]), inc="V")
                t_wt = pb.emit(TEN, lambda e, tile=tile: e.transpose(
                    P[5][0:8, 0:128], w_all[:, tile * 8:(tile + 1) * 8],
                    ident_f32[:]), [("V", v_w)], inc="T")
                w_gate_done[tile] = pb.emit(VEC, lambda e, tile=tile: e.tensor_copy(
                    wcomb[0:6, tile * 128:(tile + 1) * 128], P[5][0:6, 0:128]),
                    [("T", t_wt)], inc="V")
            acc_init_done = {}
            prev_ai = None
            for tile in range(NT_OWN):
                for hc in range(4):
                    w = [("V", w_gate_done[tile])]
                    if tile == 0 and hc == 0:
                        w.append(("SO", prev_iter_SO))
                        w.append(("LB_G", LG_BDN))
                    if prev_ai:
                        w.append(("V", prev_ai))
                    t_ai = pb.emit(TEN, lambda e, tile=tile, hc=hc: e.matmul(
                        P[6][:], wcomb[0:7, tile * 128:(tile + 1) * 128],
                        bdnsd_sb[0:7, hc * 512:(hc + 1) * 512],
                        start=True, stop=True), w, inc="T")
                    prev_ai = pb.emit(
                        VEC, lambda e, tile=tile, hc=hc: e.tensor_copy(
                            fbv(OFF_ACC + tile * 2048 + hc * 512, 512), P[6][:]),
                        [("T", t_ai)], inc="V")
            V_ACCINIT = prev_ai
            DSO = None
            if dump and j == 0:
                dma_sync("SO", w_dump, w_all[:], [("V", V_ACCINIT)])
                DSO = dma_sync("SO", accd[0], fbv(OFF_ACC, 16384),
                               [("V", V_ACCINIT)])
                dump_barrier()

            sh_done = {}
            lg_su = {}

            def load_su(oc_h, w):
                oc, hh = oc_h
                lg_su[oc_h] = dma_cast(
                    f"LWd{hh}",
                    bbv(OFF_R4 + hh * 1024, 1024),
                    di["wsu_r2"][oc][:, hh * 1024:(hh + 1) * 1024], w)

            wsu0 = [("T", prev_iter_T)]
            if dump and "C" in phases and ATD_SO:
                wsu0.append(("SO", ATD_SO))
            load_su((0, 0), wsu0)
            load_su((0, 1), [])
            lg_sa = dma_cast(
                "LB_G", bbv(OFF_R4 + 2048, 8192), di["wsa_r2"],
                [("T", prev_iter_T)])
            for oc in range(16):
                bank = P[oc % 2]
                s_prev_th = None
                for th in range(2):
                    w = []
                    if th == 0:
                        w = [("LWd0", lg_su[(oc, 0)]), ("LWd1", lg_su[(oc, 1)]),
                             ("V", V_ACCINIT)]
                        if oc >= 1:
                            w.append(("S", sh_done[oc - 1]))
                    else:
                        w = [("S", s_prev_th)]
                    for fc in range(16):
                        sl = bbv(OFF_R4 + (fc // 8) * 1024, 1024)
                        pb.emit(TEN, lambda e, fc=fc, th=th, bank=bank, sl=sl:
                                e.matmul(
                            bank[:], sl[:, (fc % 8) * 128:(fc % 8 + 1) * 128],
                            xtt_rhs(fc, th), start=(fc == 0), stop=(fc == 15)),
                            w if fc == 0 else (), inc="T" if fc == 15 else None)
                    t_sh = pb.cnt["T"]
                    s_prev_th = pb.emit(SCA, lambda e, oc=oc, th=th, bank=bank:
                                        e.activation(
                        bbv(OFF_R2 + oc * 1024 + th * 512, 512), bank[:],
                        ACT.Identity, bias=bsu_sb[:, oc:oc + 1]),
                        [("T", t_sh)], inc="S")
                sh_done[oc] = s_prev_th
                if oc + 1 < 16:
                    load_su((oc + 1, 0), [("S", s_prev_th)])
                    load_su((oc + 1, 1), [])
            S_SH_END = pb.cnt["S"]
            if dump and j == 0:
                dma_sync("SO", sh_dump, bbv(OFF_R2, 16384), [("S", S_SH_END)])
                dump_barrier()
            wsa_sb = bbv(OFF_R4 + 2048, 8192)
            z_done = {}
            prev_zmul = 0
            for g2 in range(2):
                for e2c in range(8):
                    zc = g2 * 8 + e2c
                    bank = P[zc % 2]
                    v_prev_th = None
                    for th in range(2):
                        w = []
                        if th == 0:
                            if zc == 0:
                                w = [("LB_G", lg_sa), ("S", S_SH_END)]
                            elif zc >= 2:
                                w = [("V", z_done[zc - 2])]
                        else:
                            w = [("V", v_prev_th)]
                        for ec in range(8):
                            pb.emit(TEN, lambda e, ec=ec, e2c=e2c, g2=g2, th=th,
                                    bank=bank: e.matmul(
                                bank[:],
                                wsa_sb[:, ec * 1024 + e2c * 128:ec * 1024 + (e2c + 1) * 128],
                                bbv(OFF_R2 + (g2 * 8 + ec) * 1024 + th * 512, 512),
                                start=(ec == 0), stop=(ec == 7)),
                                w if ec == 0 else (), inc="T" if ec == 7 else None)
                        t_a = pb.cnt["T"]
                        ws = [("T", t_a)]
                        if prev_zmul:
                            ws.append(("V", prev_zmul))
                        s_si = pb.emit(SCA, lambda e, e2c=e2c, bank=bank:
                                       e.activation(
                            tmp_bf[:], bank[:], ACT.Silu,
                            bias=bsa_sb[:, e2c:e2c + 1]), ws, inc="S")
                        v_prev_th = pb.emit(VEC, lambda e, zc=zc, th=th: e.tensor_mul(
                            bbv(OFF_R3 + zc * 1024 + th * 512, 512), tmp_bf[:],
                            bbv(OFF_R2 + zc * 1024 + th * 512, 512)),
                            [("S", s_si)], inc="V")
                        prev_zmul = v_prev_th
                    z_done[zc] = v_prev_th
            V_Z_END = pb.cnt["V"]
            if dump and j == 0:
                ZSO = dma_sync("SO", z_dump, bbv(OFF_R3, 16384), [("V", V_Z_END)])
                dump_barrier()
            sd_evac = {}
            lg_sd = {}

            def load_sd(hq, w):
                lg_sd[hq] = dma_cast(
                    f"LWd{hq % 2}",
                    bbv(OFF_R2 + (hq % 2) * 8192, 8192), di["wsd_r2"][hq], w)

            w0 = [("V", V_Z_END), ("S", S_SH_END)]
            if DSO:
                w0.append(("SO", DSO))
            load_sd(0, w0)
            load_sd(1, [])
            for hq in range(4):
                wsdq = bbv(OFF_R2 + (hq % 2) * 8192, 8192)
                for quad in range(2):
                    for zc in range(16):
                        for tl in range(4):
                            tile = quad * 4 + tl
                            w = []
                            if zc == 0 and tl == 0:
                                w = [(f"LWd{hq % 2}", lg_sd[hq]), ("V", V_Z_END)]
                                pk = (hq, 0, 3) if quad == 1 else (hq - 1, 1, 3)
                                if pk in sd_evac:
                                    w.append(("V", sd_evac[pk]))
                            pb.emit(TEN, lambda e, zc=zc, tile=tile, tl=tl,
                                    wq=wsdq: e.matmul(
                                P[tl][:],
                                bbv(OFF_R3 + zc * 1024 + tile * 128, 128),
                                wq[:, zc * 512:(zc + 1) * 512],
                                start=(zc == 0), stop=(zc == 15)),
                                w, inc="T" if zc == 15 else None)
                    t_sd = pb.cnt["T"]
                    for tl in range(4):
                        tile = quad * 4 + tl
                        sd_evac[(hq, quad, tl)] = pb.emit(
                            VEC, lambda e, tile=tile, hq=hq, tl=tl: e.tensor_add(
                                fbv(OFF_ACC + tile * 2048 + hq * 512, 512),
                                fbv(OFF_ACC + tile * 2048 + hq * 512, 512),
                                P[tl][:]), [("T", t_sd)], inc="V")
                if hq + 2 < 4:
                    load_sd(hq + 2, [("T", pb.cnt["T"])])
            V_SD_END = pb.cnt["V"]
            DSO2 = None
            if dump and j == 0:
                DSO2 = dma_sync("SO", accd[1], fbv(OFF_ACC, 16384),
                                [("V", V_SD_END)])
                dump_barrier()

            ex_evac_last = V_SD_END
            prev_h_readers_T = pb.cnt["T"]
            for n in range(NE):
                h_done = {}
                lg_up = {}

                def load_up(ec, w, n=n):
                    lg_up[ec] = dma_cast(
                        f"LWa{ec % 2}",
                        bbv(OFF_R4 + (ec % 2) * 2048, 2048),
                        di["wup_r"][n][ec], w)

                w0 = [("T", prev_h_readers_T), ("V", ex_evac_last)]
                if n == 0 and DSO2:
                    w0.append(("SO", DSO2))
                load_up(0, w0)
                load_up(1, [])
                for ec in range(8):
                    bank = P[ec % 2]
                    slab = bbv(OFF_R4 + (ec % 2) * 2048, 2048)
                    s_prev_th = None
                    for th in range(2):
                        w = []
                        if th == 0:
                            w = [(f"LWa{ec % 2}", lg_up[ec])]
                            if ec == 0:
                                w.append(("V", ex_evac_last))
                            if ec >= 2:
                                w.append(("S", h_done[ec - 2]))
                        else:
                            w = [("S", s_prev_th)]
                        for fc in range(16):
                            pb.emit(TEN, lambda e, fc=fc, th=th, bank=bank,
                                    sl=slab: e.matmul(
                                bank[:], sl[:, fc * 128:(fc + 1) * 128],
                                xtt_rhs(fc, th), start=(fc == 0), stop=(fc == 15)),
                                w if fc == 0 else (), inc="T" if fc == 15 else None)
                        t_h = pb.cnt["T"]
                        s_prev_th = pb.emit(SCA, lambda e, n=n, ec=ec, th=th,
                                            bank=bank: e.activation(
                            bbv(OFF_R2 + ec * 1024 + th * 512, 512), bank[:],
                            ACT.Identity, bias=bup_sb[:, n * 8 + ec:n * 8 + ec + 1]),
                            [("T", t_h)], inc="S")
                    h_done[ec] = s_prev_th
                    if ec + 2 < 8:
                        load_up(ec + 2, [("T", pb.cnt["T"])])
                S_H_END = pb.cnt["S"]
                prev_h_readers_T = pb.cnt["T"]
                u_done = {}
                prev_umul = 0
                lg_sw = {}

                def load_sw(e2c, w, n=n):
                    lg_sw[e2c] = dma_cast(
                        f"LWb{e2c % 2}",
                        bbv(OFF_R4 + 4096 + (e2c % 2) * 1024, 1024),
                        di["wsw_r"][n][e2c], w)

                load_sw(0, [("V", ex_evac_last)])
                load_sw(1, [])
                for e2c in range(8):
                    bank = P[2 + (e2c % 2)]
                    slab = bbv(OFF_R4 + 4096 + (e2c % 2) * 1024, 1024)
                    v_prev_th = None
                    for th in range(2):
                        w = []
                        if th == 0:
                            w = [(f"LWb{e2c % 2}", lg_sw[e2c]), ("S", S_H_END)]
                            if e2c >= 2:
                                w.append(("V", u_done[e2c - 2]))
                        else:
                            w = [("V", v_prev_th)]
                        for ec in range(8):
                            pb.emit(TEN, lambda e, ec=ec, th=th, bank=bank,
                                    sl=slab: e.matmul(
                                bank[:], sl[:, ec * 128:(ec + 1) * 128],
                                bbv(OFF_R2 + ec * 1024 + th * 512, 512),
                                start=(ec == 0), stop=(ec == 7)),
                                w if ec == 0 else (), inc="T" if ec == 7 else None)
                        t_u = pb.cnt["T"]
                        ws = [("T", t_u)]
                        if prev_umul:
                            ws.append(("V", prev_umul))
                        s_si = pb.emit(SCA, lambda e, n=n, e2c=e2c, bank=bank:
                                       e.activation(
                            tmp_bf[:], bank[:], ACT.Silu,
                            bias=bsw_sb[:, n * 8 + e2c:n * 8 + e2c + 1]),
                            ws, inc="S")
                        v_prev_th = pb.emit(VEC, lambda e, e2c=e2c, th=th:
                                            e.tensor_mul(
                            bbv(OFF_R3 + e2c * 1024 + th * 512, 512), tmp_bf[:],
                            bbv(OFF_R2 + e2c * 1024 + th * 512, 512)),
                            [("S", s_si)], inc="V")
                        prev_umul = v_prev_th
                    u_done[e2c] = v_prev_th
                    if e2c + 2 < 8:
                        load_sw(e2c + 2, [("T", pb.cnt["T"])])
                V_U_END = pb.cnt["V"]
                dn_evac = {}
                lg_dn = {}

                def load_dn(hq, w, n=n):
                    lg_dn[hq] = dma_cast(
                        f"LWc{hq % 2}",
                        bbv(OFF_R2 + (hq % 2) * 4096, 4096),
                        di["wdn_r"][n][hq], w)

                load_dn(0, [("V", V_U_END)])
                load_dn(1, [("V", V_U_END)])
                for hq in range(4):
                    wdnq = bbv(OFF_R2 + (hq % 2) * 4096, 4096)
                    for quad in range(2):
                        for ec in range(8):
                            for tl in range(4):
                                tile = quad * 4 + tl
                                w = []
                                if ec == 0 and tl == 0:
                                    w = [(f"LWc{hq % 2}", lg_dn[hq]), ("V", V_U_END)]
                                    pk = (hq, 0, 3) if quad == 1 else (hq - 1, 1, 3)
                                    if pk in dn_evac:
                                        w.append(("V", dn_evac[pk]))
                                pb.emit(TEN, lambda e, ec=ec, tile=tile, tl=tl,
                                        wq=wdnq: e.matmul(
                                    P[tl][:],
                                    bbv(OFF_R3 + ec * 1024 + tile * 128, 128),
                                    wq[:, ec * 512:(ec + 1) * 512],
                                    start=(ec == 0), stop=(ec == 7)),
                                    w, inc="T" if ec == 7 else None)
                        t_dn = pb.cnt["T"]
                        for tl in range(4):
                            tile = quad * 4 + tl
                            dn_evac[(hq, quad, tl)] = pb.emit(
                                VEC, lambda e, tile=tile, hq=hq, n=n, tl=tl:
                                e.scalar_tensor_tensor(
                                    fbv(OFF_ACC + tile * 2048 + hq * 512, 512),
                                    P[tl][:],
                                    w_all[:, tile * 8 + n:tile * 8 + n + 1],
                                    fbv(OFF_ACC + tile * 2048 + hq * 512, 512),
                                    ALU.mult, ALU.add), [("T", t_dn)], inc="V")
                    if hq + 2 < 4:
                        load_dn(hq + 2, [("T", pb.cnt["T"])])
                ex_evac_last = pb.cnt["V"]
                prev_h_readers_T = pb.cnt["T"]

            DSO3 = None
            if dump and j == 0:
                DSO3 = dma_sync("SO", accd[2], fbv(OFF_ACC, 16384),
                                [("V", ex_evac_last)])
                dump_barrier()
            comb_v = {}
            comb_trans = {}
            comb_st = {}
            for tile in range(NT_OWN):
                xslot = fbv(OFF_XF + (tile % 2) * 2048, 2048)
                w = [("V", ex_evac_last), ("SO", D_SO_END)]
                if DSO3:
                    w.append(("SO", DSO3))
                if tile >= 2:
                    w.append(("V", comb_v[tile - 2]))
                    w.append(("SO", pb.cnt["SO"]))
                lh_res = dma_sync(f"LX{tile % 2}", xslot, res_scr.rearrange(
                    "(t p) h -> t p h", p=128)[tile], w)
                accap = fbv(OFF_ACC + tile * 2048, 2048)
                v_add = pb.emit(VEC, lambda e, a=accap, xs=xslot: e.tensor_add(
                    a, a, xs), [(f"LX{tile % 2}", lh_res)], inc="V")
                if j == 0:
                    xlslot = bbv(OFF_XLN_DE + (tile % 2) * 2048, 2048)
                    w = [("V", v_add)]
                    if tile >= 2:
                        w.append(("T", comb_trans[tile - 2]))
                    s_c = pb.emit(SCA, lambda e, a=accap, xl=xlslot:
                                  e.activation(xl, a, ACT.Copy), w, inc="S")
                    comb_v[tile] = emit_transpose16(
                        lambda fc, xl=xlslot: xl[:, fc * 128:(fc + 1) * 128],
                        OFF_XTT + tile * 2048, [("S", s_c)])
                    comb_trans[tile] = pb.cnt["T"]
                else:
                    lh_x2 = dma_sync(f"LX{tile % 2}", xslot, x2_scr.rearrange(
                        "(t p) h -> t p h", p=128)[tile], [("V", v_add)])
                    v_f = pb.emit(VEC, lambda e, a=accap, xs=xslot: e.tensor_add(
                        a, a, xs), [(f"LX{tile % 2}", lh_x2)], inc="V")
                    comb_v[tile] = v_f
                    comb_st[tile] = dma_sync(
                        "SO", out_own.rearrange("(t p) h -> t p h", p=128)[tile],
                        accap, [("V", v_f)])
            prev_iter_T = pb.cnt["T"]
            prev_iter_V = pb.cnt["V"]
            prev_iter_SO = pb.cnt["SO"]

    pb.wait("sync", [(s, pb.cnt[s]) for s in SYNC_SEMS if pb.cnt[s]])
    pb.wait("gpsimd", [(s, pb.cnt[s]) for s in GPS_SEMS if pb.cnt[s]])
    pb.run()
    return nc


def _own_tiles(i):
    return [4 * s + i if s < 4 else 4 * s + 3 - i for s in range(8)]


def _make_cmask(i):
    m = np.zeros((2, 128, 512), np.float32)
    tri = (np.arange(128)[:, None] <= np.arange(128)[None, :]).astype(np.float32)
    for side, r in enumerate((i, 3 - i)):
        for jj in range(4):
            blk = m[side, :, jj * 128:(jj + 1) * 128]
            if jj < r:
                blk[:] = 1.0
            elif jj == r:
                blk[:] = tri
    return m


def make_in_maps(x, w_qkv, w_o, Wg, bg, Wup, bup, Wsw, bsw, Wdn, bdn,
                 Wsu, bsu, Wsa, bsa, Wsd, bsd):
    f = lambda a: np.ascontiguousarray(np.asarray(a, dtype=np.float32))
    x = f(x); w_qkv = f(w_qkv)
    wqT = np.ascontiguousarray(w_qkv[0:H].T)
    wkT = np.ascontiguousarray(w_qkv[H:2 * H].T)
    wvT = np.ascontiguousarray(w_qkv[2 * H:3 * H].T)
    woT = np.ascontiguousarray(f(w_o).T)
    c = np.ascontiguousarray
    shared = dict(
        wq_r=c(wqT.reshape(16, 128, 16, 128).transpose(2, 1, 0, 3).reshape(16, 128, 2048)),
        wk_r=c(wkT.reshape(16, 128, 2048).transpose(1, 0, 2).reshape(128, 32768)),
        wv_r=c(wvT.reshape(16, 128, 2048).transpose(1, 0, 2).reshape(128, 32768)),
        wo_r=c(woT.reshape(16, 128, 4, 512).transpose(2, 1, 0, 3).reshape(4, 128, 8192)),
        wg_in=c(f(Wg).reshape(2, 16, 128, 8).transpose(2, 0, 1, 3).reshape(128, 256)),
        bg_in=f(bg).reshape(1, 16),
        wup_r=c(f(Wup).reshape(6, 16, 128, 8, 128).transpose(0, 3, 2, 1, 4).reshape(6, 8, 128, 2048)),
        wsw_r=c(f(Wsw).reshape(6, 8, 128, 8, 128).transpose(0, 3, 2, 1, 4).reshape(6, 8, 128, 1024)),
        wdn_r=c(f(Wdn).reshape(6, 8, 128, 4, 512).transpose(0, 3, 2, 1, 4).reshape(6, 4, 128, 4096)),
        wsu_r2=c(f(Wsu).reshape(16, 128, 16, 128).transpose(2, 1, 0, 3).reshape(16, 128, 2048)),
        wsa_r2=c(f(Wsa).reshape(8, 128, 1024).transpose(1, 0, 2).reshape(128, 8192)),
        wsd_r2=c(f(Wsd).reshape(16, 128, 4, 512).transpose(2, 1, 0, 3).reshape(4, 128, 8192)),
        bup_r=np.ascontiguousarray(
            f(bup).reshape(6, 8, 128).transpose(2, 0, 1).reshape(128, 48)),
        bsw_r=np.ascontiguousarray(
            f(bsw).reshape(6, 8, 128).transpose(2, 0, 1).reshape(128, 48)),
        bsu_r=np.ascontiguousarray(f(bsu).reshape(16, 128).T),
        bsa_r=np.ascontiguousarray(f(bsa).reshape(8, 128).T),
        bdnsd=np.ascontiguousarray(
            np.concatenate([f(bdn), f(bsd)[None]], 0)),
    )
    in_maps = []
    for c in range(8):
        b, i = c // 4, c % 4
        m = dict(shared)
        m["x_own"] = np.ascontiguousarray(
            x[b].reshape(NT_SEQ, 128, H)[_own_tiles(i)].reshape(1024, H))
        m["x_seq"] = x[b]
        m["cmask"] = _make_cmask(i)
        in_maps.append(m)
    return in_maps


def kernel(L0_attn_hidden_states, L0_attn_w_qkv, L0_attn_w_o,
           L0_ffn_Wg, L0_ffn_bg, L0_ffn_Wup, L0_ffn_bup, L0_ffn_Wsw, L0_ffn_bsw,
           L0_ffn_Wdn, L0_ffn_bdn, L0_ffn_Wsu, L0_ffn_bsu, L0_ffn_Wsa,
           L0_ffn_bsa, L0_ffn_Wsd, L0_ffn_bsd):
    in_maps = make_in_maps(L0_attn_hidden_states, L0_attn_w_qkv, L0_attn_w_o,
                           L0_ffn_Wg, L0_ffn_bg, L0_ffn_Wup, L0_ffn_bup,
                           L0_ffn_Wsw, L0_ffn_bsw, L0_ffn_Wdn, L0_ffn_bdn,
                           L0_ffn_Wsu, L0_ffn_bsu, L0_ffn_Wsa, L0_ffn_bsa,
                           L0_ffn_Wsd, L0_ffn_bsd)
    nc = build()
    res = run_bass_kernel_spmd(nc, in_maps, core_ids=list(range(8)))
    out = np.empty((2, SEQ, H), np.float32)
    for c in range(8):
        b, i = c // 4, c % 4
        oc = np.asarray(res.results[c]["out_own"]).reshape(NT_OWN, 128, H)
        for s, t in enumerate(_own_tiles(i)):
            out[b, t * 128:(t + 1) * 128, :] = oc[s]
    return out


# version 03
# speedup vs baseline: 1.0045x; 1.0045x over previous
import math
import numpy as np
import concourse.bass as bass
import concourse.mybir as mybir
from concourse.bass_utils import run_bass_kernel_spmd

F32 = mybir.dt.float32
BF16 = mybir.dt.bfloat16
AX = mybir.AxisListType.X
ALU = mybir.AluOpType
ACT = mybir.ActivationFunctionType

H = 2048
SEQ = 4096
NT_SEQ = 32
NT_OWN = 8
NE = 6
EX = 1024
C_SHIFT = 10.0
SCALE = 1.0 / math.sqrt(128.0)
EPS = 1e-5

CFG = dict(heads=16, dump=False, phases="ABCDEF")
ENGINES = ("sync", "tensor", "vector", "scalar", "gpsimd")


class PB:
    def __init__(self, nc):
        self.nc = nc
        self.q = {e: [] for e in ENGINES}
        self.sem = {}
        self.cnt = {}

    def new_sem(self, name):
        self.sem[name] = self.nc.alloc_semaphore(name)
        self.cnt[name] = 0

    def emit(self, eng, fn, waits=(), inc=None, inc_n=1):
        tgt = None
        if inc is not None:
            self.cnt[inc] += inc_n
            tgt = self.cnt[inc]
        self.q[eng].append((fn, tuple(waits), inc, inc_n))
        return tgt

    def wait(self, eng, waits):
        self.q[eng].append((None, tuple(waits), None, 0))

    def run(self):
        sems = self.sem

        def body(eng_name):
            def _(eng_obj):
                for fn, waits, inc, inc_n in self.q[eng_name]:
                    for s, v in waits:
                        if v and v > 0:
                            eng_obj.wait_ge(sems[s], v)
                    if fn is not None:
                        inst = fn(eng_obj)
                        if inc is not None:
                            inst.then_inc(sems[inc], inc_n)
            return _

        with self.nc.Block() as blk:
            blk.sync(body("sync"))
            blk.tensor(body("tensor"))
            blk.vector(body("vector"))
            blk.scalar(body("scalar"))
            blk.gpsimd(body("gpsimd"))


def build():
    heads = CFG["heads"]
    dump = CFG["dump"]
    phases = CFG["phases"]

    nc = bass.Bass("TRN2", target_bir_lowering=False,
                   detect_race_conditions=False,
                   dynamic_dma_scratch_size=16384)

    di = {}

    def inp(name, shape):
        di[name] = nc.dram_tensor(name, list(shape), F32, kind="ExternalInput").ap()

    inp("x_own", (1024, H))
    inp("x_seq", (SEQ, H))
    inp("wq_r", (16, 128, 2048)); inp("wk_r", (128, 32768)); inp("wv_r", (128, 32768))
    inp("wo_r", (4, 128, 8192))
    inp("wg_in", (128, 256)); inp("bg_in", (1, 16))
    inp("wup_r", (NE, 8, 128, 2048)); inp("wsw_r", (NE, 8, 128, 1024))
    inp("wdn_r", (NE, 4, 128, 4096))
    inp("wsu_r2", (16, 128, 2048)); inp("wsa_r2", (128, 8192))
    inp("wsd_r2", (4, 128, 8192))
    inp("bup_r", (128, 48)); inp("bsw_r", (128, 48)); inp("bsu_r", (128, 16))
    inp("bsa_r", (128, 8)); inp("bdnsd", (7, H))
    inp("cmask", (2, 128, 512))

    out_own = nc.dram_tensor("out_own", [1024, H], F32, kind="ExternalOutput").ap()
    skind = "ExternalOutput" if dump else "Internal"
    kT_scr = nc.dram_tensor("kT_scr", [H, SEQ], BF16, kind=skind).ap()
    vT_scr = nc.dram_tensor("vT_scr", [SEQ, H], BF16, kind=skind).ap()
    xlt_scr = nc.dram_tensor("xlt_scr", [NT_SEQ, 128, H], BF16, kind=skind).ap()
    res_scr = nc.dram_tensor("res_scr", [1024, H], F32, kind=skind).ap()
    x2_scr = nc.dram_tensor("x2_scr", [1024, H], F32, kind=skind).ap()
    if dump:
        q_dump = nc.dram_tensor("q_dump", [128, 16384], BF16, kind="ExternalOutput").ap()
        at_dump = nc.dram_tensor("at_dump", [128, 16384], BF16, kind="ExternalOutput").ap()
        accd = [nc.dram_tensor(f"acc_dump{k}", [128, 16384], F32,
                               kind="ExternalOutput").ap() for k in range(3)]
        w_dump = nc.dram_tensor("w_dump", [128, 64], F32, kind="ExternalOutput").ap()
        sh_dump = nc.dram_tensor("sh_dump", [128, 16384], BF16, kind="ExternalOutput").ap()
        z_dump = nc.dram_tensor("z_dump", [128, 16384], BF16, kind="ExternalOutput").ap()
        sml_dump = nc.dram_tensor("sml_dump", [128, 128], F32, kind="ExternalOutput").ap()
        x_dump = nc.dram_tensor("x_dump", [4, 128, 2048], F32, kind="ExternalOutput").ap()
        xltb_dump = nc.dram_tensor("xltb_dump", [128, 16384], BF16, kind="ExternalOutput").ap()
        id_dump = nc.dram_tensor("id_dump", [128, 256], F32, kind="ExternalOutput").ap()
        xtt_dump = nc.dram_tensor("xtt_dump", [128, 16384], BF16, kind="ExternalOutput").ap()
        id_dump2 = nc.dram_tensor("id_dump2", [128, 128], F32, kind="ExternalOutput").ap()
        x_dump_v = [x_dump[k] for k in range(4)]

    sb = nc.alloc_sbuf_tensor
    ident_bf = sb("ident_bf", [128, 128], BF16)
    ident_f32 = sb("ident_f32", [128, 128], F32)
    onescol_bf = sb("onescol_bf", [128, 1], BF16)
    onesrow_f32 = sb("onesrow_f32", [1, 128], F32)
    negC = sb("negC", [128, 1], F32)
    c1e30 = sb("c1e30", [128, 1], F32)
    c1e4 = sb("c1e4", [128, 1], F32)
    wg_bf = sb("wg_bf", [128, 256], BF16)
    bg_sb = sb("bg_sb", [1, 16], F32)
    bup_sb = sb("bup_sb", [128, 48], F32)
    bsw_sb = sb("bsw_sb", [128, 48], F32)
    bsu_sb = sb("bsu_sb", [128, 16], F32)
    bsa_sb = sb("bsa_sb", [128, 8], F32)
    bdnsd_sb = sb("bdnsd_sb", [8, H], BF16)
    wcomb = sb("wcomb", [8, 1024], BF16)
    w_all = sb("w_all", [128, 64], F32)
    srl = sb("srl", [1, 128], F32)
    srlb = sb("srlb", [128, 128], F32)
    sml = sb("sml", [128, 160], F32)
    smlsnap = sb("smlsnap", [128, 128], F32) if dump else None
    tmp_bf = sb("tmp_bf", [128, 512], BF16)

    NBF = 59392
    BB = sb("BB", [128, NBF], BF16)
    NF = 20480
    FB = sb("FB", [128, NF], F32)
    assert nc.sbuf_bytes_remaining >= 0

    def bbv(off, n):
        return BB[:, off:off + n]

    def fbv(off, n):
        return FB[:, off:off + n]

    OFF_WKT = 0
    OFF_XLTA = 32768
    OFF_XLTR = 40960
    OFF_KVO = 49152
    OFF_Q = 0
    OFF_XLTB = 16384
    OFF_ATT = 16384
    OFF_KVH = 32768
    OFF_PT = 50176
    OFF_CMASK = 54272
    OFF_WSLAB = 50176
    OFF_XTT = 0
    OFF_WOQ = 32768
    OFF_R2 = 16384
    OFF_R3 = 32768
    OFF_R4 = 49152

    OFF_XLN_AB = 45056
    OFF_XLN_DE = 53248
    OFF_ACC = 0
    OFF_XF = 16384
    OFF_XA4 = 8192

    P = [nc.alloc_psum_tensor(f"P{i}", [128, 512], F32) for i in range(7)]
    PB7 = nc.alloc_psum_tensor("PB7", [128, 1024], BF16)

    pb = PB(nc)
    SYNC_SEMS = ["LB_H", "LX0", "LX1", "LX2", "LX3", "LK0", "LK1",
                 "SKV0", "SKV1", "SX0", "SX1", "SX2", "SX3", "SO"]
    GPS_SEMS = ["LB_G", "LWa0", "LWa1", "LWb0", "LWb1", "LWc0", "LWc1",
                "LWd0", "LWd1"]
    for s in SYNC_SEMS + GPS_SEMS + ["T", "S", "V", "G"]:
        pb.new_sem(s)
    sem_nums = [h.num for h in pb.sem.values()]
    clear_range = range(min(sem_nums), max(sem_nums) + 1)
    if CFG.get("preamble", True):
        nc.gpsimd.sem_clear(clear_range)
        nc.all_engine_barrier()

    def dma_sync(sem, dst, src, waits=()):
        return pb.emit("sync", lambda e, d=dst, s=src: e.dma_start(out=d, in_=s),
                       waits, sem, 16)

    def dma_cast(sem, dst, src, waits=()):
        return pb.emit("gpsimd", lambda e, d=dst, s=src: e.dma_start(out=d, in_=s),
                       waits, sem, 16)

    def dump_barrier():
        v = pb.cnt["SO"]
        for eng in ENGINES:
            pb.wait(eng, [("SO", v)])

    TEN, VEC, SCA = "tensor", "vector", "scalar"
    GPS = "gpsimd"

    pb.emit(GPS, lambda e: e.memset(ident_bf[:], 0.0))
    pb.emit(GPS, lambda e: e.affine_select(
        out=ident_bf[:], in_=ident_bf[:], compare_op=ALU.not_equal, fill=1.0,
        base=0, pattern=[[-1, 128]], channel_multiplier=1))
    pb.emit(GPS, lambda e: e.memset(ident_f32[:], 0.0))
    pb.emit(GPS, lambda e: e.affine_select(
        out=ident_f32[:], in_=ident_f32[:], compare_op=ALU.not_equal, fill=1.0,
        base=0, pattern=[[-1, 128]], channel_multiplier=1))
    pb.emit(GPS, lambda e: e.memset(onescol_bf[:], 1.0))
    pb.emit(GPS, lambda e: e.memset(onesrow_f32[:], 1.0))
    pb.emit(GPS, lambda e: e.memset(negC[:], -C_SHIFT))
    pb.emit(GPS, lambda e: e.memset(sml[:, 144:160], 0.0))
    pb.emit(GPS, lambda e: e.memset(c1e30[:], 1.0e30))
    pb.emit(GPS, lambda e: e.memset(c1e4[:], 1.0e4))
    G_CONST = pb.emit(GPS, lambda e: e.memset(wcomb[:], 1.0), inc="G")

    cmask_bf = BB[:, OFF_CMASK:OFF_CMASK + 1024]
    dma_cast("LB_G", cmask_bf.rearrange("p (i j) -> p i j", i=2),
             di["cmask"].rearrange("i p j -> p i j"))
    LG_WGC = dma_cast("LB_G", wg_bf[:], di["wg_in"])
    LH_BIAS = 0
    for name, dst in (("bg_in", bg_sb[:]), ("bup_r", bup_sb[:]),
                      ("bsw_r", bsw_sb[:]), ("bsu_r", bsu_sb[:]),
                      ("bsa_r", bsa_sb[:])):
        LH_BIAS = dma_sync("LB_H", dst, di[name])
    LG_BDN = dma_cast("LB_G", bdnsd_sb[0:7, :], di["bdnsd"])

    wkT_sb = bbv(OFF_WKT, 32768)
    LG_WKT = 0
    if "A" in phases:
        for q4 in range(4):
            LG_WKT = dma_cast(
                "LB_G", bbv(OFF_WKT + q4 * 8192, 8192),
                di["wk_r"][:, q4 * 8192:(q4 + 1) * 8192])

    state = dict(pb7_free=0)

    LNDBG = dict(tile=None)
    fill_state = dict(k=0)

    def vfill(n=2):
        for _ in range(n):
            k = fill_state["k"] = (fill_state["k"] + 1) % 4
            pb.emit(VEC, lambda e, k=k: e.tensor_copy(
                sml[:, 151 + k:152 + k], sml[:, 150:151]))

    def emit_ln(xap, out_bf_ap, out_f32_ap, waits_v, par):
        o = par * 64
        for c in range(4):
            pb.emit(VEC, lambda e, c=c, o=o: e.bn_stats(
                sml[:, o + 6 * c:o + 6 * c + 6], xap[:, 512 * c:512 * (c + 1)]),
                waits_v if c == 0 else ())
        for c in range(4):
            pb.emit(VEC, lambda e, c=c, o=o: e.bn_aggr(
                sml[:, o + 24 + 2 * c:o + 26 + 2 * c], sml[:, o + 6 * c:o + 6 * c + 6]))
        mview = sml[:, o + 24:o + 32].rearrange("p (c t) -> p c t", t=2)[:, :, 0]
        vview = sml[:, o + 24:o + 32].rearrange("p (c t) -> p c t", t=2)[:, :, 1]
        vfill()
        pb.emit(VEC, lambda e: e.reduce_sum(sml[:, o + 32:o + 33], mview, AX))
        pb.emit(VEC, lambda e: e.tensor_mul(sml[:, o + 36:o + 40], mview, mview))
        pb.emit(VEC, lambda e: e.reduce_sum(sml[:, o + 41:o + 42], vview, AX))
        pb.emit(VEC, lambda e: e.tensor_scalar_mul(
            sml[:, o + 33:o + 34], sml[:, o + 32:o + 33], -0.25))
        vfill(1)
        pb.emit(VEC, lambda e: e.reduce_sum(
            sml[:, o + 40:o + 41], sml[:, o + 36:o + 40], AX))
        pb.emit(VEC, lambda e: e.tensor_mul(
            sml[:, o + 44:o + 45], sml[:, o + 33:o + 34], sml[:, o + 33:o + 34]))
        vfill(1)
        pb.emit(VEC, lambda e: e.tensor_add(
            sml[:, o + 42:o + 43], sml[:, o + 40:o + 41], sml[:, o + 41:o + 42]))
        vfill(2)
        pb.emit(VEC, lambda e: e.tensor_scalar_mul(
            sml[:, o + 43:o + 44], sml[:, o + 42:o + 43], 0.25))
        vfill(2)
        pb.emit(VEC, lambda e: e.tensor_sub(
            sml[:, o + 45:o + 46], sml[:, o + 43:o + 44], sml[:, o + 44:o + 45]))
        vfill(2)
        v_stats = pb.emit(VEC, lambda e: e.tensor_scalar_add(
            sml[:, o + 46:o + 47], sml[:, o + 45:o + 46], EPS), inc="V")
        pb.emit(SCA, lambda e: e.activation(
            sml[:, o + 47:o + 48], sml[:, o + 46:o + 47], ACT.Ln), [("V", v_stats)])
        s_r = pb.emit(SCA, lambda e: e.activation(
            sml[:, o + 48:o + 49], sml[:, o + 47:o + 48], ACT.Exp, scale=-0.5),
            inc="S")
        v_negmr = pb.emit(VEC, lambda e: e.tensor_mul(
            sml[:, o + 49:o + 50], sml[:, o + 33:o + 34],
            sml[:, o + 48:o + 49]), [("S", s_r)], inc="V")
        if dump and LNDBG["tile"] is not None:
            tdbg = LNDBG["tile"]
            vfill(2)
            pb.emit(VEC, lambda e, tdbg=tdbg: e.tensor_copy(
                smlsnap[:, tdbg * 4:tdbg * 4 + 2], sml[:, o + 48:o + 50]))
            pb.emit(VEC, lambda e, tdbg=tdbg: e.tensor_copy(
                smlsnap[:, tdbg * 4 + 2:tdbg * 4 + 4], sml[:, o + 45:o + 47]))
            if 2 <= tdbg <= 5:
                dma_sync("SO", x_dump_v[tdbg - 2], xap, [("V", pb.cnt["V"])])
            LNDBG["tile"] = None
        s_cast = pb.emit(SCA, lambda e: e.activation(
            out_bf_ap, xap, ACT.Identity, bias=sml[:, o + 49:o + 50],
            scale=sml[:, o + 48:o + 49]), [("V", v_negmr)], inc="S")
        if out_f32_ap is not None:
            s_cast = pb.emit(SCA, lambda e: e.activation(
                out_f32_ap, xap, ACT.Identity, bias=sml[:, o + 49:o + 50],
                scale=sml[:, o + 48:o + 49]), inc="S")
        return s_cast

    def emit_transpose16(src_fn, dst_off, waits_t):
        vlast = None
        for r in range(2):
            for k in range(8):
                fc = r * 8 + k
                w = []
                if k == 0:
                    w = list(waits_t) if r == 0 else []
                    w.append(("V", state["pb7_free"] if r == 0 else vlast))
                pb.emit(TEN, lambda e, fc=fc, k=k: e.transpose(
                    PB7[:, k * 128:(k + 1) * 128], src_fn(fc), ident_bf[:]),
                    w, inc="T" if k == 7 else None)
            t_done = pb.cnt["T"]
            vlast = pb.emit(VEC, lambda e, r=r: e.tensor_copy(
                bbv(dst_off + r * 1024, 1024), PB7[:]), [("T", t_done)], inc="V")
        state["pb7_free"] = vlast
        return vlast

    ST_XLT = {}
    A1_T_END = A2_T_END = 0
    if "A" in phases:
        xcast = {}
        ttrans = {}
        vcopy = {}
        kevac = {}
        kst = {}
        kmm_last = {}
        A_ST = CFG.get("a_st", 8)
        for st in range(A_ST):
            for tt in range(4):
                t = 4 * st + tt
                xslot = fbv(OFF_XA4 + (t % 4) * 2048, 2048)
                w = [("S", xcast[t - 4])] if t >= 4 else []
                if dump and t >= 4:
                    w.append(("SO", pb.cnt["SO"]))
                lh_x = dma_sync(f"LX{t % 4}", xslot, di["x_seq"].rearrange(
                    "(t p) h -> t p h", p=128)[t], w)
                wv = [(f"LX{t % 4}", lh_x)]
                if t >= 2:
                    wv.append(("T", ttrans[t - 2]))
                    wv.append(("S", xcast[t - 2]))
                xlslot = bbv(OFF_XLN_AB + (t % 2) * 2048, 2048)
                LNDBG["tile"] = t
                xcast[t] = emit_ln(xslot, xlslot, None, wv, t % 2)
                wt = [("S", xcast[t]), ("G", G_CONST)]
                if st >= 1:
                    wt.append((f"SX{tt}", ST_XLT[4 * (st - 1) + tt]))
                vcopy[t] = emit_transpose16(
                    lambda fc, xl=xlslot: xl[:, fc * 128:(fc + 1) * 128],
                    OFF_XLTA + tt * 2048, wt)
                ttrans[t] = pb.cnt["T"]
                ST_XLT[t] = dma_sync(f"SX{tt}", xlt_scr[t],
                                     bbv(OFF_XLTA + tt * 2048, 2048),
                                     [("V", vcopy[t])])
            xlt_view = bbv(OFF_XLTA, 8192).rearrange(
                "p (tile fc tl) -> p tile fc tl", tile=4, fc=16)
            for dc in range(16):
                bank = P[dc % 2]
                if dc < 2:
                    w = [("LB_G", LG_WKT), ("V", vcopy[4 * st + 3])]
                else:
                    w = [("V", kevac[(st, dc - 2)])]
                for fc in range(16):
                    pb.emit(TEN, lambda e, dc=dc, fc=fc, bank=bank, xv=xlt_view:
                            e.matmul(
                        bank[:], wkT_sb[:, fc * 2048 + dc * 128:fc * 2048 + (dc + 1) * 128],
                        xv[:, :, fc, :], start=(fc == 0), stop=(fc == 15)),
                        w if fc == 0 else (), inc="T" if fc == 15 else None)
                kmm = pb.cnt["T"]
                kmm_last[st] = kmm
                kv_slot = bbv(OFF_KVO + (dc % 2) * 512, 512)
                w2 = [("T", kmm)]
                prevkey = (st, dc - 2) if dc >= 2 else (st - 1, dc + 14)
                if prevkey in kst:
                    w2.append((f"SKV{dc % 2}", kst[prevkey]))
                kevac[(st, dc)] = pb.emit(
                    VEC, lambda e, bank=bank, kv=kv_slot: e.tensor_copy(kv, bank[:]),
                    w2, inc="V")
                kst[(st, dc)] = dma_sync(
                    f"SKV{dc % 2}",
                    kT_scr[dc * 128:(dc + 1) * 128, st * 512:(st + 1) * 512],
                    kv_slot, [("V", kevac[(st, dc)])])
        if dump:
            dma_sync("SO", sml_dump, smlsnap[:], [("V", pb.cnt["V"])])
            dump_barrier()
        A1_T_END = pb.cnt["T"]

        LG_WVT = 0
        pb.wait(GPS, [("T", A1_T_END)])
        for q4 in range(4):
            LG_WVT = dma_cast(
                "LB_G", bbv(OFF_WKT + q4 * 8192, 8192),
                di["wv_r"][:, q4 * 8192:(q4 + 1) * 8192])
        wvT_sb = bbv(OFF_WKT, 32768)
        vevac = {}
        vst = {}
        vmm_last = {}
        for t in range(4 * A_ST):
            slot = bbv(OFF_XLTR + (t % 2) * 2048, 2048)
            w = [(f"SX{t % 4}", ST_XLT[t])]
            if t >= 2:
                w.append(("T", vmm_last[t - 2]))
            xltld = dma_sync(f"LX{t % 2}", slot, xlt_scr[t], w)
            for dvg in range(4):
                bank = P[dvg % 2]
                if dvg < 2:
                    w = [(f"LX{t % 2}", xltld), ("LB_G", LG_WVT)]
                    prevkey = (t - 1, dvg + 2)
                else:
                    w = []
                    prevkey = (t, dvg - 2)
                if prevkey in vevac:
                    w.append(("V", vevac[prevkey]))
                for fc in range(16):
                    pb.emit(TEN, lambda e, fc=fc, dvg=dvg, bank=bank, sl=slot:
                            e.matmul(
                        bank[:], sl[:, fc * 128:(fc + 1) * 128],
                        wvT_sb[:, fc * 2048 + dvg * 512:fc * 2048 + (dvg + 1) * 512],
                        start=(fc == 0), stop=(fc == 15)),
                        w if fc == 0 else (), inc="T" if fc == 15 else None)
                vmm = pb.cnt["T"]
                vmm_last[t] = vmm
                kv_slot = bbv(OFF_KVO + (dvg % 2) * 512, 512)
                w2 = [("T", vmm)]
                prevkey = (t, dvg - 2) if dvg >= 2 else (t - 1, dvg + 2)
                if prevkey in vst:
                    w2.append((f"SKV{dvg % 2}", vst[prevkey]))
                vevac[(t, dvg)] = pb.emit(
                    VEC, lambda e, bank=bank, kv=kv_slot: e.tensor_copy(kv, bank[:]),
                    w2, inc="V")
                vst[(t, dvg)] = dma_sync(
                    f"SKV{dvg % 2}",
                    vT_scr[t * 128:(t + 1) * 128, dvg * 512:(dvg + 1) * 512],
                    kv_slot, [("V", vevac[(t, dvg)])])
        A2_T_END = pb.cnt["T"]

    B_V_END = 0
    if "B" in phases:
        if dump:
            _vid = pb.emit(VEC, lambda e: e.tensor_copy(
                srlb[:], ident_bf[:]), [("T", A2_T_END)], inc="V")
            dma_sync("SO", id_dump[:, 0:128], srlb[:], [("V", _vid)])
            dump_barrier()
        bcast = {}
        btrans = {}
        for ot in range(NT_OWN):
            xslot = fbv(OFF_XF + (ot % 2) * 2048, 2048)
            w = [("S", bcast[ot - 2])] if ot >= 2 else []
            lh_x = dma_sync(f"LX{ot % 2}", xslot, di["x_own"].rearrange(
                "(t p) h -> t p h", p=128)[ot], w)
            wv = [(f"LX{ot % 2}", lh_x)]
            if ot == 0:
                wv.append(("T", A2_T_END))
            if ot >= 2:
                wv.append(("T", btrans[ot - 2]))
                wv.append(("S", bcast[ot - 2]))
            xlslot = bbv(OFF_XLN_AB + (ot % 2) * 2048, 2048)
            if dump and "A" not in phases:
                LNDBG["tile"] = ot
            bcast[ot] = emit_ln(xslot, xlslot, None, wv, ot % 2)
            wt = [("S", bcast[ot]), ("G", G_CONST), ("T", A2_T_END)]
            emit_transpose16(lambda fc, xl=xlslot: xl[:, fc * 128:(fc + 1) * 128],
                             OFF_XLTB + ot * 2048, wt)
            btrans[ot] = pb.cnt["T"]
        v_xltb = pb.cnt["V"]
        XLTB_SO = None
        if dump:
            XLTB_SO = dma_sync("SO", xltb_dump, bbv(OFF_XLTB, 16384),
                               [("V", v_xltb)])
            dump_barrier()
        skip_q = bool(CFG.get("noq"))
        xltb_view = bbv(OFF_XLTB, 16384).rearrange(
            "p (tile fc tl) -> p tile fc tl", tile=8, fc=16)
        qevac = {}
        qmm_done = {}
        lg_wq = {}

        def load_wq(dc, w):
            lg_wq[dc] = dma_cast(
                f"LWd{dc % 2}",
                bbv(OFF_WSLAB + (dc % 2) * 2048, 2048), di["wq_r"][dc], w)

        if not skip_q:
            wq0w = [("T", A2_T_END)]
            if CFG.get("serial_wq"):
                wq0w.append(("V", v_xltb))
            load_wq(0, wq0w)
            load_wq(1, [])
        for dc in range(16 if not skip_q else 0):
            slab = bbv(OFF_WSLAB + (dc % 2) * 2048, 2048)
            for tq in range(2):
                bank = P[tq]
                w = []
                if tq == 0:
                    w = [(f"LWd{dc % 2}", lg_wq[dc]), ("V", v_xltb)]
                if (dc - 1, tq) in qevac:
                    w.append(("V", qevac[(dc - 1, tq)]))
                for fc in range(16):
                    pb.emit(TEN, lambda e, fc=fc, tq=tq, bank=bank, sl=slab: e.matmul(
                        bank[:], sl[:, fc * 128:(fc + 1) * 128],
                        xltb_view[:, tq * 4:(tq + 1) * 4, fc, :],
                        start=(fc == 0), stop=(fc == 15)),
                        w if fc == 0 else (), inc="T" if fc == 15 else None)
                qmm_done[dc] = pb.cnt["T"]
                qevac[(dc, tq)] = pb.emit(
                    VEC, lambda e, dc=dc, tq=tq, bank=bank: e.tensor_copy(
                        bbv(OFF_Q + dc * 1024 + tq * 512, 512), bank[:]),
                    [("T", qmm_done[dc])], inc="V")
            if dc + 2 < 16:
                load_wq(dc + 2, [("T", qmm_done[dc])])
        if dump and "A" not in phases:
            dma_sync("SO", sml_dump, smlsnap[:], [("V", pb.cnt["V"])])
        B_V_END = pb.cnt["V"]
        QDUMP_SO = None
        if dump:
            QDUMP_SO = dma_sync("SO", q_dump, bbv(OFF_Q, 16384), [("V", B_V_END)])
            dump_barrier()

    C_V_END = C_T_END = 0
    if "C" in phases:
        kvload = {}
        head_done = {}
        prev_evac = B_V_END
        prev_l_user = 0
        prev_rlb_user = 0
        bank_free_S = {0: 0, 1: 0}
        for h in range(heads):
            buf = h % 2
            koff = OFF_KVH + buf * 8192
            voff = OFF_KVH + buf * 8192 + 4096
            w = [("V", B_V_END)] if h < 2 else [("T", head_done[h - 2])]
            dma_sync(f"LK{h % 2}", bbv(koff, 4096),
                     kT_scr[h * 128:(h + 1) * 128, :], w)
            for vh in range(2):
                kvload[h] = dma_sync(
                    f"LK{h % 2}",
                    bbv(voff + vh * 2048, 2048).rearrange("p (kt d) -> p kt d", kt=16),
                    vT_scr.rearrange("(kt p) d -> p kt d", p=128)[
                        :, vh * 16:(vh + 1) * 16, h * 128:(h + 1) * 128])
            for s in range(NT_OWN):
                ngr = s + 1
                exp_done = {}
                av_done = {}
                for g in range(ngr):
                    bank = P[g % 2]
                    w = []
                    if g == 0:
                        w = [(f"LK{h % 2}", kvload[h])]
                        if h == 0 and s == 0:
                            w.append(("G", G_CONST))
                    if g >= 2:
                        w.append(("S", exp_done[g - 2]))
                    elif bank_free_S[g % 2]:
                        w.append(("S", bank_free_S[g % 2]))
                    for jj in range(4):
                        kt = 4 * g + jj
                        pb.emit(TEN, lambda e, kt=kt, jj=jj, bank=bank, ko=koff,
                                h=h, s=s: e.matmul(
                            bank[:, jj * 128:(jj + 1) * 128],
                            bbv(ko + kt * 128, 128),
                            bbv(OFF_Q + h * 1024 + s * 128, 128),
                            start=True, stop=True),
                            w if jj == 0 else (), inc="T" if jj == 3 else None)
                    smm = pb.cnt["T"]
                    wexp = [("T", smm)]
                    if g >= 2:
                        wexp.append(("T", av_done[g - 2]))
                    ptap = bbv(OFF_PT + (g % 2) * 512, 512)
                    exp_done[g] = pb.emit(SCA, lambda e, bank=bank, pt=ptap:
                                          e.activation(
                        pt, bank[:], ACT.Exp, bias=negC[:, 0:1], scale=SCALE),
                        wexp, inc="S")
                    bank_free_S[g % 2] = exp_done[g]
                    gate_sem = ("S", exp_done[g])
                    if g == s:
                        side = 0 if s < 4 else 1
                        vm = pb.emit(VEC, lambda e, pt=ptap, side=side: e.tensor_mul(
                            pt, pt, BB[:, OFF_CMASK + side * 512:
                                       OFF_CMASK + (side + 1) * 512]),
                            [("S", exp_done[g]), ("LB_G", LG_WGC)], inc="V")
                        gate_sem = ("V", vm)
                    w = [gate_sem]
                    if g == 0:
                        w.append(("V", prev_evac))
                        if prev_l_user:
                            w.append(("V", prev_l_user))
                    first = (g == 0)
                    last = (g == ngr - 1)
                    for jj in range(4):
                        kt = 4 * g + jj
                        pb.emit(TEN, lambda e, kt=kt, jj=jj, pt=ptap, vo=voff,
                                fi=first, la=last: e.matmul(
                            P[2][:, 0:128], bbv(vo + kt * 128, 128),
                            pt[:, jj * 128:(jj + 1) * 128],
                            start=(fi and jj == 0), stop=(la and jj == 3)),
                            w if jj == 0 else ())
                    for jj in range(4):
                        pb.emit(TEN, lambda e, jj=jj, pt=ptap, fi=first, la=last:
                                e.matmul(
                            P[3][0:1, 0:128], onescol_bf[:],
                            pt[:, jj * 128:(jj + 1) * 128],
                            start=(fi and jj == 0), stop=(la and jj == 3)),
                            (), inc="T" if jj == 3 else None)
                    av_done[g] = pb.cnt["T"]
                v_rl = pb.emit(VEC, lambda e: e.reciprocal(srl[:], P[3][0:1, 0:128]),
                               [("T", av_done[ngr - 1])], inc="V")
                prev_l_user = v_rl
                w = [("V", v_rl)]
                if prev_rlb_user:
                    w.append(("V", prev_rlb_user))
                t_rlb = pb.emit(TEN, lambda e: e.matmul(
                    P[4][:, 0:128], onesrow_f32[:], srl[:], start=True, stop=True),
                    w, inc="T")
                v_cp = pb.emit(VEC, lambda e: e.tensor_copy(srlb[:], P[4][:, 0:128]),
                               [("T", t_rlb)], inc="V")
                vfill(2)
                wev = [("V", v_cp)]
                if h == 0 and s == 0 and dump and "B" in phases and XLTB_SO:
                    wev.append(("SO", XLTB_SO))
                prev_evac = pb.emit(VEC, lambda e, h=h, s=s: e.tensor_mul(
                    bbv(OFF_ATT + h * 1024 + s * 128, 128), P[2][:, 0:128], srlb[:]),
                    wev, inc="V")
                prev_rlb_user = prev_evac
            head_done[h] = pb.cnt["T"]
        C_V_END = pb.cnt["V"]
        C_T_END = pb.cnt["T"]
        ATD_SO = None
        if dump:
            ATD_SO = dma_sync("SO", at_dump, bbv(OFF_ATT, 16384), [("V", C_V_END)])
            dump_barrier()

    D_T_END = D_V_END = D_SO_END = 0
    if "D" in phases:
        att_view = bbv(OFF_ATT, 16384)
        d_evac = {}
        oq_done = {}
        lg_woq = {}

        def load_woq(oq, w):
            lg_woq[oq] = dma_cast(
                f"LWd{oq % 2}",
                bbv(OFF_WOQ + (oq % 2) * 8192, 8192), di["wo_r"][oq], w)

        if dump:
            _vid3 = pb.emit(VEC, lambda e: e.tensor_copy(
                srlb[:], ident_bf[:]), [("T", pb.cnt["T"]), ("V", pb.cnt["V"])],
                inc="V")
            dma_sync("SO", id_dump2, srlb[:], [("V", _vid3)])
            dump_barrier()
        load_woq(0, [("T", C_T_END)])
        load_woq(1, [])
        for oq in range(4 if not CFG.get("skip_dmm") else 0):
            woq = bbv(OFF_WOQ + (oq % 2) * 8192, 8192)
            for quad in range(2):
                for dc in range(16):
                    for tl in range(4):
                        tile = quad * 4 + tl
                        w = []
                        if dc == 0 and tl == 0:
                            w = [(f"LWd{oq % 2}", lg_woq[oq]), ("V", C_V_END)]
                            pk = (oq, 0, 3) if quad == 1 else (oq - 1, 1, 3)
                            if pk in d_evac:
                                w.append(("V", d_evac[pk]))
                        pb.emit(TEN, lambda e, dc=dc, tile=tile, tl=tl, wq=woq:
                                e.matmul(
                            P[tl][:], att_view[:, dc * 1024 + tile * 128:
                                               dc * 1024 + (tile + 1) * 128],
                            wq[:, dc * 512:(dc + 1) * 512],
                            start=(dc == 0), stop=(dc == 15)),
                            w, inc="T" if dc == 15 else None)
                dmm = pb.cnt["T"]
                oq_done[oq] = dmm
                for tl in range(4):
                    tile = quad * 4 + tl
                    if CFG.get("evac_alt"):
                        d_evac[(oq, quad, tl)] = pb.emit(
                            VEC, lambda e, tl=tl: e.tensor_copy(
                                srlb[:], P[tl][:, 0:128]),
                            [("T", dmm)], inc="V")
                    else:
                        d_evac[(oq, quad, tl)] = pb.emit(
                            VEC, lambda e, tile=tile, oq=oq, tl=tl: e.tensor_copy(
                                fbv(OFF_ACC + tile * 2048 + oq * 512, 512), P[tl][:]),
                            [("T", dmm)], inc="V")
            if oq + 2 < 4:
                load_woq(oq + 2, [("T", oq_done[oq])])
        if dump:
            _vid2 = pb.emit(VEC, lambda e: e.tensor_copy(
                srlb[:], ident_bf[:]), [("T", pb.cnt["T"]), ("V", pb.cnt["V"])],
                inc="V")
            dma_sync("SO", id_dump[:, 128:256], srlb[:], [("V", _vid2)])
            dump_barrier()
        dcast = {}
        dtrans = {}
        st_res = {}
        for tile in range(NT_OWN if not CFG.get("skip_dmm") else 0):
            xslot = fbv(OFF_XF + (tile % 2) * 2048, 2048)
            w = [("V", d_evac[(3, tile // 4, tile % 4)])]
            if tile >= 2:
                w.append(("S", dcast[tile - 2]))
                w.append(("SO", pb.cnt["SO"]))
            lh_x = dma_sync(f"LX{tile % 2}", xslot, di["x_own"].rearrange(
                "(t p) h -> t p h", p=128)[tile], w)
            x2ap = fbv(OFF_ACC + tile * 2048, 2048)
            v_x2 = pb.emit(VEC, lambda e, a=x2ap, xs=xslot: e.tensor_add(a, a, xs),
                           [(f"LX{tile % 2}", lh_x)], inc="V")
            dma_sync("SO", x2_scr.rearrange("(t p) h -> t p h", p=128)[tile], x2ap,
                     [("V", v_x2)])
            xlslot = bbv(OFF_XLN_DE + (tile % 2) * 2048, 2048)
            wv = [("V", v_x2)]
            if tile >= 2:
                wv.append(("T", dtrans[tile - 2]))
                wv.append(("S", dcast[tile - 2]))
            dcast[tile] = emit_ln(x2ap, xlslot, xslot, wv, tile % 2)
            st_res[tile] = dma_sync(
                "SO", res_scr.rearrange("(t p) h -> t p h", p=128)[tile], xslot,
                [("S", dcast[tile])])
            wtr = [("S", dcast[tile])]
            if tile == 0 and dump and "B" in phases and QDUMP_SO:
                wtr.append(("SO", QDUMP_SO))
            emit_transpose16(lambda fc, xl=xlslot: xl[:, fc * 128:(fc + 1) * 128],
                             OFF_XTT + tile * 2048, wtr)
            dtrans[tile] = pb.cnt["T"]
        if dump and not CFG.get("skip_dmm"):
            dma_sync("SO", xtt_dump, bbv(OFF_XTT, 16384),
                     [("T", pb.cnt["T"]), ("V", pb.cnt["V"])])
            dump_barrier()
        D_T_END = pb.cnt["T"]
        D_V_END = pb.cnt["V"]
        D_SO_END = pb.cnt["SO"]

    if "E" in phases:
        xtt_view = bbv(OFF_XTT, 16384).rearrange(
            "p (tile fc tl) -> p tile fc tl", tile=8, fc=16)

        def xtt_rhs(fc, th):
            return xtt_view[:, th * 4:(th + 1) * 4, fc, :]

        prev_iter_T = D_T_END
        prev_iter_V = D_V_END
        prev_iter_SO = D_SO_END
        for j in range(2):
            w_gate_done = {}
            for tile in range(NT_OWN):
                if tile == 0:
                    w = [("T", prev_iter_T), ("V", prev_iter_V), ("LB_G", LG_WGC)]
                else:
                    w = [("V", w_gate_done[tile - 1])]
                for fc in range(16):
                    pb.emit(TEN, lambda e, fc=fc, tile=tile, j=j: e.matmul(
                        P[4][:, 0:8], xtt_view[:, tile, fc, :],
                        wg_bf[:, j * 128 + fc * 8:j * 128 + (fc + 1) * 8],
                        start=(fc == 0), stop=False),
                        w if fc == 0 else ())
                t_g = pb.emit(TEN, lambda e, j=j: e.matmul(
                    P[4][:, 0:8], onesrow_f32[:], bg_sb[0:1, j * 8:(j + 1) * 8],
                    start=False, stop=True), [("LB_H", LH_BIAS)], inc="T")
                o = 100
                pb.emit(VEC, lambda e: e.tensor_copy(sml[:, o:o + 8], P[4][:, 0:8]),
                        [("T", t_g)])
                vfill(2)
                pb.emit(VEC, lambda e: e.tensor_reduce(
                    sml[:, o + 8:o + 9], sml[:, o:o + 8], AX, ALU.min))
                vfill(2)
                pb.emit(VEC, lambda e: e.tensor_scalar(
                    sml[:, o + 10:o + 18], sml[:, o:o + 8], sml[:, o + 8:o + 9],
                    None, ALU.is_equal))
                vfill(2)
                pb.emit(VEC, lambda e: e.scalar_tensor_tensor(
                    sml[:, o + 18:o + 26], sml[:, o + 10:o + 18], c1e30[:, 0:1],
                    sml[:, o:o + 8], ALU.mult, ALU.add))
                vfill(2)
                pb.emit(VEC, lambda e: e.tensor_reduce(
                    sml[:, o + 9:o + 10], sml[:, o + 18:o + 26], AX, ALU.min))
                vfill(2)
                pb.emit(VEC, lambda e: e.tensor_scalar(
                    sml[:, o + 10:o + 18], sml[:, o:o + 8], sml[:, o + 9:o + 10],
                    None, ALU.is_gt))
                vfill(2)
                pb.emit(VEC, lambda e: e.scalar_tensor_tensor(
                    sml[:, o + 18:o + 26], sml[:, o:o + 8], c1e4[:, 0:1],
                    sml[:, o + 10:o + 18], ALU.add, ALU.mult))
                vfill(2)
                v_m = pb.emit(VEC, lambda e: e.reduce_max(
                    sml[:, o + 8:o + 9], sml[:, o + 18:o + 26], AX, negate=True),
                    inc="V")
                s_e = pb.emit(SCA, lambda e: e.activation(
                    sml[:, o + 26:o + 34], sml[:, o + 18:o + 26], ACT.Exp,
                    bias=sml[:, o + 8:o + 9]), [("V", v_m)], inc="S")
                pb.emit(VEC, lambda e: e.reduce_sum(
                    sml[:, o + 9:o + 10], sml[:, o + 26:o + 34], AX), [("S", s_e)])
                vfill(2)
                pb.emit(VEC, lambda e: e.reciprocal(
                    sml[:, o + 8:o + 9], sml[:, o + 9:o + 10]))
                vfill(2)
                v_w = pb.emit(VEC, lambda e, tile=tile: e.tensor_scalar_mul(
                    w_all[:, tile * 8:(tile + 1) * 8], sml[:, o + 26:o + 34],
                    sml[:, o + 8:o + 9]), inc="V")
                t_wt = pb.emit(TEN, lambda e, tile=tile: e.transpose(
                    P[5][0:8, 0:128], w_all[:, tile * 8:(tile + 1) * 8],
                    ident_f32[:]), [("V", v_w)], inc="T")
                w_gate_done[tile] = pb.emit(VEC, lambda e, tile=tile: e.tensor_copy(
                    wcomb[0:6, tile * 128:(tile + 1) * 128], P[5][0:6, 0:128]),
                    [("T", t_wt)], inc="V")
            acc_init_done = {}
            prev_ai = None
            for tile in range(NT_OWN):
                for hc in range(4):
                    w = [("V", w_gate_done[tile])]
                    if tile == 0 and hc == 0:
                        w.append(("SO", prev_iter_SO))
                        w.append(("LB_G", LG_BDN))
                    if prev_ai:
                        w.append(("V", prev_ai))
                    t_ai = pb.emit(TEN, lambda e, tile=tile, hc=hc: e.matmul(
                        P[6][:], wcomb[0:7, tile * 128:(tile + 1) * 128],
                        bdnsd_sb[0:7, hc * 512:(hc + 1) * 512],
                        start=True, stop=True), w, inc="T")
                    prev_ai = pb.emit(
                        VEC, lambda e, tile=tile, hc=hc: e.tensor_copy(
                            fbv(OFF_ACC + tile * 2048 + hc * 512, 512), P[6][:]),
                        [("T", t_ai)], inc="V")
            V_ACCINIT = prev_ai
            DSO = None
            if dump and j == 0:
                dma_sync("SO", w_dump, w_all[:], [("V", V_ACCINIT)])
                DSO = dma_sync("SO", accd[0], fbv(OFF_ACC, 16384),
                               [("V", V_ACCINIT)])
                dump_barrier()

            sh_done = {}
            lg_su = {}

            def load_su(oc_h, w):
                oc, hh = oc_h
                lg_su[oc_h] = dma_cast(
                    f"LWd{hh}",
                    bbv(OFF_R4 + hh * 1024, 1024),
                    di["wsu_r2"][oc][:, hh * 1024:(hh + 1) * 1024], w)

            wsu0 = [("T", prev_iter_T)]
            if dump and "C" in phases and ATD_SO:
                wsu0.append(("SO", ATD_SO))
            load_su((0, 0), wsu0)
            load_su((0, 1), [])
            lg_sa = dma_cast(
                "LB_G", bbv(OFF_R4 + 2048, 8192), di["wsa_r2"],
                [("T", prev_iter_T)])
            for oc in range(16):
                bank = P[oc % 2]
                s_prev_th = None
                for th in range(2):
                    w = []
                    if th == 0:
                        w = [("LWd0", lg_su[(oc, 0)]), ("LWd1", lg_su[(oc, 1)]),
                             ("V", V_ACCINIT)]
                        if oc >= 1:
                            w.append(("S", sh_done[oc - 1]))
                    else:
                        w = [("S", s_prev_th)]
                    for fc in range(16):
                        sl = bbv(OFF_R4 + (fc // 8) * 1024, 1024)
                        pb.emit(TEN, lambda e, fc=fc, th=th, bank=bank, sl=sl:
                                e.matmul(
                            bank[:], sl[:, (fc % 8) * 128:(fc % 8 + 1) * 128],
                            xtt_rhs(fc, th), start=(fc == 0), stop=(fc == 15)),
                            w if fc == 0 else (), inc="T" if fc == 15 else None)
                    t_sh = pb.cnt["T"]
                    s_prev_th = pb.emit(SCA, lambda e, oc=oc, th=th, bank=bank:
                                        e.activation(
                        bbv(OFF_R2 + oc * 1024 + th * 512, 512), bank[:],
                        ACT.Identity, bias=bsu_sb[:, oc:oc + 1]),
                        [("T", t_sh)], inc="S")
                sh_done[oc] = s_prev_th
                if oc + 1 < 16:
                    load_su((oc + 1, 0), [("S", s_prev_th)])
                    load_su((oc + 1, 1), [])
            S_SH_END = pb.cnt["S"]
            if dump and j == 0:
                dma_sync("SO", sh_dump, bbv(OFF_R2, 16384), [("S", S_SH_END)])
                dump_barrier()
            wsa_sb = bbv(OFF_R4 + 2048, 8192)
            z_done = {}
            prev_zmul = 0
            for g2 in range(2):
                for e2c in range(8):
                    zc = g2 * 8 + e2c
                    bank = P[zc % 2]
                    v_prev_th = None
                    for th in range(2):
                        w = []
                        if th == 0:
                            if zc == 0:
                                w = [("LB_G", lg_sa), ("S", S_SH_END)]
                            elif zc >= 2:
                                w = [("V", z_done[zc - 2])]
                        else:
                            w = [("V", v_prev_th)]
                        for ec in range(8):
                            pb.emit(TEN, lambda e, ec=ec, e2c=e2c, g2=g2, th=th,
                                    bank=bank: e.matmul(
                                bank[:],
                                wsa_sb[:, ec * 1024 + e2c * 128:ec * 1024 + (e2c + 1) * 128],
                                bbv(OFF_R2 + (g2 * 8 + ec) * 1024 + th * 512, 512),
                                start=(ec == 0), stop=(ec == 7)),
                                w if ec == 0 else (), inc="T" if ec == 7 else None)
                        t_a = pb.cnt["T"]
                        ws = [("T", t_a)]
                        if prev_zmul:
                            ws.append(("V", prev_zmul))
                        s_si = pb.emit(SCA, lambda e, e2c=e2c, bank=bank:
                                       e.activation(
                            tmp_bf[:], bank[:], ACT.Silu,
                            bias=bsa_sb[:, e2c:e2c + 1]), ws, inc="S")
                        v_prev_th = pb.emit(VEC, lambda e, zc=zc, th=th: e.tensor_mul(
                            bbv(OFF_R3 + zc * 1024 + th * 512, 512), tmp_bf[:],
                            bbv(OFF_R2 + zc * 1024 + th * 512, 512)),
                            [("S", s_si)], inc="V")
                        prev_zmul = v_prev_th
                    z_done[zc] = v_prev_th
            V_Z_END = pb.cnt["V"]
            if dump and j == 0:
                ZSO = dma_sync("SO", z_dump, bbv(OFF_R3, 16384), [("V", V_Z_END)])
                dump_barrier()
            sd_evac = {}
            lg_sd = {}

            def load_sd(hq, w):
                lg_sd[hq] = dma_cast(
                    f"LWd{hq % 2}",
                    bbv(OFF_R2 + (hq % 2) * 8192, 8192), di["wsd_r2"][hq], w)

            w0 = [("V", V_Z_END), ("S", S_SH_END)]
            if DSO:
                w0.append(("SO", DSO))
            load_sd(0, w0)
            load_sd(1, [])
            for hq in range(4):
                wsdq = bbv(OFF_R2 + (hq % 2) * 8192, 8192)
                for quad in range(2):
                    for zc in range(16):
                        for tl in range(4):
                            tile = quad * 4 + tl
                            w = []
                            if zc == 0 and tl == 0:
                                w = [(f"LWd{hq % 2}", lg_sd[hq]), ("V", V_Z_END)]
                                pk = (hq, 0, 3) if quad == 1 else (hq - 1, 1, 3)
                                if pk in sd_evac:
                                    w.append(("V", sd_evac[pk]))
                            pb.emit(TEN, lambda e, zc=zc, tile=tile, tl=tl,
                                    wq=wsdq: e.matmul(
                                P[tl][:],
                                bbv(OFF_R3 + zc * 1024 + tile * 128, 128),
                                wq[:, zc * 512:(zc + 1) * 512],
                                start=(zc == 0), stop=(zc == 15)),
                                w, inc="T" if zc == 15 else None)
                    t_sd = pb.cnt["T"]
                    for tl in range(4):
                        tile = quad * 4 + tl
                        sd_evac[(hq, quad, tl)] = pb.emit(
                            VEC, lambda e, tile=tile, hq=hq, tl=tl: e.tensor_add(
                                fbv(OFF_ACC + tile * 2048 + hq * 512, 512),
                                fbv(OFF_ACC + tile * 2048 + hq * 512, 512),
                                P[tl][:]), [("T", t_sd)], inc="V")
                if hq + 2 < 4:
                    load_sd(hq + 2, [("T", pb.cnt["T"])])
            V_SD_END = pb.cnt["V"]
            DSO2 = None
            if dump and j == 0:
                DSO2 = dma_sync("SO", accd[1], fbv(OFF_ACC, 16384),
                                [("V", V_SD_END)])
                dump_barrier()

            ex_evac_last = V_SD_END
            prev_h_readers_T = pb.cnt["T"]
            for n in range(NE):
                h_done = {}
                lg_up = {}

                def load_up(ec, w, n=n):
                    lg_up[ec] = dma_cast(
                        f"LWa{ec % 2}",
                        bbv(OFF_R4 + (ec % 2) * 2048, 2048),
                        di["wup_r"][n][ec], w)

                w0 = [("T", prev_h_readers_T), ("V", ex_evac_last)]
                if n == 0 and DSO2:
                    w0.append(("SO", DSO2))
                load_up(0, w0)
                load_up(1, [])
                for ec in range(8):
                    bank = P[ec % 2]
                    slab = bbv(OFF_R4 + (ec % 2) * 2048, 2048)
                    s_prev_th = None
                    for th in range(2):
                        w = []
                        if th == 0:
                            w = [(f"LWa{ec % 2}", lg_up[ec])]
                            if ec == 0:
                                w.append(("V", ex_evac_last))
                            if ec >= 2:
                                w.append(("S", h_done[ec - 2]))
                        else:
                            w = [("S", s_prev_th)]
                        for fc in range(16):
                            pb.emit(TEN, lambda e, fc=fc, th=th, bank=bank,
                                    sl=slab: e.matmul(
                                bank[:], sl[:, fc * 128:(fc + 1) * 128],
                                xtt_rhs(fc, th), start=(fc == 0), stop=(fc == 15)),
                                w if fc == 0 else (), inc="T" if fc == 15 else None)
                        t_h = pb.cnt["T"]
                        s_prev_th = pb.emit(SCA, lambda e, n=n, ec=ec, th=th,
                                            bank=bank: e.activation(
                            bbv(OFF_R2 + ec * 1024 + th * 512, 512), bank[:],
                            ACT.Identity, bias=bup_sb[:, n * 8 + ec:n * 8 + ec + 1]),
                            [("T", t_h)], inc="S")
                    h_done[ec] = s_prev_th
                    if ec + 2 < 8:
                        load_up(ec + 2, [("T", pb.cnt["T"])])
                S_H_END = pb.cnt["S"]
                prev_h_readers_T = pb.cnt["T"]
                u_done = {}
                prev_umul = 0
                lg_sw = {}

                def load_sw(e2c, w, n=n):
                    lg_sw[e2c] = dma_cast(
                        f"LWb{e2c % 2}",
                        bbv(OFF_R4 + 4096 + (e2c % 2) * 1024, 1024),
                        di["wsw_r"][n][e2c], w)

                load_sw(0, [("V", ex_evac_last)])
                load_sw(1, [])
                for e2c in range(8):
                    bank = P[2 + (e2c % 2)]
                    slab = bbv(OFF_R4 + 4096 + (e2c % 2) * 1024, 1024)
                    v_prev_th = None
                    for th in range(2):
                        w = []
                        if th == 0:
                            w = [(f"LWb{e2c % 2}", lg_sw[e2c]), ("S", S_H_END)]
                            if e2c >= 2:
                                w.append(("V", u_done[e2c - 2]))
                        else:
                            w = [("V", v_prev_th)]
                        for ec in range(8):
                            pb.emit(TEN, lambda e, ec=ec, th=th, bank=bank,
                                    sl=slab: e.matmul(
                                bank[:], sl[:, ec * 128:(ec + 1) * 128],
                                bbv(OFF_R2 + ec * 1024 + th * 512, 512),
                                start=(ec == 0), stop=(ec == 7)),
                                w if ec == 0 else (), inc="T" if ec == 7 else None)
                        t_u = pb.cnt["T"]
                        ws = [("T", t_u)]
                        if prev_umul:
                            ws.append(("V", prev_umul))
                        s_si = pb.emit(SCA, lambda e, n=n, e2c=e2c, bank=bank:
                                       e.activation(
                            tmp_bf[:], bank[:], ACT.Silu,
                            bias=bsw_sb[:, n * 8 + e2c:n * 8 + e2c + 1]),
                            ws, inc="S")
                        v_prev_th = pb.emit(VEC, lambda e, e2c=e2c, th=th:
                                            e.tensor_mul(
                            bbv(OFF_R3 + e2c * 1024 + th * 512, 512), tmp_bf[:],
                            bbv(OFF_R2 + e2c * 1024 + th * 512, 512)),
                            [("S", s_si)], inc="V")
                        prev_umul = v_prev_th
                    u_done[e2c] = v_prev_th
                    if e2c + 2 < 8:
                        load_sw(e2c + 2, [("T", pb.cnt["T"])])
                V_U_END = pb.cnt["V"]
                dn_evac = {}
                lg_dn = {}

                def load_dn(hq, w, n=n):
                    lg_dn[hq] = dma_cast(
                        f"LWc{hq % 2}",
                        bbv(OFF_R2 + (hq % 2) * 4096, 4096),
                        di["wdn_r"][n][hq], w)

                load_dn(0, [("V", V_U_END)])
                load_dn(1, [("V", V_U_END)])
                for hq in range(4):
                    wdnq = bbv(OFF_R2 + (hq % 2) * 4096, 4096)
                    for quad in range(2):
                        for ec in range(8):
                            for tl in range(4):
                                tile = quad * 4 + tl
                                w = []
                                if ec == 0 and tl == 0:
                                    w = [(f"LWc{hq % 2}", lg_dn[hq]), ("V", V_U_END)]
                                    pk = (hq, 0, 3) if quad == 1 else (hq - 1, 1, 3)
                                    if pk in dn_evac:
                                        w.append(("V", dn_evac[pk]))
                                pb.emit(TEN, lambda e, ec=ec, tile=tile, tl=tl,
                                        wq=wdnq: e.matmul(
                                    P[tl][:],
                                    bbv(OFF_R3 + ec * 1024 + tile * 128, 128),
                                    wq[:, ec * 512:(ec + 1) * 512],
                                    start=(ec == 0), stop=(ec == 7)),
                                    w, inc="T" if ec == 7 else None)
                        t_dn = pb.cnt["T"]
                        for tl in range(4):
                            tile = quad * 4 + tl
                            dn_evac[(hq, quad, tl)] = pb.emit(
                                VEC, lambda e, tile=tile, hq=hq, n=n, tl=tl:
                                e.scalar_tensor_tensor(
                                    fbv(OFF_ACC + tile * 2048 + hq * 512, 512),
                                    P[tl][:],
                                    w_all[:, tile * 8 + n:tile * 8 + n + 1],
                                    fbv(OFF_ACC + tile * 2048 + hq * 512, 512),
                                    ALU.mult, ALU.add), [("T", t_dn)], inc="V")
                    if hq + 2 < 4:
                        load_dn(hq + 2, [("T", pb.cnt["T"])])
                ex_evac_last = pb.cnt["V"]
                prev_h_readers_T = pb.cnt["T"]

            DSO3 = None
            if dump and j == 0:
                DSO3 = dma_sync("SO", accd[2], fbv(OFF_ACC, 16384),
                                [("V", ex_evac_last)])
                dump_barrier()
            comb_v = {}
            comb_trans = {}
            comb_st = {}
            for tile in range(NT_OWN):
                xslot = fbv(OFF_XF + (tile % 2) * 2048, 2048)
                w = [("V", ex_evac_last), ("SO", D_SO_END)]
                if DSO3:
                    w.append(("SO", DSO3))
                if tile >= 2:
                    w.append(("V", comb_v[tile - 2]))
                    w.append(("SO", pb.cnt["SO"]))
                lh_res = dma_sync(f"LX{tile % 2}", xslot, res_scr.rearrange(
                    "(t p) h -> t p h", p=128)[tile], w)
                accap = fbv(OFF_ACC + tile * 2048, 2048)
                v_add = pb.emit(VEC, lambda e, a=accap, xs=xslot: e.tensor_add(
                    a, a, xs), [(f"LX{tile % 2}", lh_res)], inc="V")
                if j == 0:
                    xlslot = bbv(OFF_XLN_DE + (tile % 2) * 2048, 2048)
                    w = [("V", v_add)]
                    if tile >= 2:
                        w.append(("T", comb_trans[tile - 2]))
                    s_c = pb.emit(SCA, lambda e, a=accap, xl=xlslot:
                                  e.activation(xl, a, ACT.Copy), w, inc="S")
                    comb_v[tile] = emit_transpose16(
                        lambda fc, xl=xlslot: xl[:, fc * 128:(fc + 1) * 128],
                        OFF_XTT + tile * 2048, [("S", s_c)])
                    comb_trans[tile] = pb.cnt["T"]
                else:
                    lh_x2 = dma_sync(f"LX{tile % 2}", xslot, x2_scr.rearrange(
                        "(t p) h -> t p h", p=128)[tile], [("V", v_add)])
                    v_f = pb.emit(VEC, lambda e, a=accap, xs=xslot: e.tensor_add(
                        a, a, xs), [(f"LX{tile % 2}", lh_x2)], inc="V")
                    comb_v[tile] = v_f
                    comb_st[tile] = dma_sync(
                        "SO", out_own.rearrange("(t p) h -> t p h", p=128)[tile],
                        accap, [("V", v_f)])
            prev_iter_T = pb.cnt["T"]
            prev_iter_V = pb.cnt["V"]
            prev_iter_SO = pb.cnt["SO"]

    pb.wait("sync", [(s, pb.cnt[s]) for s in SYNC_SEMS if pb.cnt[s]])
    pb.wait("gpsimd", [(s, pb.cnt[s]) for s in GPS_SEMS if pb.cnt[s]])
    pb.run()
    return nc


def _own_tiles(i):
    return [4 * s + i if s < 4 else 4 * s + 3 - i for s in range(8)]


def _make_cmask(i):
    m = np.zeros((2, 128, 512), np.float32)
    tri = (np.arange(128)[:, None] <= np.arange(128)[None, :]).astype(np.float32)
    for side, r in enumerate((i, 3 - i)):
        for jj in range(4):
            blk = m[side, :, jj * 128:(jj + 1) * 128]
            if jj < r:
                blk[:] = 1.0
            elif jj == r:
                blk[:] = tri
    return m


def make_in_maps(x, w_qkv, w_o, Wg, bg, Wup, bup, Wsw, bsw, Wdn, bdn,
                 Wsu, bsu, Wsa, bsa, Wsd, bsd):
    f = lambda a: np.ascontiguousarray(np.asarray(a, dtype=np.float32))
    x = f(x); w_qkv = f(w_qkv)
    wqT = np.ascontiguousarray(w_qkv[0:H].T)
    wkT = np.ascontiguousarray(w_qkv[H:2 * H].T)
    wvT = np.ascontiguousarray(w_qkv[2 * H:3 * H].T)
    woT = np.ascontiguousarray(f(w_o).T)
    c = np.ascontiguousarray
    shared = dict(
        wq_r=c(wqT.reshape(16, 128, 16, 128).transpose(2, 1, 0, 3).reshape(16, 128, 2048)),
        wk_r=c(wkT.reshape(16, 128, 2048).transpose(1, 0, 2).reshape(128, 32768)),
        wv_r=c(wvT.reshape(16, 128, 2048).transpose(1, 0, 2).reshape(128, 32768)),
        wo_r=c(woT.reshape(16, 128, 4, 512).transpose(2, 1, 0, 3).reshape(4, 128, 8192)),
        wg_in=c(f(Wg).reshape(2, 16, 128, 8).transpose(2, 0, 1, 3).reshape(128, 256)),
        bg_in=f(bg).reshape(1, 16),
        wup_r=c(f(Wup).reshape(6, 16, 128, 8, 128).transpose(0, 3, 2, 1, 4).reshape(6, 8, 128, 2048)),
        wsw_r=c(f(Wsw).reshape(6, 8, 128, 8, 128).transpose(0, 3, 2, 1, 4).reshape(6, 8, 128, 1024)),
        wdn_r=c(f(Wdn).reshape(6, 8, 128, 4, 512).transpose(0, 3, 2, 1, 4).reshape(6, 4, 128, 4096)),
        wsu_r2=c(f(Wsu).reshape(16, 128, 16, 128).transpose(2, 1, 0, 3).reshape(16, 128, 2048)),
        wsa_r2=c(f(Wsa).reshape(8, 128, 1024).transpose(1, 0, 2).reshape(128, 8192)),
        wsd_r2=c(f(Wsd).reshape(16, 128, 4, 512).transpose(2, 1, 0, 3).reshape(4, 128, 8192)),
        bup_r=np.ascontiguousarray(
            f(bup).reshape(6, 8, 128).transpose(2, 0, 1).reshape(128, 48)),
        bsw_r=np.ascontiguousarray(
            f(bsw).reshape(6, 8, 128).transpose(2, 0, 1).reshape(128, 48)),
        bsu_r=np.ascontiguousarray(f(bsu).reshape(16, 128).T),
        bsa_r=np.ascontiguousarray(f(bsa).reshape(8, 128).T),
        bdnsd=np.ascontiguousarray(
            np.concatenate([f(bdn), f(bsd)[None]], 0)),
    )
    in_maps = []
    for c in range(8):
        b, i = c // 4, c % 4
        m = dict(shared)
        m["x_own"] = np.ascontiguousarray(
            x[b].reshape(NT_SEQ, 128, H)[_own_tiles(i)].reshape(1024, H))
        m["x_seq"] = x[b]
        m["cmask"] = _make_cmask(i)
        in_maps.append(m)
    return in_maps


def kernel(L0_attn_hidden_states, L0_attn_w_qkv, L0_attn_w_o,
           L0_ffn_Wg, L0_ffn_bg, L0_ffn_Wup, L0_ffn_bup, L0_ffn_Wsw, L0_ffn_bsw,
           L0_ffn_Wdn, L0_ffn_bdn, L0_ffn_Wsu, L0_ffn_bsu, L0_ffn_Wsa,
           L0_ffn_bsa, L0_ffn_Wsd, L0_ffn_bsd):
    in_maps = make_in_maps(L0_attn_hidden_states, L0_attn_w_qkv, L0_attn_w_o,
                           L0_ffn_Wg, L0_ffn_bg, L0_ffn_Wup, L0_ffn_bup,
                           L0_ffn_Wsw, L0_ffn_bsw, L0_ffn_Wdn, L0_ffn_bdn,
                           L0_ffn_Wsu, L0_ffn_bsu, L0_ffn_Wsa, L0_ffn_bsa,
                           L0_ffn_Wsd, L0_ffn_bsd)
    nc = build()
    res = run_bass_kernel_spmd(nc, in_maps, core_ids=list(range(8)))
    out = np.empty((2, SEQ, H), np.float32)
    for c in range(8):
        b, i = c // 4, c % 4
        oc = np.asarray(res.results[c]["out_own"]).reshape(NT_OWN, 128, H)
        for s, t in enumerate(_own_tiles(i)):
            out[b, t * 128:(t + 1) * 128, :] = oc[s]
    return out


# version 04
# speedup vs baseline: 1.0191x; 1.0146x over previous
import math
import numpy as np
import concourse.bass as bass
import concourse.mybir as mybir
from concourse.bass_utils import run_bass_kernel_spmd

F32 = mybir.dt.float32
BF16 = mybir.dt.bfloat16
AX = mybir.AxisListType.X
ALU = mybir.AluOpType
ACT = mybir.ActivationFunctionType

H = 2048
SEQ = 4096
NT_SEQ = 32
NT_OWN = 8
NE = 6
EX = 1024
C_SHIFT = 10.0
SCALE = float(np.float32(1.0) / np.sqrt(np.float32(128.0)))
EPS = 1e-5

CFG = dict(heads=16, dump=False, phases="ABCDEF")
ENGINES = ("sync", "tensor", "vector", "scalar", "gpsimd")


class PB:
    def __init__(self, nc):
        self.nc = nc
        self.q = {e: [] for e in ENGINES}
        self.sem = {}
        self.cnt = {}

    def new_sem(self, name):
        self.sem[name] = self.nc.alloc_semaphore(name)
        self.cnt[name] = 0

    def emit(self, eng, fn, waits=(), inc=None, inc_n=1):
        tgt = None
        if inc is not None:
            self.cnt[inc] += inc_n
            tgt = self.cnt[inc]
        self.q[eng].append((fn, tuple(waits), inc, inc_n))
        return tgt

    def wait(self, eng, waits):
        self.q[eng].append((None, tuple(waits), None, 0))

    def run(self):
        sems = self.sem

        def body(eng_name):
            def _(eng_obj):
                for fn, waits, inc, inc_n in self.q[eng_name]:
                    for s, v in waits:
                        if v and v > 0:
                            eng_obj.wait_ge(sems[s], v)
                    if fn is not None:
                        inst = fn(eng_obj)
                        if inc is not None:
                            inst.then_inc(sems[inc], inc_n)
            return _

        with self.nc.Block() as blk:
            blk.sync(body("sync"))
            blk.tensor(body("tensor"))
            blk.vector(body("vector"))
            blk.scalar(body("scalar"))
            blk.gpsimd(body("gpsimd"))


def build():
    heads = CFG["heads"]
    dump = CFG["dump"]
    phases = CFG["phases"]

    nc = bass.Bass("TRN2", target_bir_lowering=False,
                   detect_race_conditions=False,
                   dynamic_dma_scratch_size=16384)

    di = {}

    def inp(name, shape):
        di[name] = nc.dram_tensor(name, list(shape), F32, kind="ExternalInput").ap()

    inp("x_own", (1024, H))
    inp("x_seq", (SEQ, H))
    inp("wq_r", (16, 128, 2048)); inp("wk_r", (128, 32768)); inp("wv_r", (128, 32768))
    inp("wo_r", (4, 128, 8192))
    inp("wg_in", (128, 256)); inp("bg_in", (1, 16))
    inp("wup_r", (NE, 8, 128, 2048)); inp("wsw_r", (NE, 8, 128, 1024))
    inp("wdn_r", (NE, 4, 128, 4096))
    inp("wsu_r2", (16, 128, 2048)); inp("wsa_r2", (128, 8192))
    inp("wsd_r2", (4, 128, 8192))
    inp("bup_r", (128, 48)); inp("bsw_r", (128, 48)); inp("bsu_r", (128, 16))
    inp("bsa_r", (128, 8)); inp("bdnsd", (7, H))
    inp("cmask", (2, 128, 512))

    out_own = nc.dram_tensor("out_own", [1024, H], F32, kind="ExternalOutput").ap()
    skind = "ExternalOutput" if dump else "Internal"
    kT_scr = nc.dram_tensor("kT_scr", [H, SEQ], BF16, kind=skind).ap()
    vT_scr = nc.dram_tensor("vT_scr", [SEQ, H], BF16, kind=skind).ap()
    xlt_scr = nc.dram_tensor("xlt_scr", [NT_SEQ, 128, H], BF16, kind=skind).ap()
    res_scr = nc.dram_tensor("res_scr", [1024, H], F32, kind=skind).ap()
    x2_scr = nc.dram_tensor("x2_scr", [1024, H], F32, kind=skind).ap()
    if dump:
        q_dump = nc.dram_tensor("q_dump", [128, 16384], BF16, kind="ExternalOutput").ap()
        at_dump = nc.dram_tensor("at_dump", [128, 16384], BF16, kind="ExternalOutput").ap()
        accd = [nc.dram_tensor(f"acc_dump{k}", [128, 16384], F32,
                               kind="ExternalOutput").ap() for k in range(3)]
        w_dump = nc.dram_tensor("w_dump", [128, 64], F32, kind="ExternalOutput").ap()
        sh_dump = nc.dram_tensor("sh_dump", [128, 16384], BF16, kind="ExternalOutput").ap()
        z_dump = nc.dram_tensor("z_dump", [128, 16384], BF16, kind="ExternalOutput").ap()
        sml_dump = nc.dram_tensor("sml_dump", [128, 128], F32, kind="ExternalOutput").ap()
        x_dump = nc.dram_tensor("x_dump", [4, 128, 2048], F32, kind="ExternalOutput").ap()
        xltb_dump = nc.dram_tensor("xltb_dump", [128, 16384], BF16, kind="ExternalOutput").ap()
        id_dump = nc.dram_tensor("id_dump", [128, 256], F32, kind="ExternalOutput").ap()
        xtt_dump = nc.dram_tensor("xtt_dump", [128, 16384], BF16, kind="ExternalOutput").ap()
        id_dump2 = nc.dram_tensor("id_dump2", [128, 128], F32, kind="ExternalOutput").ap()
        x_dump_v = [x_dump[k] for k in range(4)]

    sb = nc.alloc_sbuf_tensor
    ident_bf = sb("ident_bf", [128, 128], BF16)
    ident_f32 = sb("ident_f32", [128, 128], F32)
    onescol_bf = sb("onescol_bf", [128, 1], BF16)
    onesrow_f32 = sb("onesrow_f32", [1, 128], F32)
    negC = sb("negC", [128, 1], F32)
    c1e30 = sb("c1e30", [128, 1], F32)
    c1e4 = sb("c1e4", [128, 1], F32)
    wg_bf = sb("wg_bf", [128, 256], BF16)
    bg_sb = sb("bg_sb", [1, 16], F32)
    bup_sb = sb("bup_sb", [128, 48], F32)
    bsw_sb = sb("bsw_sb", [128, 48], F32)
    bsu_sb = sb("bsu_sb", [128, 16], F32)
    bsa_sb = sb("bsa_sb", [128, 8], F32)
    bdnsd_sb = sb("bdnsd_sb", [8, H], BF16)
    wcomb = sb("wcomb", [8, 1024], BF16)
    w_all = sb("w_all", [128, 64], F32)
    srl = sb("srl", [1, 128], F32)
    srlb = sb("srlb", [128, 128], F32)
    sml = sb("sml", [128, 160], F32)
    smlsnap = sb("smlsnap", [128, 128], F32) if dump else None
    tmp_bf = sb("tmp_bf", [128, 512], BF16)

    NBF = 59392
    BB = sb("BB", [128, NBF], BF16)
    NF = 20480
    FB = sb("FB", [128, NF], F32)
    assert nc.sbuf_bytes_remaining >= 0

    def bbv(off, n):
        return BB[:, off:off + n]

    def fbv(off, n):
        return FB[:, off:off + n]

    OFF_WKT = 0
    OFF_XLTA = 32768
    OFF_XLTR = 40960
    OFF_KVO = 49152
    OFF_Q = 0
    OFF_XLTB = 16384
    OFF_ATT = 16384
    OFF_KVH = 32768
    OFF_PT = 50176
    OFF_CMASK = 54272
    OFF_WSLAB = 50176
    OFF_XTT = 0
    OFF_WOQ = 32768
    OFF_R2 = 16384
    OFF_R3 = 32768
    OFF_R4 = 49152

    OFF_XLN_AB = 45056
    OFF_XLN_DE = 53248
    OFF_ACC = 0
    OFF_XF = 16384
    OFF_XA4 = 8192

    P = [nc.alloc_psum_tensor(f"P{i}", [128, 512], F32) for i in range(7)]
    PB7 = nc.alloc_psum_tensor("PB7", [128, 1024], BF16)

    pb = PB(nc)
    SYNC_SEMS = ["LB_H", "LX0", "LX1", "LX2", "LX3", "LK0", "LK1",
                 "SKV0", "SKV1", "SX0", "SX1", "SX2", "SX3", "SO"]
    GPS_SEMS = ["LB_G", "LWa0", "LWa1", "LWb0", "LWb1", "LWc0", "LWc1",
                "LWd0", "LWd1"]
    for s in SYNC_SEMS + GPS_SEMS + ["T", "S", "V", "G"]:
        pb.new_sem(s)
    sem_nums = [h.num for h in pb.sem.values()]
    clear_range = range(min(sem_nums), max(sem_nums) + 1)
    if CFG.get("preamble", True):
        nc.gpsimd.sem_clear(clear_range)
        nc.all_engine_barrier()

    def dma_sync(sem, dst, src, waits=()):
        return pb.emit("sync", lambda e, d=dst, s=src: e.dma_start(out=d, in_=s),
                       waits, sem, 16)

    def dma_cast(sem, dst, src, waits=()):
        return pb.emit("gpsimd", lambda e, d=dst, s=src: e.dma_start(out=d, in_=s),
                       waits, sem, 16)

    def dump_barrier():
        v = pb.cnt["SO"]
        for eng in ENGINES:
            pb.wait(eng, [("SO", v)])

    TEN, VEC, SCA = "tensor", "vector", "scalar"
    GPS = "gpsimd"

    pb.emit(GPS, lambda e: e.memset(ident_bf[:], 0.0))
    pb.emit(GPS, lambda e: e.affine_select(
        out=ident_bf[:], in_=ident_bf[:], compare_op=ALU.not_equal, fill=1.0,
        base=0, pattern=[[-1, 128]], channel_multiplier=1))
    pb.emit(GPS, lambda e: e.memset(ident_f32[:], 0.0))
    pb.emit(GPS, lambda e: e.affine_select(
        out=ident_f32[:], in_=ident_f32[:], compare_op=ALU.not_equal, fill=1.0,
        base=0, pattern=[[-1, 128]], channel_multiplier=1))
    pb.emit(GPS, lambda e: e.memset(onescol_bf[:], 1.0))
    pb.emit(GPS, lambda e: e.memset(onesrow_f32[:], 1.0))
    pb.emit(GPS, lambda e: e.memset(negC[:], -C_SHIFT))
    pb.emit(GPS, lambda e: e.memset(sml[:, 144:160], 0.0))
    pb.emit(GPS, lambda e: e.memset(c1e30[:], 1.0e30))
    pb.emit(GPS, lambda e: e.memset(c1e4[:], 1.0e4))
    G_CONST = pb.emit(GPS, lambda e: e.memset(wcomb[:], 1.0), inc="G")

    cmask_bf = BB[:, OFF_CMASK:OFF_CMASK + 1024]
    dma_cast("LB_G", cmask_bf.rearrange("p (i j) -> p i j", i=2),
             di["cmask"].rearrange("i p j -> p i j"))
    LG_WGC = dma_cast("LB_G", wg_bf[:], di["wg_in"])
    LH_BIAS = 0
    for name, dst in (("bg_in", bg_sb[:]), ("bup_r", bup_sb[:]),
                      ("bsw_r", bsw_sb[:]), ("bsu_r", bsu_sb[:]),
                      ("bsa_r", bsa_sb[:])):
        LH_BIAS = dma_sync("LB_H", dst, di[name])
    LG_BDN = dma_cast("LB_G", bdnsd_sb[0:7, :], di["bdnsd"])

    wkT_sb = bbv(OFF_WKT, 32768)
    LG_WKT = 0
    if "A" in phases:
        for q4 in range(4):
            LG_WKT = dma_cast(
                "LB_G", bbv(OFF_WKT + q4 * 8192, 8192),
                di["wk_r"][:, q4 * 8192:(q4 + 1) * 8192])

    state = dict(pb7_free=0)

    LNDBG = dict(tile=None)
    fill_state = dict(k=0)

    def vfill(n=2):
        for _ in range(n):
            k = fill_state["k"] = (fill_state["k"] + 1) % 4
            pb.emit(VEC, lambda e, k=k: e.tensor_copy(
                sml[:, 151 + k:152 + k], sml[:, 150:151]))

    def emit_ln(xap, out_bf_ap, out_f32_ap, waits_v, par):
        o = par * 64
        for c in range(4):
            pb.emit(VEC, lambda e, c=c, o=o: e.bn_stats(
                sml[:, o + 6 * c:o + 6 * c + 6], xap[:, 512 * c:512 * (c + 1)]),
                waits_v if c == 0 else ())
        for c in range(4):
            pb.emit(VEC, lambda e, c=c, o=o: e.bn_aggr(
                sml[:, o + 24 + 2 * c:o + 26 + 2 * c], sml[:, o + 6 * c:o + 6 * c + 6]))
        mview = sml[:, o + 24:o + 32].rearrange("p (c t) -> p c t", t=2)[:, :, 0]
        vview = sml[:, o + 24:o + 32].rearrange("p (c t) -> p c t", t=2)[:, :, 1]
        vfill()
        pb.emit(VEC, lambda e: e.reduce_sum(sml[:, o + 32:o + 33], mview, AX))
        pb.emit(VEC, lambda e: e.tensor_mul(sml[:, o + 36:o + 40], mview, mview))
        pb.emit(VEC, lambda e: e.reduce_sum(sml[:, o + 41:o + 42], vview, AX))
        pb.emit(VEC, lambda e: e.tensor_scalar_mul(
            sml[:, o + 33:o + 34], sml[:, o + 32:o + 33], -0.25))
        vfill(1)
        pb.emit(VEC, lambda e: e.reduce_sum(
            sml[:, o + 40:o + 41], sml[:, o + 36:o + 40], AX))
        pb.emit(VEC, lambda e: e.tensor_mul(
            sml[:, o + 44:o + 45], sml[:, o + 33:o + 34], sml[:, o + 33:o + 34]))
        vfill(1)
        pb.emit(VEC, lambda e: e.tensor_add(
            sml[:, o + 42:o + 43], sml[:, o + 40:o + 41], sml[:, o + 41:o + 42]))
        vfill(2)
        pb.emit(VEC, lambda e: e.tensor_scalar_mul(
            sml[:, o + 43:o + 44], sml[:, o + 42:o + 43], 0.25))
        vfill(2)
        pb.emit(VEC, lambda e: e.tensor_sub(
            sml[:, o + 45:o + 46], sml[:, o + 43:o + 44], sml[:, o + 44:o + 45]))
        vfill(2)
        v_stats = pb.emit(VEC, lambda e: e.tensor_scalar_add(
            sml[:, o + 46:o + 47], sml[:, o + 45:o + 46], EPS), inc="V")
        pb.emit(SCA, lambda e: e.activation(
            sml[:, o + 47:o + 48], sml[:, o + 46:o + 47], ACT.Ln), [("V", v_stats)])
        s_r = pb.emit(SCA, lambda e: e.activation(
            sml[:, o + 48:o + 49], sml[:, o + 47:o + 48], ACT.Exp, scale=-0.5),
            inc="S")
        v_negmr = pb.emit(VEC, lambda e: e.tensor_mul(
            sml[:, o + 49:o + 50], sml[:, o + 33:o + 34],
            sml[:, o + 48:o + 49]), [("S", s_r)], inc="V")
        if dump and LNDBG["tile"] is not None:
            tdbg = LNDBG["tile"]
            vfill(2)
            pb.emit(VEC, lambda e, tdbg=tdbg: e.tensor_copy(
                smlsnap[:, tdbg * 4:tdbg * 4 + 2], sml[:, o + 48:o + 50]))
            pb.emit(VEC, lambda e, tdbg=tdbg: e.tensor_copy(
                smlsnap[:, tdbg * 4 + 2:tdbg * 4 + 4], sml[:, o + 45:o + 47]))
            if 2 <= tdbg <= 5:
                dma_sync("SO", x_dump_v[tdbg - 2], xap, [("V", pb.cnt["V"])])
            LNDBG["tile"] = None
        s_cast = pb.emit(SCA, lambda e: e.activation(
            out_bf_ap, xap, ACT.Identity, bias=sml[:, o + 49:o + 50],
            scale=sml[:, o + 48:o + 49]), [("V", v_negmr)], inc="S")
        if out_f32_ap is not None:
            s_cast = pb.emit(SCA, lambda e: e.activation(
                out_f32_ap, xap, ACT.Identity, bias=sml[:, o + 49:o + 50],
                scale=sml[:, o + 48:o + 49]), inc="S")
        return s_cast

    def emit_transpose16(src_fn, dst_off, waits_t):
        vlast = None
        for r in range(2):
            for k in range(8):
                fc = r * 8 + k
                w = []
                if k == 0:
                    w = list(waits_t) if r == 0 else []
                    w.append(("V", state["pb7_free"] if r == 0 else vlast))
                pb.emit(TEN, lambda e, fc=fc, k=k: e.transpose(
                    PB7[:, k * 128:(k + 1) * 128], src_fn(fc), ident_bf[:]),
                    w, inc="T" if k == 7 else None)
            t_done = pb.cnt["T"]
            vlast = pb.emit(VEC, lambda e, r=r: e.tensor_copy(
                bbv(dst_off + r * 1024, 1024), PB7[:]), [("T", t_done)], inc="V")
        state["pb7_free"] = vlast
        return vlast

    ST_XLT = {}
    A1_T_END = A2_T_END = 0
    if "A" in phases:
        xcast = {}
        ttrans = {}
        vcopy = {}
        kevac = {}
        kst = {}
        kmm_last = {}
        A_ST = CFG.get("a_st", 8)
        for st in range(A_ST):
            for tt in range(4):
                t = 4 * st + tt
                xslot = fbv(OFF_XA4 + (t % 4) * 2048, 2048)
                w = [("S", xcast[t - 4])] if t >= 4 else []
                if dump and t >= 4:
                    w.append(("SO", pb.cnt["SO"]))
                lh_x = dma_sync(f"LX{t % 4}", xslot, di["x_seq"].rearrange(
                    "(t p) h -> t p h", p=128)[t], w)
                wv = [(f"LX{t % 4}", lh_x)]
                if t >= 2:
                    wv.append(("T", ttrans[t - 2]))
                    wv.append(("S", xcast[t - 2]))
                xlslot = bbv(OFF_XLN_AB + (t % 2) * 2048, 2048)
                LNDBG["tile"] = t
                xcast[t] = emit_ln(xslot, xlslot, None, wv, t % 2)
                wt = [("S", xcast[t]), ("G", G_CONST)]
                if st >= 1:
                    wt.append((f"SX{tt}", ST_XLT[4 * (st - 1) + tt]))
                vcopy[t] = emit_transpose16(
                    lambda fc, xl=xlslot: xl[:, fc * 128:(fc + 1) * 128],
                    OFF_XLTA + tt * 2048, wt)
                ttrans[t] = pb.cnt["T"]
                ST_XLT[t] = dma_sync(f"SX{tt}", xlt_scr[t],
                                     bbv(OFF_XLTA + tt * 2048, 2048),
                                     [("V", vcopy[t])])
            xlt_view = bbv(OFF_XLTA, 8192).rearrange(
                "p (tile fc tl) -> p tile fc tl", tile=4, fc=16)
            for dc in range(16):
                bank = P[dc % 2]
                if dc < 2:
                    w = [("LB_G", LG_WKT), ("V", vcopy[4 * st + 3])]
                else:
                    w = [("V", kevac[(st, dc - 2)])]
                for fc in range(16):
                    pb.emit(TEN, lambda e, dc=dc, fc=fc, bank=bank, xv=xlt_view:
                            e.matmul(
                        bank[:], wkT_sb[:, fc * 2048 + dc * 128:fc * 2048 + (dc + 1) * 128],
                        xv[:, :, fc, :], start=(fc == 0), stop=(fc == 15)),
                        w if fc == 0 else (), inc="T" if fc == 15 else None)
                kmm = pb.cnt["T"]
                kmm_last[st] = kmm
                kv_slot = bbv(OFF_KVO + (dc % 2) * 512, 512)
                w2 = [("T", kmm)]
                prevkey = (st, dc - 2) if dc >= 2 else (st - 1, dc + 14)
                if prevkey in kst:
                    w2.append((f"SKV{dc % 2}", kst[prevkey]))
                kevac[(st, dc)] = pb.emit(
                    VEC, lambda e, bank=bank, kv=kv_slot: e.tensor_copy(kv, bank[:]),
                    w2, inc="V")
                kst[(st, dc)] = dma_sync(
                    f"SKV{dc % 2}",
                    kT_scr[dc * 128:(dc + 1) * 128, st * 512:(st + 1) * 512],
                    kv_slot, [("V", kevac[(st, dc)])])
        if dump:
            dma_sync("SO", sml_dump, smlsnap[:], [("V", pb.cnt["V"])])
            dump_barrier()
        A1_T_END = pb.cnt["T"]

        LG_WVT = 0
        pb.wait(GPS, [("T", A1_T_END)])
        for q4 in range(4):
            LG_WVT = dma_cast(
                "LB_G", bbv(OFF_WKT + q4 * 8192, 8192),
                di["wv_r"][:, q4 * 8192:(q4 + 1) * 8192])
        wvT_sb = bbv(OFF_WKT, 32768)
        vevac = {}
        vst = {}
        vmm_last = {}
        for t in range(4 * A_ST):
            slot = bbv(OFF_XLTR + (t % 2) * 2048, 2048)
            w = [(f"SX{t % 4}", ST_XLT[t])]
            if t >= 2:
                w.append(("T", vmm_last[t - 2]))
            xltld = dma_sync(f"LX{t % 2}", slot, xlt_scr[t], w)
            for dvg in range(4):
                bank = P[dvg % 2]
                if dvg < 2:
                    w = [(f"LX{t % 2}", xltld), ("LB_G", LG_WVT)]
                    prevkey = (t - 1, dvg + 2)
                else:
                    w = []
                    prevkey = (t, dvg - 2)
                if prevkey in vevac:
                    w.append(("V", vevac[prevkey]))
                for fc in range(16):
                    pb.emit(TEN, lambda e, fc=fc, dvg=dvg, bank=bank, sl=slot:
                            e.matmul(
                        bank[:], sl[:, fc * 128:(fc + 1) * 128],
                        wvT_sb[:, fc * 2048 + dvg * 512:fc * 2048 + (dvg + 1) * 512],
                        start=(fc == 0), stop=(fc == 15)),
                        w if fc == 0 else (), inc="T" if fc == 15 else None)
                vmm = pb.cnt["T"]
                vmm_last[t] = vmm
                kv_slot = bbv(OFF_KVO + (dvg % 2) * 512, 512)
                w2 = [("T", vmm)]
                prevkey = (t, dvg - 2) if dvg >= 2 else (t - 1, dvg + 2)
                if prevkey in vst:
                    w2.append((f"SKV{dvg % 2}", vst[prevkey]))
                vevac[(t, dvg)] = pb.emit(
                    VEC, lambda e, bank=bank, kv=kv_slot: e.tensor_copy(kv, bank[:]),
                    w2, inc="V")
                vst[(t, dvg)] = dma_sync(
                    f"SKV{dvg % 2}",
                    vT_scr[t * 128:(t + 1) * 128, dvg * 512:(dvg + 1) * 512],
                    kv_slot, [("V", vevac[(t, dvg)])])
        A2_T_END = pb.cnt["T"]

    B_V_END = 0
    if "B" in phases:
        if dump:
            _vid = pb.emit(VEC, lambda e: e.tensor_copy(
                srlb[:], ident_bf[:]), [("T", A2_T_END)], inc="V")
            dma_sync("SO", id_dump[:, 0:128], srlb[:], [("V", _vid)])
            dump_barrier()
        bcast = {}
        btrans = {}
        for ot in range(NT_OWN):
            xslot = fbv(OFF_XF + (ot % 2) * 2048, 2048)
            w = [("S", bcast[ot - 2])] if ot >= 2 else []
            lh_x = dma_sync(f"LX{ot % 2}", xslot, di["x_own"].rearrange(
                "(t p) h -> t p h", p=128)[ot], w)
            wv = [(f"LX{ot % 2}", lh_x)]
            if ot == 0:
                wv.append(("T", A2_T_END))
            if ot >= 2:
                wv.append(("T", btrans[ot - 2]))
                wv.append(("S", bcast[ot - 2]))
            xlslot = bbv(OFF_XLN_AB + (ot % 2) * 2048, 2048)
            if dump and "A" not in phases:
                LNDBG["tile"] = ot
            bcast[ot] = emit_ln(xslot, xlslot, None, wv, ot % 2)
            wt = [("S", bcast[ot]), ("G", G_CONST), ("T", A2_T_END)]
            emit_transpose16(lambda fc, xl=xlslot: xl[:, fc * 128:(fc + 1) * 128],
                             OFF_XLTB + ot * 2048, wt)
            btrans[ot] = pb.cnt["T"]
        v_xltb = pb.cnt["V"]
        XLTB_SO = None
        if dump:
            XLTB_SO = dma_sync("SO", xltb_dump, bbv(OFF_XLTB, 16384),
                               [("V", v_xltb)])
            dump_barrier()
        skip_q = bool(CFG.get("noq"))
        xltb_view = bbv(OFF_XLTB, 16384).rearrange(
            "p (tile fc tl) -> p tile fc tl", tile=8, fc=16)
        qevac = {}
        qmm_done = {}
        lg_wq = {}

        def load_wq(dc, w):
            lg_wq[dc] = dma_cast(
                f"LWd{dc % 2}",
                bbv(OFF_WSLAB + (dc % 2) * 2048, 2048), di["wq_r"][dc], w)

        if not skip_q:
            wq0w = [("T", A2_T_END)]
            if CFG.get("serial_wq"):
                wq0w.append(("V", v_xltb))
            load_wq(0, wq0w)
            load_wq(1, [])
        for dc in range(16 if not skip_q else 0):
            slab = bbv(OFF_WSLAB + (dc % 2) * 2048, 2048)
            for tq in range(2):
                bank = P[tq]
                w = []
                if tq == 0:
                    w = [(f"LWd{dc % 2}", lg_wq[dc]), ("V", v_xltb)]
                if (dc - 1, tq) in qevac:
                    w.append(("V", qevac[(dc - 1, tq)]))
                for fc in range(16):
                    pb.emit(TEN, lambda e, fc=fc, tq=tq, bank=bank, sl=slab: e.matmul(
                        bank[:], sl[:, fc * 128:(fc + 1) * 128],
                        xltb_view[:, tq * 4:(tq + 1) * 4, fc, :],
                        start=(fc == 0), stop=(fc == 15)),
                        w if fc == 0 else (), inc="T" if fc == 15 else None)
                qmm_done[dc] = pb.cnt["T"]
                qevac[(dc, tq)] = pb.emit(
                    VEC, lambda e, dc=dc, tq=tq, bank=bank: e.tensor_copy(
                        bbv(OFF_Q + dc * 1024 + tq * 512, 512), bank[:]),
                    [("T", qmm_done[dc])], inc="V")
            if dc + 2 < 16:
                load_wq(dc + 2, [("T", qmm_done[dc])])
        if dump and "A" not in phases:
            dma_sync("SO", sml_dump, smlsnap[:], [("V", pb.cnt["V"])])
        B_V_END = pb.cnt["V"]
        QDUMP_SO = None
        if dump:
            QDUMP_SO = dma_sync("SO", q_dump, bbv(OFF_Q, 16384), [("V", B_V_END)])
            dump_barrier()

    C_V_END = C_T_END = 0
    if "C" in phases:
        kvload = {}
        head_done = {}
        prev_evac = B_V_END
        prev_l_user = 0
        prev_rlb_user = 0
        bank_free_S = {0: 0, 1: 0}
        for h in range(heads):
            buf = h % 2
            koff = OFF_KVH + buf * 8192
            voff = OFF_KVH + buf * 8192 + 4096
            w = [("V", B_V_END)] if h < 2 else [("T", head_done[h - 2])]
            dma_sync(f"LK{h % 2}", bbv(koff, 4096),
                     kT_scr[h * 128:(h + 1) * 128, :], w)
            for vh in range(2):
                kvload[h] = dma_sync(
                    f"LK{h % 2}",
                    bbv(voff + vh * 2048, 2048).rearrange("p (kt d) -> p kt d", kt=16),
                    vT_scr.rearrange("(kt p) d -> p kt d", p=128)[
                        :, vh * 16:(vh + 1) * 16, h * 128:(h + 1) * 128])
            for s in range(NT_OWN):
                ngr = s + 1
                exp_done = {}
                av_done = {}
                for g in range(ngr):
                    bank = P[g % 2]
                    w = []
                    if g == 0:
                        w = [(f"LK{h % 2}", kvload[h])]
                        if h == 0 and s == 0:
                            w.append(("G", G_CONST))
                    if g >= 2:
                        w.append(("S", exp_done[g - 2]))
                    elif bank_free_S[g % 2]:
                        w.append(("S", bank_free_S[g % 2]))
                    for jj in range(4):
                        kt = 4 * g + jj
                        pb.emit(TEN, lambda e, kt=kt, jj=jj, bank=bank, ko=koff,
                                h=h, s=s: e.matmul(
                            bank[:, jj * 128:(jj + 1) * 128],
                            bbv(ko + kt * 128, 128),
                            bbv(OFF_Q + h * 1024 + s * 128, 128),
                            start=True, stop=True),
                            w if jj == 0 else (), inc="T" if jj == 3 else None)
                    smm = pb.cnt["T"]
                    wexp = [("T", smm)]
                    if g >= 2:
                        wexp.append(("T", av_done[g - 2]))
                    ptap = bbv(OFF_PT + (g % 2) * 512, 512)
                    exp_done[g] = pb.emit(SCA, lambda e, bank=bank, pt=ptap:
                                          e.activation(
                        pt, bank[:], ACT.Exp, bias=negC[:, 0:1], scale=SCALE),
                        wexp, inc="S")
                    bank_free_S[g % 2] = exp_done[g]
                    gate_sem = ("S", exp_done[g])
                    if g == s:
                        side = 0 if s < 4 else 1
                        vm = pb.emit(VEC, lambda e, pt=ptap, side=side: e.tensor_mul(
                            pt, pt, BB[:, OFF_CMASK + side * 512:
                                       OFF_CMASK + (side + 1) * 512]),
                            [("S", exp_done[g]), ("LB_G", LG_WGC)], inc="V")
                        gate_sem = ("V", vm)
                    w = [gate_sem]
                    if g == 0:
                        w.append(("V", prev_evac))
                        if prev_l_user:
                            w.append(("V", prev_l_user))
                    first = (g == 0)
                    last = (g == ngr - 1)
                    for jj in range(4):
                        kt = 4 * g + jj
                        pb.emit(TEN, lambda e, kt=kt, jj=jj, pt=ptap, vo=voff,
                                fi=first, la=last: e.matmul(
                            P[2][:, 0:128], bbv(vo + kt * 128, 128),
                            pt[:, jj * 128:(jj + 1) * 128],
                            start=(fi and jj == 0), stop=(la and jj == 3)),
                            w if jj == 0 else ())
                    for jj in range(4):
                        pb.emit(TEN, lambda e, jj=jj, pt=ptap, fi=first, la=last:
                                e.matmul(
                            P[3][0:1, 0:128], onescol_bf[:],
                            pt[:, jj * 128:(jj + 1) * 128],
                            start=(fi and jj == 0), stop=(la and jj == 3)),
                            (), inc="T" if jj == 3 else None)
                    av_done[g] = pb.cnt["T"]
                v_rl = pb.emit(VEC, lambda e: e.reciprocal(srl[:], P[3][0:1, 0:128]),
                               [("T", av_done[ngr - 1])], inc="V")
                prev_l_user = v_rl
                w = [("V", v_rl)]
                if prev_rlb_user:
                    w.append(("V", prev_rlb_user))
                t_rlb = pb.emit(TEN, lambda e: e.matmul(
                    P[4][:, 0:128], onesrow_f32[:], srl[:], start=True, stop=True),
                    w, inc="T")
                v_cp = pb.emit(VEC, lambda e: e.tensor_copy(srlb[:], P[4][:, 0:128]),
                               [("T", t_rlb)], inc="V")
                vfill(2)
                wev = [("V", v_cp)]
                if h == 0 and s == 0 and dump and "B" in phases and XLTB_SO:
                    wev.append(("SO", XLTB_SO))
                prev_evac = pb.emit(VEC, lambda e, h=h, s=s: e.tensor_mul(
                    bbv(OFF_ATT + h * 1024 + s * 128, 128), P[2][:, 0:128], srlb[:]),
                    wev, inc="V")
                prev_rlb_user = prev_evac
            head_done[h] = pb.cnt["T"]
        C_V_END = pb.cnt["V"]
        C_T_END = pb.cnt["T"]
        ATD_SO = None
        if dump:
            ATD_SO = dma_sync("SO", at_dump, bbv(OFF_ATT, 16384), [("V", C_V_END)])
            dump_barrier()

    D_T_END = D_V_END = D_SO_END = 0
    if "D" in phases:
        att_view = bbv(OFF_ATT, 16384)
        d_evac = {}
        oq_done = {}
        lg_woq = {}

        def load_woq(oq, w):
            lg_woq[oq] = dma_cast(
                f"LWd{oq % 2}",
                bbv(OFF_WOQ + (oq % 2) * 8192, 8192), di["wo_r"][oq], w)

        if dump:
            _vid3 = pb.emit(VEC, lambda e: e.tensor_copy(
                srlb[:], ident_bf[:]), [("T", pb.cnt["T"]), ("V", pb.cnt["V"])],
                inc="V")
            dma_sync("SO", id_dump2, srlb[:], [("V", _vid3)])
            dump_barrier()
        load_woq(0, [("T", C_T_END)])
        load_woq(1, [])
        for oq in range(4 if not CFG.get("skip_dmm") else 0):
            woq = bbv(OFF_WOQ + (oq % 2) * 8192, 8192)
            for quad in range(2):
                for dc in range(16):
                    for tl in range(4):
                        tile = quad * 4 + tl
                        w = []
                        if dc == 0 and tl == 0:
                            w = [(f"LWd{oq % 2}", lg_woq[oq]), ("V", C_V_END)]
                            pk = (oq, 0, 3) if quad == 1 else (oq - 1, 1, 3)
                            if pk in d_evac:
                                w.append(("V", d_evac[pk]))
                        pb.emit(TEN, lambda e, dc=dc, tile=tile, tl=tl, wq=woq:
                                e.matmul(
                            P[tl][:], att_view[:, dc * 1024 + tile * 128:
                                               dc * 1024 + (tile + 1) * 128],
                            wq[:, dc * 512:(dc + 1) * 512],
                            start=(dc == 0), stop=(dc == 15)),
                            w, inc="T" if dc == 15 else None)
                dmm = pb.cnt["T"]
                oq_done[oq] = dmm
                for tl in range(4):
                    tile = quad * 4 + tl
                    if CFG.get("evac_alt"):
                        d_evac[(oq, quad, tl)] = pb.emit(
                            VEC, lambda e, tl=tl: e.tensor_copy(
                                srlb[:], P[tl][:, 0:128]),
                            [("T", dmm)], inc="V")
                    else:
                        d_evac[(oq, quad, tl)] = pb.emit(
                            VEC, lambda e, tile=tile, oq=oq, tl=tl: e.tensor_copy(
                                fbv(OFF_ACC + tile * 2048 + oq * 512, 512), P[tl][:]),
                            [("T", dmm)], inc="V")
            if oq + 2 < 4:
                load_woq(oq + 2, [("T", oq_done[oq])])
        if dump:
            _vid2 = pb.emit(VEC, lambda e: e.tensor_copy(
                srlb[:], ident_bf[:]), [("T", pb.cnt["T"]), ("V", pb.cnt["V"])],
                inc="V")
            dma_sync("SO", id_dump[:, 128:256], srlb[:], [("V", _vid2)])
            dump_barrier()
        dcast = {}
        dtrans = {}
        st_res = {}
        for tile in range(NT_OWN if not CFG.get("skip_dmm") else 0):
            xslot = fbv(OFF_XF + (tile % 2) * 2048, 2048)
            w = [("V", d_evac[(3, tile // 4, tile % 4)])]
            if tile >= 2:
                w.append(("S", dcast[tile - 2]))
                w.append(("SO", pb.cnt["SO"]))
            lh_x = dma_sync(f"LX{tile % 2}", xslot, di["x_own"].rearrange(
                "(t p) h -> t p h", p=128)[tile], w)
            x2ap = fbv(OFF_ACC + tile * 2048, 2048)
            v_x2 = pb.emit(VEC, lambda e, a=x2ap, xs=xslot: e.tensor_add(a, a, xs),
                           [(f"LX{tile % 2}", lh_x)], inc="V")
            dma_sync("SO", x2_scr.rearrange("(t p) h -> t p h", p=128)[tile], x2ap,
                     [("V", v_x2)])
            xlslot = bbv(OFF_XLN_DE + (tile % 2) * 2048, 2048)
            wv = [("V", v_x2)]
            if tile >= 2:
                wv.append(("T", dtrans[tile - 2]))
                wv.append(("S", dcast[tile - 2]))
            dcast[tile] = emit_ln(x2ap, xlslot, xslot, wv, tile % 2)
            st_res[tile] = dma_sync(
                "SO", res_scr.rearrange("(t p) h -> t p h", p=128)[tile], xslot,
                [("S", dcast[tile])])
            wtr = [("S", dcast[tile])]
            if tile == 0 and dump and "B" in phases and QDUMP_SO:
                wtr.append(("SO", QDUMP_SO))
            emit_transpose16(lambda fc, xl=xlslot: xl[:, fc * 128:(fc + 1) * 128],
                             OFF_XTT + tile * 2048, wtr)
            dtrans[tile] = pb.cnt["T"]
        if dump and not CFG.get("skip_dmm"):
            dma_sync("SO", xtt_dump, bbv(OFF_XTT, 16384),
                     [("T", pb.cnt["T"]), ("V", pb.cnt["V"])])
            dump_barrier()
        D_T_END = pb.cnt["T"]
        D_V_END = pb.cnt["V"]
        D_SO_END = pb.cnt["SO"]

    if "E" in phases:
        xtt_view = bbv(OFF_XTT, 16384).rearrange(
            "p (tile fc tl) -> p tile fc tl", tile=8, fc=16)

        def xtt_rhs(fc, th):
            return xtt_view[:, th * 4:(th + 1) * 4, fc, :]

        prev_iter_T = D_T_END
        prev_iter_V = D_V_END
        prev_iter_SO = D_SO_END
        for j in range(2):
            w_gate_done = {}
            for tile in range(NT_OWN):
                if tile == 0:
                    w = [("T", prev_iter_T), ("V", prev_iter_V), ("LB_G", LG_WGC)]
                else:
                    w = [("V", w_gate_done[tile - 1])]
                for fc in range(16):
                    pb.emit(TEN, lambda e, fc=fc, tile=tile, j=j: e.matmul(
                        P[4][:, 0:8], xtt_view[:, tile, fc, :],
                        wg_bf[:, j * 128 + fc * 8:j * 128 + (fc + 1) * 8],
                        start=(fc == 0), stop=False),
                        w if fc == 0 else ())
                t_g = pb.emit(TEN, lambda e, j=j: e.matmul(
                    P[4][:, 0:8], onesrow_f32[:], bg_sb[0:1, j * 8:(j + 1) * 8],
                    start=False, stop=True), [("LB_H", LH_BIAS)], inc="T")
                o = 100
                pb.emit(VEC, lambda e: e.tensor_copy(sml[:, o:o + 8], P[4][:, 0:8]),
                        [("T", t_g)])
                vfill(2)
                pb.emit(VEC, lambda e: e.tensor_reduce(
                    sml[:, o + 8:o + 9], sml[:, o:o + 8], AX, ALU.min))
                vfill(2)
                pb.emit(VEC, lambda e: e.tensor_scalar(
                    sml[:, o + 10:o + 18], sml[:, o:o + 8], sml[:, o + 8:o + 9],
                    None, ALU.is_equal))
                vfill(2)
                pb.emit(VEC, lambda e: e.scalar_tensor_tensor(
                    sml[:, o + 18:o + 26], sml[:, o + 10:o + 18], c1e30[:, 0:1],
                    sml[:, o:o + 8], ALU.mult, ALU.add))
                vfill(2)
                pb.emit(VEC, lambda e: e.tensor_reduce(
                    sml[:, o + 9:o + 10], sml[:, o + 18:o + 26], AX, ALU.min))
                vfill(2)
                pb.emit(VEC, lambda e: e.tensor_scalar(
                    sml[:, o + 10:o + 18], sml[:, o:o + 8], sml[:, o + 9:o + 10],
                    None, ALU.is_gt))
                vfill(2)
                pb.emit(VEC, lambda e: e.scalar_tensor_tensor(
                    sml[:, o + 18:o + 26], sml[:, o:o + 8], c1e4[:, 0:1],
                    sml[:, o + 10:o + 18], ALU.add, ALU.mult))
                vfill(2)
                v_m = pb.emit(VEC, lambda e: e.reduce_max(
                    sml[:, o + 8:o + 9], sml[:, o + 18:o + 26], AX, negate=True),
                    inc="V")
                s_e = pb.emit(SCA, lambda e: e.activation(
                    sml[:, o + 26:o + 34], sml[:, o + 18:o + 26], ACT.Exp,
                    bias=sml[:, o + 8:o + 9]), [("V", v_m)], inc="S")
                pb.emit(VEC, lambda e: e.reduce_sum(
                    sml[:, o + 9:o + 10], sml[:, o + 26:o + 34], AX), [("S", s_e)])
                vfill(2)
                pb.emit(VEC, lambda e: e.reciprocal(
                    sml[:, o + 8:o + 9], sml[:, o + 9:o + 10]))
                vfill(2)
                v_w = pb.emit(VEC, lambda e, tile=tile: e.tensor_scalar_mul(
                    w_all[:, tile * 8:(tile + 1) * 8], sml[:, o + 26:o + 34],
                    sml[:, o + 8:o + 9]), inc="V")
                t_wt = pb.emit(TEN, lambda e, tile=tile: e.transpose(
                    P[5][0:8, 0:128], w_all[:, tile * 8:(tile + 1) * 8],
                    ident_f32[:]), [("V", v_w)], inc="T")
                w_gate_done[tile] = pb.emit(VEC, lambda e, tile=tile: e.tensor_copy(
                    wcomb[0:6, tile * 128:(tile + 1) * 128], P[5][0:6, 0:128]),
                    [("T", t_wt)], inc="V")
            acc_init_done = {}
            prev_ai = None
            for tile in range(NT_OWN):
                for hc in range(4):
                    w = [("V", w_gate_done[tile])]
                    if tile == 0 and hc == 0:
                        w.append(("SO", prev_iter_SO))
                        w.append(("LB_G", LG_BDN))
                    if prev_ai:
                        w.append(("V", prev_ai))
                    t_ai = pb.emit(TEN, lambda e, tile=tile, hc=hc: e.matmul(
                        P[6][:], wcomb[0:7, tile * 128:(tile + 1) * 128],
                        bdnsd_sb[0:7, hc * 512:(hc + 1) * 512],
                        start=True, stop=True), w, inc="T")
                    prev_ai = pb.emit(
                        VEC, lambda e, tile=tile, hc=hc: e.tensor_copy(
                            fbv(OFF_ACC + tile * 2048 + hc * 512, 512), P[6][:]),
                        [("T", t_ai)], inc="V")
            V_ACCINIT = prev_ai
            DSO = None
            if dump and j == 0:
                dma_sync("SO", w_dump, w_all[:], [("V", V_ACCINIT)])
                DSO = dma_sync("SO", accd[0], fbv(OFF_ACC, 16384),
                               [("V", V_ACCINIT)])
                dump_barrier()

            sh_done = {}
            lg_su = {}

            def load_su(oc_h, w):
                oc, hh = oc_h
                lg_su[oc_h] = dma_cast(
                    f"LWd{hh}",
                    bbv(OFF_R4 + hh * 1024, 1024),
                    di["wsu_r2"][oc][:, hh * 1024:(hh + 1) * 1024], w)

            wsu0 = [("T", prev_iter_T)]
            if dump and "C" in phases and ATD_SO:
                wsu0.append(("SO", ATD_SO))
            load_su((0, 0), wsu0)
            load_su((0, 1), [])
            lg_sa = dma_cast(
                "LB_G", bbv(OFF_R4 + 2048, 8192), di["wsa_r2"],
                [("T", prev_iter_T)])
            for oc in range(16):
                bank = P[oc % 2]
                s_prev_th = None
                for th in range(2):
                    w = []
                    if th == 0:
                        w = [("LWd0", lg_su[(oc, 0)]), ("LWd1", lg_su[(oc, 1)]),
                             ("V", V_ACCINIT)]
                        if oc >= 1:
                            w.append(("S", sh_done[oc - 1]))
                    else:
                        w = [("S", s_prev_th)]
                    for fc in range(16):
                        sl = bbv(OFF_R4 + (fc // 8) * 1024, 1024)
                        pb.emit(TEN, lambda e, fc=fc, th=th, bank=bank, sl=sl:
                                e.matmul(
                            bank[:], sl[:, (fc % 8) * 128:(fc % 8 + 1) * 128],
                            xtt_rhs(fc, th), start=(fc == 0), stop=(fc == 15)),
                            w if fc == 0 else (), inc="T" if fc == 15 else None)
                    t_sh = pb.cnt["T"]
                    s_prev_th = pb.emit(SCA, lambda e, oc=oc, th=th, bank=bank:
                                        e.activation(
                        bbv(OFF_R2 + oc * 1024 + th * 512, 512), bank[:],
                        ACT.Identity, bias=bsu_sb[:, oc:oc + 1]),
                        [("T", t_sh)], inc="S")
                sh_done[oc] = s_prev_th
                if oc + 1 < 16:
                    load_su((oc + 1, 0), [("S", s_prev_th)])
                    load_su((oc + 1, 1), [])
            S_SH_END = pb.cnt["S"]
            if dump and j == 0:
                dma_sync("SO", sh_dump, bbv(OFF_R2, 16384), [("S", S_SH_END)])
                dump_barrier()
            wsa_sb = bbv(OFF_R4 + 2048, 8192)
            z_done = {}
            prev_zmul = 0
            for g2 in range(2):
                for e2c in range(8):
                    zc = g2 * 8 + e2c
                    bank = P[zc % 2]
                    v_prev_th = None
                    for th in range(2):
                        w = []
                        if th == 0:
                            if zc == 0:
                                w = [("LB_G", lg_sa), ("S", S_SH_END)]
                            elif zc >= 2:
                                w = [("V", z_done[zc - 2])]
                        else:
                            w = [("V", v_prev_th)]
                        for ec in range(8):
                            pb.emit(TEN, lambda e, ec=ec, e2c=e2c, g2=g2, th=th,
                                    bank=bank: e.matmul(
                                bank[:],
                                wsa_sb[:, ec * 1024 + e2c * 128:ec * 1024 + (e2c + 1) * 128],
                                bbv(OFF_R2 + (g2 * 8 + ec) * 1024 + th * 512, 512),
                                start=(ec == 0), stop=(ec == 7)),
                                w if ec == 0 else (), inc="T" if ec == 7 else None)
                        t_a = pb.cnt["T"]
                        ws = [("T", t_a)]
                        if prev_zmul:
                            ws.append(("V", prev_zmul))
                        s_si = pb.emit(SCA, lambda e, e2c=e2c, bank=bank:
                                       e.activation(
                            tmp_bf[:], bank[:], ACT.Silu,
                            bias=bsa_sb[:, e2c:e2c + 1]), ws, inc="S")
                        v_prev_th = pb.emit(VEC, lambda e, zc=zc, th=th: e.tensor_mul(
                            bbv(OFF_R3 + zc * 1024 + th * 512, 512), tmp_bf[:],
                            bbv(OFF_R2 + zc * 1024 + th * 512, 512)),
                            [("S", s_si)], inc="V")
                        prev_zmul = v_prev_th
                    z_done[zc] = v_prev_th
            V_Z_END = pb.cnt["V"]
            if dump and j == 0:
                ZSO = dma_sync("SO", z_dump, bbv(OFF_R3, 16384), [("V", V_Z_END)])
                dump_barrier()
            sd_evac = {}
            lg_sd = {}

            def load_sd(hq, w):
                lg_sd[hq] = dma_cast(
                    f"LWd{hq % 2}",
                    bbv(OFF_R2 + (hq % 2) * 8192, 8192), di["wsd_r2"][hq], w)

            w0 = [("V", V_Z_END), ("S", S_SH_END)]
            if DSO:
                w0.append(("SO", DSO))
            load_sd(0, w0)
            load_sd(1, [])
            for hq in range(4):
                wsdq = bbv(OFF_R2 + (hq % 2) * 8192, 8192)
                for quad in range(2):
                    for zc in range(16):
                        for tl in range(4):
                            tile = quad * 4 + tl
                            w = []
                            if zc == 0 and tl == 0:
                                w = [(f"LWd{hq % 2}", lg_sd[hq]), ("V", V_Z_END)]
                                pk = (hq, 0, 3) if quad == 1 else (hq - 1, 1, 3)
                                if pk in sd_evac:
                                    w.append(("V", sd_evac[pk]))
                            pb.emit(TEN, lambda e, zc=zc, tile=tile, tl=tl,
                                    wq=wsdq: e.matmul(
                                P[tl][:],
                                bbv(OFF_R3 + zc * 1024 + tile * 128, 128),
                                wq[:, zc * 512:(zc + 1) * 512],
                                start=(zc == 0), stop=(zc == 15)),
                                w, inc="T" if zc == 15 else None)
                    t_sd = pb.cnt["T"]
                    for tl in range(4):
                        tile = quad * 4 + tl
                        sd_evac[(hq, quad, tl)] = pb.emit(
                            VEC, lambda e, tile=tile, hq=hq, tl=tl: e.tensor_add(
                                fbv(OFF_ACC + tile * 2048 + hq * 512, 512),
                                fbv(OFF_ACC + tile * 2048 + hq * 512, 512),
                                P[tl][:]), [("T", t_sd)], inc="V")
                if hq + 2 < 4:
                    load_sd(hq + 2, [("T", pb.cnt["T"])])
            V_SD_END = pb.cnt["V"]
            DSO2 = None
            if dump and j == 0:
                DSO2 = dma_sync("SO", accd[1], fbv(OFF_ACC, 16384),
                                [("V", V_SD_END)])
                dump_barrier()

            ex_evac_last = V_SD_END
            prev_h_readers_T = pb.cnt["T"]
            for n in range(NE):
                h_done = {}
                lg_up = {}

                def load_up(ec, w, n=n):
                    lg_up[ec] = dma_cast(
                        f"LWa{ec % 2}",
                        bbv(OFF_R4 + (ec % 2) * 2048, 2048),
                        di["wup_r"][n][ec], w)

                w0 = [("T", prev_h_readers_T), ("V", ex_evac_last)]
                if n == 0 and DSO2:
                    w0.append(("SO", DSO2))
                load_up(0, w0)
                load_up(1, [])
                for ec in range(8):
                    bank = P[ec % 2]
                    slab = bbv(OFF_R4 + (ec % 2) * 2048, 2048)
                    s_prev_th = None
                    for th in range(2):
                        w = []
                        if th == 0:
                            w = [(f"LWa{ec % 2}", lg_up[ec])]
                            if ec == 0:
                                w.append(("V", ex_evac_last))
                            if ec >= 2:
                                w.append(("S", h_done[ec - 2]))
                        else:
                            w = [("S", s_prev_th)]
                        for fc in range(16):
                            pb.emit(TEN, lambda e, fc=fc, th=th, bank=bank,
                                    sl=slab: e.matmul(
                                bank[:], sl[:, fc * 128:(fc + 1) * 128],
                                xtt_rhs(fc, th), start=(fc == 0), stop=(fc == 15)),
                                w if fc == 0 else (), inc="T" if fc == 15 else None)
                        t_h = pb.cnt["T"]
                        s_prev_th = pb.emit(SCA, lambda e, n=n, ec=ec, th=th,
                                            bank=bank: e.activation(
                            bbv(OFF_R2 + ec * 1024 + th * 512, 512), bank[:],
                            ACT.Identity, bias=bup_sb[:, n * 8 + ec:n * 8 + ec + 1]),
                            [("T", t_h)], inc="S")
                    h_done[ec] = s_prev_th
                    if ec + 2 < 8:
                        load_up(ec + 2, [("T", pb.cnt["T"])])
                S_H_END = pb.cnt["S"]
                prev_h_readers_T = pb.cnt["T"]
                u_done = {}
                prev_umul = 0
                lg_sw = {}

                def load_sw(e2c, w, n=n):
                    lg_sw[e2c] = dma_cast(
                        f"LWb{e2c % 2}",
                        bbv(OFF_R4 + 4096 + (e2c % 2) * 1024, 1024),
                        di["wsw_r"][n][e2c], w)

                load_sw(0, [("V", ex_evac_last)])
                load_sw(1, [])
                for e2c in range(8):
                    bank = P[2 + (e2c % 2)]
                    slab = bbv(OFF_R4 + 4096 + (e2c % 2) * 1024, 1024)
                    v_prev_th = None
                    for th in range(2):
                        w = []
                        if th == 0:
                            w = [(f"LWb{e2c % 2}", lg_sw[e2c]), ("S", S_H_END)]
                            if e2c >= 2:
                                w.append(("V", u_done[e2c - 2]))
                        else:
                            w = [("V", v_prev_th)]
                        for ec in range(8):
                            pb.emit(TEN, lambda e, ec=ec, th=th, bank=bank,
                                    sl=slab: e.matmul(
                                bank[:], sl[:, ec * 128:(ec + 1) * 128],
                                bbv(OFF_R2 + ec * 1024 + th * 512, 512),
                                start=(ec == 0), stop=(ec == 7)),
                                w if ec == 0 else (), inc="T" if ec == 7 else None)
                        t_u = pb.cnt["T"]
                        ws = [("T", t_u)]
                        if prev_umul:
                            ws.append(("V", prev_umul))
                        s_si = pb.emit(SCA, lambda e, n=n, e2c=e2c, bank=bank:
                                       e.activation(
                            tmp_bf[:], bank[:], ACT.Silu,
                            bias=bsw_sb[:, n * 8 + e2c:n * 8 + e2c + 1]),
                            ws, inc="S")
                        v_prev_th = pb.emit(VEC, lambda e, e2c=e2c, th=th:
                                            e.tensor_mul(
                            bbv(OFF_R3 + e2c * 1024 + th * 512, 512), tmp_bf[:],
                            bbv(OFF_R2 + e2c * 1024 + th * 512, 512)),
                            [("S", s_si)], inc="V")
                        prev_umul = v_prev_th
                    u_done[e2c] = v_prev_th
                    if e2c + 2 < 8:
                        load_sw(e2c + 2, [("T", pb.cnt["T"])])
                V_U_END = pb.cnt["V"]
                dn_evac = {}
                lg_dn = {}

                def load_dn(hq, w, n=n):
                    lg_dn[hq] = dma_cast(
                        f"LWc{hq % 2}",
                        bbv(OFF_R2 + (hq % 2) * 4096, 4096),
                        di["wdn_r"][n][hq], w)

                load_dn(0, [("V", V_U_END)])
                load_dn(1, [("V", V_U_END)])
                for hq in range(4):
                    wdnq = bbv(OFF_R2 + (hq % 2) * 4096, 4096)
                    for quad in range(2):
                        for ec in range(8):
                            for tl in range(4):
                                tile = quad * 4 + tl
                                w = []
                                if ec == 0 and tl == 0:
                                    w = [(f"LWc{hq % 2}", lg_dn[hq]), ("V", V_U_END)]
                                    pk = (hq, 0, 3) if quad == 1 else (hq - 1, 1, 3)
                                    if pk in dn_evac:
                                        w.append(("V", dn_evac[pk]))
                                pb.emit(TEN, lambda e, ec=ec, tile=tile, tl=tl,
                                        wq=wdnq: e.matmul(
                                    P[tl][:],
                                    bbv(OFF_R3 + ec * 1024 + tile * 128, 128),
                                    wq[:, ec * 512:(ec + 1) * 512],
                                    start=(ec == 0), stop=(ec == 7)),
                                    w, inc="T" if ec == 7 else None)
                        t_dn = pb.cnt["T"]
                        for tl in range(4):
                            tile = quad * 4 + tl
                            dn_evac[(hq, quad, tl)] = pb.emit(
                                VEC, lambda e, tile=tile, hq=hq, n=n, tl=tl:
                                e.scalar_tensor_tensor(
                                    fbv(OFF_ACC + tile * 2048 + hq * 512, 512),
                                    P[tl][:],
                                    w_all[:, tile * 8 + n:tile * 8 + n + 1],
                                    fbv(OFF_ACC + tile * 2048 + hq * 512, 512),
                                    ALU.mult, ALU.add), [("T", t_dn)], inc="V")
                    if hq + 2 < 4:
                        load_dn(hq + 2, [("T", pb.cnt["T"])])
                ex_evac_last = pb.cnt["V"]
                prev_h_readers_T = pb.cnt["T"]

            DSO3 = None
            if dump and j == 0:
                DSO3 = dma_sync("SO", accd[2], fbv(OFF_ACC, 16384),
                                [("V", ex_evac_last)])
                dump_barrier()
            comb_v = {}
            comb_trans = {}
            comb_st = {}
            for tile in range(NT_OWN):
                xslot = fbv(OFF_XF + (tile % 2) * 2048, 2048)
                w = [("V", ex_evac_last), ("SO", D_SO_END)]
                if DSO3:
                    w.append(("SO", DSO3))
                if tile >= 2:
                    w.append(("V", comb_v[tile - 2]))
                    w.append(("SO", pb.cnt["SO"]))
                lh_res = dma_sync(f"LX{tile % 2}", xslot, res_scr.rearrange(
                    "(t p) h -> t p h", p=128)[tile], w)
                accap = fbv(OFF_ACC + tile * 2048, 2048)
                v_add = pb.emit(VEC, lambda e, a=accap, xs=xslot: e.tensor_add(
                    a, a, xs), [(f"LX{tile % 2}", lh_res)], inc="V")
                if j == 0:
                    xlslot = bbv(OFF_XLN_DE + (tile % 2) * 2048, 2048)
                    w = [("V", v_add)]
                    if tile >= 2:
                        w.append(("T", comb_trans[tile - 2]))
                    s_c = pb.emit(SCA, lambda e, a=accap, xl=xlslot:
                                  e.activation(xl, a, ACT.Copy), w, inc="S")
                    comb_v[tile] = emit_transpose16(
                        lambda fc, xl=xlslot: xl[:, fc * 128:(fc + 1) * 128],
                        OFF_XTT + tile * 2048, [("S", s_c)])
                    comb_trans[tile] = pb.cnt["T"]
                else:
                    lh_x2 = dma_sync(f"LX{tile % 2}", xslot, x2_scr.rearrange(
                        "(t p) h -> t p h", p=128)[tile], [("V", v_add)])
                    v_f = pb.emit(VEC, lambda e, a=accap, xs=xslot: e.tensor_add(
                        a, a, xs), [(f"LX{tile % 2}", lh_x2)], inc="V")
                    comb_v[tile] = v_f
                    comb_st[tile] = dma_sync(
                        "SO", out_own.rearrange("(t p) h -> t p h", p=128)[tile],
                        accap, [("V", v_f)])
            prev_iter_T = pb.cnt["T"]
            prev_iter_V = pb.cnt["V"]
            prev_iter_SO = pb.cnt["SO"]

    pb.wait("sync", [(s, pb.cnt[s]) for s in SYNC_SEMS if pb.cnt[s]])
    pb.wait("gpsimd", [(s, pb.cnt[s]) for s in GPS_SEMS if pb.cnt[s]])
    pb.run()
    return nc


def _own_tiles(i):
    return [4 * s + i if s < 4 else 4 * s + 3 - i for s in range(8)]


def _make_cmask(i):
    m = np.zeros((2, 128, 512), np.float32)
    tri = (np.arange(128)[:, None] <= np.arange(128)[None, :]).astype(np.float32)
    for side, r in enumerate((i, 3 - i)):
        for jj in range(4):
            blk = m[side, :, jj * 128:(jj + 1) * 128]
            if jj < r:
                blk[:] = 1.0
            elif jj == r:
                blk[:] = tri
    return m


def make_in_maps(x, w_qkv, w_o, Wg, bg, Wup, bup, Wsw, bsw, Wdn, bdn,
                 Wsu, bsu, Wsa, bsa, Wsd, bsd):
    f = lambda a: np.ascontiguousarray(np.asarray(a, dtype=np.float32))
    x = f(x); w_qkv = f(w_qkv)
    wqT = np.ascontiguousarray(w_qkv[0:H].T)
    wkT = np.ascontiguousarray(w_qkv[H:2 * H].T)
    wvT = np.ascontiguousarray(w_qkv[2 * H:3 * H].T)
    woT = np.ascontiguousarray(f(w_o).T)
    c = np.ascontiguousarray
    shared = dict(
        wq_r=c(wqT.reshape(16, 128, 16, 128).transpose(2, 1, 0, 3).reshape(16, 128, 2048)),
        wk_r=c(wkT.reshape(16, 128, 2048).transpose(1, 0, 2).reshape(128, 32768)),
        wv_r=c(wvT.reshape(16, 128, 2048).transpose(1, 0, 2).reshape(128, 32768)),
        wo_r=c(woT.reshape(16, 128, 4, 512).transpose(2, 1, 0, 3).reshape(4, 128, 8192)),
        wg_in=c(f(Wg).reshape(2, 16, 128, 8).transpose(2, 0, 1, 3).reshape(128, 256)),
        bg_in=f(bg).reshape(1, 16),
        wup_r=c(f(Wup).reshape(6, 16, 128, 8, 128).transpose(0, 3, 2, 1, 4).reshape(6, 8, 128, 2048)),
        wsw_r=c(f(Wsw).reshape(6, 8, 128, 8, 128).transpose(0, 3, 2, 1, 4).reshape(6, 8, 128, 1024)),
        wdn_r=c(f(Wdn).reshape(6, 8, 128, 4, 512).transpose(0, 3, 2, 1, 4).reshape(6, 4, 128, 4096)),
        wsu_r2=c(f(Wsu).reshape(16, 128, 16, 128).transpose(2, 1, 0, 3).reshape(16, 128, 2048)),
        wsa_r2=c(f(Wsa).reshape(8, 128, 1024).transpose(1, 0, 2).reshape(128, 8192)),
        wsd_r2=c(f(Wsd).reshape(16, 128, 4, 512).transpose(2, 1, 0, 3).reshape(4, 128, 8192)),
        bup_r=np.ascontiguousarray(
            f(bup).reshape(6, 8, 128).transpose(2, 0, 1).reshape(128, 48)),
        bsw_r=np.ascontiguousarray(
            f(bsw).reshape(6, 8, 128).transpose(2, 0, 1).reshape(128, 48)),
        bsu_r=np.ascontiguousarray(f(bsu).reshape(16, 128).T),
        bsa_r=np.ascontiguousarray(f(bsa).reshape(8, 128).T),
        bdnsd=np.ascontiguousarray(
            np.concatenate([f(bdn), f(bsd)[None]], 0)),
    )
    in_maps = []
    for c in range(8):
        b, i = c // 4, c % 4
        m = dict(shared)
        m["x_own"] = np.ascontiguousarray(
            x[b].reshape(NT_SEQ, 128, H)[_own_tiles(i)].reshape(1024, H))
        m["x_seq"] = x[b]
        m["cmask"] = _make_cmask(i)
        in_maps.append(m)
    return in_maps


def kernel(L0_attn_hidden_states, L0_attn_w_qkv, L0_attn_w_o,
           L0_ffn_Wg, L0_ffn_bg, L0_ffn_Wup, L0_ffn_bup, L0_ffn_Wsw, L0_ffn_bsw,
           L0_ffn_Wdn, L0_ffn_bdn, L0_ffn_Wsu, L0_ffn_bsu, L0_ffn_Wsa,
           L0_ffn_bsa, L0_ffn_Wsd, L0_ffn_bsd):
    in_maps = make_in_maps(L0_attn_hidden_states, L0_attn_w_qkv, L0_attn_w_o,
                           L0_ffn_Wg, L0_ffn_bg, L0_ffn_Wup, L0_ffn_bup,
                           L0_ffn_Wsw, L0_ffn_bsw, L0_ffn_Wdn, L0_ffn_bdn,
                           L0_ffn_Wsu, L0_ffn_bsu, L0_ffn_Wsa, L0_ffn_bsa,
                           L0_ffn_Wsd, L0_ffn_bsd)
    nc = build()
    res = run_bass_kernel_spmd(nc, in_maps, core_ids=list(range(8)))
    out = np.empty((2, SEQ, H), np.float32)
    for c in range(8):
        b, i = c // 4, c % 4
        oc = np.asarray(res.results[c]["out_own"]).reshape(NT_OWN, 128, H)
        for s, t in enumerate(_own_tiles(i)):
            out[b, t * 128:(t + 1) * 128, :] = oc[s]
    return out


# version 05 (final)
# speedup vs baseline: 1.0741x; 1.0539x over previous
import math
import numpy as np
import concourse.bass as bass
import concourse.mybir as mybir
from concourse.bass_utils import run_bass_kernel_spmd

F32 = mybir.dt.float32
BF16 = mybir.dt.bfloat16
AX = mybir.AxisListType.X
ALU = mybir.AluOpType
ACT = mybir.ActivationFunctionType

H = 2048
SEQ = 4096
NT_SEQ = 32
NT_OWN = 8
NE = 6
EX = 1024
C_SHIFT = 10.0
SCALE = float(np.float32(1.0) / np.sqrt(np.float32(128.0)))
EPS = 1e-5

CFG = dict(heads=16, dump=False, phases="ABCDEF")
ENGINES = ("sync", "tensor", "vector", "scalar", "gpsimd")


class PB:
    def __init__(self, nc):
        self.nc = nc
        self.q = {e: [] for e in ENGINES}
        self.sem = {}
        self.cnt = {}

    def new_sem(self, name):
        self.sem[name] = self.nc.alloc_semaphore(name)
        self.cnt[name] = 0

    def emit(self, eng, fn, waits=(), inc=None, inc_n=1):
        tgt = None
        if inc is not None:
            self.cnt[inc] += inc_n
            tgt = self.cnt[inc]
        self.q[eng].append((fn, tuple(waits), inc, inc_n))
        return tgt

    def wait(self, eng, waits):
        self.q[eng].append((None, tuple(waits), None, 0))

    def run(self):
        sems = self.sem

        def body(eng_name):
            def _(eng_obj):
                for fn, waits, inc, inc_n in self.q[eng_name]:
                    for s, v in waits:
                        if v and v > 0:
                            eng_obj.wait_ge(sems[s], v)
                    if fn is not None:
                        inst = fn(eng_obj)
                        if inc is not None:
                            inst.then_inc(sems[inc], inc_n)
            return _

        with self.nc.Block() as blk:
            blk.sync(body("sync"))
            blk.tensor(body("tensor"))
            blk.vector(body("vector"))
            blk.scalar(body("scalar"))
            blk.gpsimd(body("gpsimd"))


def build():
    heads = CFG["heads"]
    dump = CFG["dump"]
    phases = CFG["phases"]

    nc = bass.Bass("TRN2", target_bir_lowering=False,
                   detect_race_conditions=False,
                   dynamic_dma_scratch_size=16384)

    di = {}

    def inp(name, shape):
        di[name] = nc.dram_tensor(name, list(shape), F32, kind="ExternalInput").ap()

    inp("x_own", (1024, H))
    inp("x_seq", (SEQ, H))
    inp("wq_r", (16, 128, 2048)); inp("wk_r", (128, 32768)); inp("wv_r", (128, 32768))
    inp("wo_r", (4, 128, 8192))
    inp("wg_in", (128, 256)); inp("bg_in", (1, 16))
    inp("wup_r", (NE, 8, 128, 2048)); inp("wsw_r", (NE, 8, 128, 1024))
    inp("wdn_r", (NE, 4, 128, 4096))
    inp("wsu_r2", (16, 128, 2048)); inp("wsa_r2", (128, 8192))
    inp("wsd_r2", (4, 128, 8192))
    inp("bup_r", (128, 48)); inp("bsw_r", (128, 48)); inp("bsu_r", (128, 16))
    inp("bsa_r", (128, 8)); inp("bdnsd", (7, H))
    inp("cmask", (2, 128, 512))

    out_own = nc.dram_tensor("out_own", [1024, H], F32, kind="ExternalOutput").ap()
    skind = "ExternalOutput" if dump else "Internal"
    kT_scr = nc.dram_tensor("kT_scr", [H, SEQ], BF16, kind=skind).ap()
    vT_scr = nc.dram_tensor("vT_scr", [SEQ, H], BF16, kind=skind).ap()
    xlt_scr = nc.dram_tensor("xlt_scr", [NT_SEQ, 128, H], BF16, kind=skind).ap()
    res_scr = nc.dram_tensor("res_scr", [1024, H], F32, kind=skind).ap()
    x2_scr = nc.dram_tensor("x2_scr", [1024, H], F32, kind=skind).ap()
    if dump:
        q_dump = nc.dram_tensor("q_dump", [128, 16384], BF16, kind="ExternalOutput").ap()
        at_dump = nc.dram_tensor("at_dump", [128, 16384], BF16, kind="ExternalOutput").ap()
        accd = [nc.dram_tensor(f"acc_dump{k}", [128, 16384], F32,
                               kind="ExternalOutput").ap() for k in range(3)]
        w_dump = nc.dram_tensor("w_dump", [128, 64], F32, kind="ExternalOutput").ap()
        sh_dump = nc.dram_tensor("sh_dump", [128, 16384], BF16, kind="ExternalOutput").ap()
        z_dump = nc.dram_tensor("z_dump", [128, 16384], BF16, kind="ExternalOutput").ap()
        sml_dump = nc.dram_tensor("sml_dump", [128, 128], F32, kind="ExternalOutput").ap()
        x_dump = nc.dram_tensor("x_dump", [4, 128, 2048], F32, kind="ExternalOutput").ap()
        xltb_dump = nc.dram_tensor("xltb_dump", [128, 16384], BF16, kind="ExternalOutput").ap()
        id_dump = nc.dram_tensor("id_dump", [128, 256], F32, kind="ExternalOutput").ap()
        xtt_dump = nc.dram_tensor("xtt_dump", [128, 16384], BF16, kind="ExternalOutput").ap()
        id_dump2 = nc.dram_tensor("id_dump2", [128, 128], F32, kind="ExternalOutput").ap()
        x_dump_v = [x_dump[k] for k in range(4)]

    sb = nc.alloc_sbuf_tensor
    ident_bf = sb("ident_bf", [128, 128], BF16)
    ident_f32 = sb("ident_f32", [128, 128], F32)
    onescol_bf = sb("onescol_bf", [128, 1], BF16)
    onesrow_f32 = sb("onesrow_f32", [1, 128], F32)
    negC = sb("negC", [128, 1], F32)
    c1e30 = sb("c1e30", [128, 1], F32)
    c1e4 = sb("c1e4", [128, 1], F32)
    wg_bf = sb("wg_bf", [128, 256], BF16)
    bg_sb = sb("bg_sb", [1, 16], F32)
    bup_sb = sb("bup_sb", [128, 48], F32)
    bsw_sb = sb("bsw_sb", [128, 48], F32)
    bsu_sb = sb("bsu_sb", [128, 16], F32)
    bsa_sb = sb("bsa_sb", [128, 8], F32)
    bdnsd_sb = sb("bdnsd_sb", [8, H], BF16)
    wcomb = sb("wcomb", [8, 1024], BF16)
    w_all = sb("w_all", [128, 64], F32)
    srl = sb("srl", [1, 128], F32)
    srlb = sb("srlb", [128, 128], F32)
    sml = sb("sml", [128, 160], F32)
    smlsnap = sb("smlsnap", [128, 128], F32) if dump else None
    tmp_bf = sb("tmp_bf", [128, 512], BF16)

    NBF = 59392
    BB = sb("BB", [128, NBF], BF16)
    NF = 20480
    FB = sb("FB", [128, NF], F32)
    assert nc.sbuf_bytes_remaining >= 0

    def bbv(off, n):
        return BB[:, off:off + n]

    def fbv(off, n):
        return FB[:, off:off + n]

    OFF_WKT = 0
    OFF_XLTA = 32768
    OFF_XLTR = 40960
    OFF_KVO = 49152
    OFF_Q = 0
    OFF_XLTB = 16384
    OFF_ATT = 16384
    OFF_KVH = 32768
    OFF_PT = 50176
    OFF_CMASK = 54272
    OFF_WSLAB = 50176
    OFF_XTT = 0
    OFF_WOQ = 32768
    OFF_R2 = 16384
    OFF_R3 = 32768
    OFF_R4 = 49152

    OFF_XLN_AB = 45056
    OFF_XLN_DE = 53248
    OFF_ACC = 0
    OFF_XF = 16384
    OFF_XA4 = 8192

    P = [nc.alloc_psum_tensor(f"P{i}", [128, 512], F32) for i in range(7)]
    PB7 = nc.alloc_psum_tensor("PB7", [128, 1024], BF16)

    pb = PB(nc)
    SYNC_SEMS = ["LB_H", "LX0", "LX1", "LX2", "LX3", "LK0", "LK1",
                 "SKV0", "SKV1", "SX0", "SX1", "SX2", "SX3", "SO"]
    GPS_SEMS = ["LB_G", "LWa0", "LWa1", "LWb0", "LWb1", "LWc0", "LWc1",
                "LWd0", "LWd1"]
    for s in SYNC_SEMS + GPS_SEMS + ["T", "S", "V", "G"]:
        pb.new_sem(s)
    sem_nums = [h.num for h in pb.sem.values()]
    clear_range = range(min(sem_nums), max(sem_nums) + 1)
    if CFG.get("preamble", True):
        nc.gpsimd.sem_clear(clear_range)
        nc.all_engine_barrier()

    def dma_sync(sem, dst, src, waits=()):
        return pb.emit("sync", lambda e, d=dst, s=src: e.dma_start(out=d, in_=s),
                       waits, sem, 16)

    def dma_cast(sem, dst, src, waits=()):
        return pb.emit("gpsimd", lambda e, d=dst, s=src: e.dma_start(out=d, in_=s),
                       waits, sem, 16)

    def dump_barrier():
        v = pb.cnt["SO"]
        for eng in ENGINES:
            pb.wait(eng, [("SO", v)])

    TEN, VEC, SCA = "tensor", "vector", "scalar"
    GPS = "gpsimd"

    pb.emit(GPS, lambda e: e.memset(ident_bf[:], 0.0))
    pb.emit(GPS, lambda e: e.affine_select(
        out=ident_bf[:], in_=ident_bf[:], compare_op=ALU.not_equal, fill=1.0,
        base=0, pattern=[[-1, 128]], channel_multiplier=1))
    pb.emit(GPS, lambda e: e.memset(ident_f32[:], 0.0))
    pb.emit(GPS, lambda e: e.affine_select(
        out=ident_f32[:], in_=ident_f32[:], compare_op=ALU.not_equal, fill=1.0,
        base=0, pattern=[[-1, 128]], channel_multiplier=1))
    pb.emit(GPS, lambda e: e.memset(onescol_bf[:], 1.0))
    pb.emit(GPS, lambda e: e.memset(onesrow_f32[:], 1.0))
    pb.emit(GPS, lambda e: e.memset(negC[:], -C_SHIFT))
    pb.emit(GPS, lambda e: e.memset(sml[:, 144:160], 0.0))
    pb.emit(GPS, lambda e: e.memset(c1e30[:], 1.0e30))
    pb.emit(GPS, lambda e: e.memset(c1e4[:], 1.0e4))
    G_CONST = pb.emit(GPS, lambda e: e.memset(wcomb[:], 1.0), inc="G")

    cmask_bf = BB[:, OFF_CMASK:OFF_CMASK + 1024]
    dma_cast("LB_G", cmask_bf.rearrange("p (i j) -> p i j", i=2),
             di["cmask"].rearrange("i p j -> p i j"))
    LG_WGC = dma_cast("LB_G", wg_bf[:], di["wg_in"])
    LH_BIAS = 0
    for name, dst in (("bg_in", bg_sb[:]), ("bup_r", bup_sb[:]),
                      ("bsw_r", bsw_sb[:]), ("bsu_r", bsu_sb[:]),
                      ("bsa_r", bsa_sb[:])):
        LH_BIAS = dma_sync("LB_H", dst, di[name])
    LG_BDN = dma_cast("LB_G", bdnsd_sb[0:7, :], di["bdnsd"])

    wkT_sb = bbv(OFF_WKT, 32768)
    LG_WKT = 0
    if "A" in phases:
        for q4 in range(4):
            LG_WKT = dma_cast(
                "LB_G", bbv(OFF_WKT + q4 * 8192, 8192),
                di["wk_r"][:, q4 * 8192:(q4 + 1) * 8192])

    state = dict(pb7_free=0)

    LNDBG = dict(tile=None)
    fill_state = dict(k=0)

    def vfill(n=2):
        for _ in range(n):
            k = fill_state["k"] = (fill_state["k"] + 1) % 4
            pb.emit(VEC, lambda e, k=k: e.tensor_copy(
                sml[:, 151 + k:152 + k], sml[:, 150:151]))

    def emit_ln(xap, out_bf_ap, out_f32_ap, waits_v, par):
        o = par * 64
        for c in range(4):
            pb.emit(VEC, lambda e, c=c, o=o: e.bn_stats(
                sml[:, o + 6 * c:o + 6 * c + 6], xap[:, 512 * c:512 * (c + 1)]),
                waits_v if c == 0 else ())
        for c in range(4):
            pb.emit(VEC, lambda e, c=c, o=o: e.bn_aggr(
                sml[:, o + 24 + 2 * c:o + 26 + 2 * c], sml[:, o + 6 * c:o + 6 * c + 6]))
        mview = sml[:, o + 24:o + 32].rearrange("p (c t) -> p c t", t=2)[:, :, 0]
        vview = sml[:, o + 24:o + 32].rearrange("p (c t) -> p c t", t=2)[:, :, 1]
        vfill()
        pb.emit(VEC, lambda e: e.reduce_sum(sml[:, o + 32:o + 33], mview, AX))
        pb.emit(VEC, lambda e: e.tensor_mul(sml[:, o + 36:o + 40], mview, mview))
        pb.emit(VEC, lambda e: e.reduce_sum(sml[:, o + 41:o + 42], vview, AX))
        pb.emit(VEC, lambda e: e.tensor_scalar_mul(
            sml[:, o + 33:o + 34], sml[:, o + 32:o + 33], -0.25))
        vfill(1)
        pb.emit(VEC, lambda e: e.reduce_sum(
            sml[:, o + 40:o + 41], sml[:, o + 36:o + 40], AX))
        pb.emit(VEC, lambda e: e.tensor_mul(
            sml[:, o + 44:o + 45], sml[:, o + 33:o + 34], sml[:, o + 33:o + 34]))
        vfill(1)
        pb.emit(VEC, lambda e: e.tensor_add(
            sml[:, o + 42:o + 43], sml[:, o + 40:o + 41], sml[:, o + 41:o + 42]))
        vfill(2)
        pb.emit(VEC, lambda e: e.tensor_scalar_mul(
            sml[:, o + 43:o + 44], sml[:, o + 42:o + 43], 0.25))
        vfill(2)
        pb.emit(VEC, lambda e: e.tensor_sub(
            sml[:, o + 45:o + 46], sml[:, o + 43:o + 44], sml[:, o + 44:o + 45]))
        vfill(2)
        v_stats = pb.emit(VEC, lambda e: e.tensor_scalar_add(
            sml[:, o + 46:o + 47], sml[:, o + 45:o + 46], EPS), inc="V")
        pb.emit(SCA, lambda e: e.activation(
            sml[:, o + 47:o + 48], sml[:, o + 46:o + 47], ACT.Ln), [("V", v_stats)])
        s_r = pb.emit(SCA, lambda e: e.activation(
            sml[:, o + 48:o + 49], sml[:, o + 47:o + 48], ACT.Exp, scale=-0.5),
            inc="S")
        v_negmr = pb.emit(VEC, lambda e: e.tensor_mul(
            sml[:, o + 49:o + 50], sml[:, o + 33:o + 34],
            sml[:, o + 48:o + 49]), [("S", s_r)], inc="V")
        if dump and LNDBG["tile"] is not None:
            tdbg = LNDBG["tile"]
            vfill(2)
            pb.emit(VEC, lambda e, tdbg=tdbg: e.tensor_copy(
                smlsnap[:, tdbg * 4:tdbg * 4 + 2], sml[:, o + 48:o + 50]))
            pb.emit(VEC, lambda e, tdbg=tdbg: e.tensor_copy(
                smlsnap[:, tdbg * 4 + 2:tdbg * 4 + 4], sml[:, o + 45:o + 47]))
            if 2 <= tdbg <= 5:
                dma_sync("SO", x_dump_v[tdbg - 2], xap, [("V", pb.cnt["V"])])
            LNDBG["tile"] = None
        s_cast = pb.emit(SCA, lambda e: e.activation(
            out_bf_ap, xap, ACT.Identity, bias=sml[:, o + 49:o + 50],
            scale=sml[:, o + 48:o + 49]), [("V", v_negmr)], inc="S")
        if out_f32_ap is not None:
            s_cast = pb.emit(SCA, lambda e: e.activation(
                out_f32_ap, xap, ACT.Identity, bias=sml[:, o + 49:o + 50],
                scale=sml[:, o + 48:o + 49]), inc="S")
        return s_cast

    def emit_transpose16(src_fn, dst_off, waits_t):
        vlast = None
        for r in range(2):
            for k in range(8):
                fc = r * 8 + k
                w = []
                if k == 0:
                    w = list(waits_t) if r == 0 else []
                    w.append(("V", state["pb7_free"] if r == 0 else vlast))
                pb.emit(TEN, lambda e, fc=fc, k=k: e.transpose(
                    PB7[:, k * 128:(k + 1) * 128], src_fn(fc), ident_bf[:]),
                    w, inc="T" if k == 7 else None)
            t_done = pb.cnt["T"]
            vlast = pb.emit(VEC, lambda e, r=r: e.tensor_copy(
                bbv(dst_off + r * 1024, 1024), PB7[:]), [("T", t_done)], inc="V")
        state["pb7_free"] = vlast
        return vlast

    ST_XLT = {}
    A1_T_END = A2_T_END = 0
    if "A" in phases:
        xcast = {}
        ttrans = {}
        vcopy = {}
        kevac = {}
        kst = {}
        kmm_last = {}
        A_ST = CFG.get("a_st", 8)
        for st in range(A_ST):
            for tt in range(4):
                t = 4 * st + tt
                xslot = fbv(OFF_XA4 + (t % 4) * 2048, 2048)
                w = [("S", xcast[t - 4])] if t >= 4 else []
                if dump and t >= 4:
                    w.append(("SO", pb.cnt["SO"]))
                lh_x = dma_sync(f"LX{t % 4}", xslot, di["x_seq"].rearrange(
                    "(t p) h -> t p h", p=128)[t], w)
                wv = [(f"LX{t % 4}", lh_x)]
                if t >= 2:
                    wv.append(("T", ttrans[t - 2]))
                    wv.append(("S", xcast[t - 2]))
                xlslot = bbv(OFF_XLN_AB + (t % 2) * 2048, 2048)
                LNDBG["tile"] = t
                xcast[t] = emit_ln(xslot, xlslot, None, wv, t % 2)
                wt = [("S", xcast[t]), ("G", G_CONST)]
                if st >= 1:
                    wt.append((f"SX{tt}", ST_XLT[4 * (st - 1) + tt]))
                vcopy[t] = emit_transpose16(
                    lambda fc, xl=xlslot: xl[:, fc * 128:(fc + 1) * 128],
                    OFF_XLTA + tt * 2048, wt)
                ttrans[t] = pb.cnt["T"]
                ST_XLT[t] = dma_sync(f"SX{tt}", xlt_scr[t],
                                     bbv(OFF_XLTA + tt * 2048, 2048),
                                     [("V", vcopy[t])])
            xlt_view = bbv(OFF_XLTA, 8192).rearrange(
                "p (tile fc tl) -> p tile fc tl", tile=4, fc=16)
            for dc in range(16):
                bank = P[dc % 2]
                if dc < 2:
                    w = [("LB_G", LG_WKT), ("V", vcopy[4 * st + 3])]
                else:
                    w = [("V", kevac[(st, dc - 2)])]
                for fc in range(16):
                    pb.emit(TEN, lambda e, dc=dc, fc=fc, bank=bank, xv=xlt_view:
                            e.matmul(
                        bank[:], wkT_sb[:, fc * 2048 + dc * 128:fc * 2048 + (dc + 1) * 128],
                        xv[:, :, fc, :], start=(fc == 0), stop=(fc == 15)),
                        w if fc == 0 else (), inc="T" if fc == 15 else None)
                kmm = pb.cnt["T"]
                kmm_last[st] = kmm
                kv_slot = bbv(OFF_KVO + (dc % 2) * 512, 512)
                w2 = [("T", kmm)]
                prevkey = (st, dc - 2) if dc >= 2 else (st - 1, dc + 14)
                if prevkey in kst:
                    w2.append((f"SKV{dc % 2}", kst[prevkey]))
                kevac[(st, dc)] = pb.emit(
                    VEC, lambda e, bank=bank, kv=kv_slot: e.tensor_copy(kv, bank[:]),
                    w2, inc="V")
                kst[(st, dc)] = dma_sync(
                    f"SKV{dc % 2}",
                    kT_scr[dc * 128:(dc + 1) * 128, st * 512:(st + 1) * 512],
                    kv_slot, [("V", kevac[(st, dc)])])
        if dump:
            dma_sync("SO", sml_dump, smlsnap[:], [("V", pb.cnt["V"])])
            dump_barrier()
        A1_T_END = pb.cnt["T"]

        LG_WVT = 0
        pb.wait(GPS, [("T", A1_T_END)])
        for q4 in range(4):
            LG_WVT = dma_cast(
                "LB_G", bbv(OFF_WKT + q4 * 8192, 8192),
                di["wv_r"][:, q4 * 8192:(q4 + 1) * 8192])
        wvT_sb = bbv(OFF_WKT, 32768)
        vevac = {}
        vst = {}
        vmm_last = {}
        for t in range(4 * A_ST):
            slot = bbv(OFF_XLTR + (t % 2) * 2048, 2048)
            w = [(f"SX{t % 4}", ST_XLT[t])]
            if t >= 2:
                w.append(("T", vmm_last[t - 2]))
            xltld = dma_sync(f"LX{t % 2}", slot, xlt_scr[t], w)
            for dvg in range(4):
                bank = P[dvg % 2]
                if dvg < 2:
                    w = [(f"LX{t % 2}", xltld), ("LB_G", LG_WVT)]
                    prevkey = (t - 1, dvg + 2)
                else:
                    w = []
                    prevkey = (t, dvg - 2)
                if prevkey in vevac:
                    w.append(("V", vevac[prevkey]))
                for fc in range(16):
                    pb.emit(TEN, lambda e, fc=fc, dvg=dvg, bank=bank, sl=slot:
                            e.matmul(
                        bank[:], sl[:, fc * 128:(fc + 1) * 128],
                        wvT_sb[:, fc * 2048 + dvg * 512:fc * 2048 + (dvg + 1) * 512],
                        start=(fc == 0), stop=(fc == 15)),
                        w if fc == 0 else (), inc="T" if fc == 15 else None)
                vmm = pb.cnt["T"]
                vmm_last[t] = vmm
                kv_slot = bbv(OFF_KVO + (dvg % 2) * 512, 512)
                w2 = [("T", vmm)]
                prevkey = (t, dvg - 2) if dvg >= 2 else (t - 1, dvg + 2)
                if prevkey in vst:
                    w2.append((f"SKV{dvg % 2}", vst[prevkey]))
                vevac[(t, dvg)] = pb.emit(
                    VEC, lambda e, bank=bank, kv=kv_slot: e.tensor_copy(kv, bank[:]),
                    w2, inc="V")
                vst[(t, dvg)] = dma_sync(
                    f"SKV{dvg % 2}",
                    vT_scr[t * 128:(t + 1) * 128, dvg * 512:(dvg + 1) * 512],
                    kv_slot, [("V", vevac[(t, dvg)])])
        A2_T_END = pb.cnt["T"]

    B_V_END = 0
    if "B" in phases:
        if dump:
            _vid = pb.emit(VEC, lambda e: e.tensor_copy(
                srlb[:], ident_bf[:]), [("T", A2_T_END)], inc="V")
            dma_sync("SO", id_dump[:, 0:128], srlb[:], [("V", _vid)])
            dump_barrier()
        bcast = {}
        btrans = {}
        for ot in range(NT_OWN):
            xslot = fbv(OFF_XF + (ot % 2) * 2048, 2048)
            w = [("S", bcast[ot - 2])] if ot >= 2 else []
            lh_x = dma_sync(f"LX{ot % 2}", xslot, di["x_own"].rearrange(
                "(t p) h -> t p h", p=128)[ot], w)
            wv = [(f"LX{ot % 2}", lh_x)]
            if ot == 0:
                wv.append(("T", A2_T_END))
            if ot >= 2:
                wv.append(("T", btrans[ot - 2]))
                wv.append(("S", bcast[ot - 2]))
            xlslot = bbv(OFF_XLN_AB + (ot % 2) * 2048, 2048)
            if dump and "A" not in phases:
                LNDBG["tile"] = ot
            bcast[ot] = emit_ln(xslot, xlslot, None, wv, ot % 2)
            wt = [("S", bcast[ot]), ("G", G_CONST), ("T", A2_T_END)]
            emit_transpose16(lambda fc, xl=xlslot: xl[:, fc * 128:(fc + 1) * 128],
                             OFF_XLTB + ot * 2048, wt)
            btrans[ot] = pb.cnt["T"]
        v_xltb = pb.cnt["V"]
        XLTB_SO = None
        if dump:
            XLTB_SO = dma_sync("SO", xltb_dump, bbv(OFF_XLTB, 16384),
                               [("V", v_xltb)])
            dump_barrier()
        skip_q = bool(CFG.get("noq"))
        xltb_view = bbv(OFF_XLTB, 16384).rearrange(
            "p (tile fc tl) -> p tile fc tl", tile=8, fc=16)
        qevac = {}
        qmm_done = {}
        lg_wq = {}

        def load_wq(dc, w):
            lg_wq[dc] = dma_cast(
                f"LWd{dc % 2}",
                bbv(OFF_WSLAB + (dc % 2) * 2048, 2048), di["wq_r"][dc], w)

        if not skip_q:
            wq0w = [("T", A2_T_END)]
            if CFG.get("serial_wq"):
                wq0w.append(("V", v_xltb))
            load_wq(0, wq0w)
            load_wq(1, [])
        for dc in range(16 if not skip_q else 0):
            slab = bbv(OFF_WSLAB + (dc % 2) * 2048, 2048)
            for tq in range(2):
                bank = P[tq]
                w = []
                if tq == 0:
                    w = [(f"LWd{dc % 2}", lg_wq[dc]), ("V", v_xltb)]
                if (dc - 1, tq) in qevac:
                    w.append(("V", qevac[(dc - 1, tq)]))
                for fc in range(16):
                    pb.emit(TEN, lambda e, fc=fc, tq=tq, bank=bank, sl=slab: e.matmul(
                        bank[:], sl[:, fc * 128:(fc + 1) * 128],
                        xltb_view[:, tq * 4:(tq + 1) * 4, fc, :],
                        start=(fc == 0), stop=(fc == 15)),
                        w if fc == 0 else (), inc="T" if fc == 15 else None)
                qmm_done[dc] = pb.cnt["T"]
                qevac[(dc, tq)] = pb.emit(
                    VEC, lambda e, dc=dc, tq=tq, bank=bank: e.tensor_copy(
                        bbv(OFF_Q + dc * 1024 + tq * 512, 512), bank[:]),
                    [("T", qmm_done[dc])], inc="V")
            if dc + 2 < 16:
                load_wq(dc + 2, [("T", qmm_done[dc])])
        if dump and "A" not in phases:
            dma_sync("SO", sml_dump, smlsnap[:], [("V", pb.cnt["V"])])
        B_V_END = pb.cnt["V"]
        QDUMP_SO = None
        if dump:
            QDUMP_SO = dma_sync("SO", q_dump, bbv(OFF_Q, 16384), [("V", B_V_END)])
            dump_barrier()

    C_V_END = C_T_END = 0
    if "C" in phases:
        kvload = {}
        head_done = {}
        prev_evac = B_V_END
        prev_l_user = 0
        prev_rlb_user = 0
        bank_free_S = {0: 0, 1: 0}
        for h in range(heads):
            buf = h % 2
            koff = OFF_KVH + buf * 8192
            voff = OFF_KVH + buf * 8192 + 4096
            w = [("V", B_V_END)] if h < 2 else [("T", head_done[h - 2])]
            dma_sync(f"LK{h % 2}", bbv(koff, 4096),
                     kT_scr[h * 128:(h + 1) * 128, :], w)
            for vh in range(2):
                kvload[h] = dma_sync(
                    f"LK{h % 2}",
                    bbv(voff + vh * 2048, 2048).rearrange("p (kt d) -> p kt d", kt=16),
                    vT_scr.rearrange("(kt p) d -> p kt d", p=128)[
                        :, vh * 16:(vh + 1) * 16, h * 128:(h + 1) * 128])
            for s in range(NT_OWN):
                ngr = s + 1
                exp_done = {}
                av_done = {}
                smm = {}

                def emit_st(g):
                    bank = P[g % 2]
                    w = []
                    if g == 0:
                        w = [(f"LK{h % 2}", kvload[h])]
                        if h == 0 and s == 0:
                            w.append(("G", G_CONST))
                    if g >= 2:
                        w.append(("S", exp_done[g - 2]))
                    elif bank_free_S[g % 2]:
                        w.append(("S", bank_free_S[g % 2]))
                    for jj in range(4):
                        kt = 4 * g + jj
                        pb.emit(TEN, lambda e, kt=kt, jj=jj, bank=bank, ko=koff,
                                h=h, s=s: e.matmul(
                            bank[:, jj * 128:(jj + 1) * 128],
                            bbv(ko + kt * 128, 128),
                            bbv(OFF_Q + h * 1024 + s * 128, 128),
                            start=True, stop=True),
                            w if jj == 0 else (), inc="T" if jj == 3 else None)
                    smm[g] = pb.cnt["T"]

                emit_st(0)
                for g in range(ngr):
                    bank = P[g % 2]
                    if g + 1 < ngr:
                        emit_st(g + 1)
                    wexp = [("T", smm[g])]
                    if g >= 2:
                        wexp.append(("T", av_done[g - 2]))
                    ptap = bbv(OFF_PT + (g % 2) * 512, 512)
                    exp_done[g] = pb.emit(SCA, lambda e, bank=bank, pt=ptap:
                                          e.activation(
                        pt, bank[:], ACT.Exp, bias=negC[:, 0:1], scale=SCALE),
                        wexp, inc="S")
                    bank_free_S[g % 2] = exp_done[g]
                    gate_sem = ("S", exp_done[g])
                    if g == s:
                        side = 0 if s < 4 else 1
                        vm = pb.emit(VEC, lambda e, pt=ptap, side=side: e.tensor_mul(
                            pt, pt, BB[:, OFF_CMASK + side * 512:
                                       OFF_CMASK + (side + 1) * 512]),
                            [("S", exp_done[g]), ("LB_G", LG_WGC)], inc="V")
                        gate_sem = ("V", vm)
                    w = [gate_sem]
                    if g == 0:
                        w.append(("V", prev_evac))
                        if prev_l_user:
                            w.append(("V", prev_l_user))
                    first = (g == 0)
                    last = (g == ngr - 1)
                    for jj in range(4):
                        kt = 4 * g + jj
                        pb.emit(TEN, lambda e, kt=kt, jj=jj, pt=ptap, vo=voff,
                                fi=first, la=last: e.matmul(
                            P[2][:, 0:128], bbv(vo + kt * 128, 128),
                            pt[:, jj * 128:(jj + 1) * 128],
                            start=(fi and jj == 0), stop=(la and jj == 3)),
                            w if jj == 0 else ())
                    for jj in range(4):
                        pb.emit(TEN, lambda e, jj=jj, pt=ptap, fi=first, la=last:
                                e.matmul(
                            P[3][0:1, 0:128], onescol_bf[:],
                            pt[:, jj * 128:(jj + 1) * 128],
                            start=(fi and jj == 0), stop=(la and jj == 3)),
                            (), inc="T" if jj == 3 else None)
                    av_done[g] = pb.cnt["T"]
                v_rl = pb.emit(VEC, lambda e: e.reciprocal(srl[:], P[3][0:1, 0:128]),
                               [("T", av_done[ngr - 1])], inc="V")
                prev_l_user = v_rl
                w = [("V", v_rl)]
                if prev_rlb_user:
                    w.append(("V", prev_rlb_user))
                t_rlb = pb.emit(TEN, lambda e: e.matmul(
                    P[4][:, 0:128], onesrow_f32[:], srl[:], start=True, stop=True),
                    w, inc="T")
                v_cp = pb.emit(VEC, lambda e: e.tensor_copy(srlb[:], P[4][:, 0:128]),
                               [("T", t_rlb)], inc="V")
                vfill(2)
                wev = [("V", v_cp)]
                if h == 0 and s == 0 and dump and "B" in phases and XLTB_SO:
                    wev.append(("SO", XLTB_SO))
                prev_evac = pb.emit(VEC, lambda e, h=h, s=s: e.tensor_mul(
                    bbv(OFF_ATT + h * 1024 + s * 128, 128), P[2][:, 0:128], srlb[:]),
                    wev, inc="V")
                prev_rlb_user = prev_evac
            head_done[h] = pb.cnt["T"]
        C_V_END = pb.cnt["V"]
        C_T_END = pb.cnt["T"]
        ATD_SO = None
        if dump:
            ATD_SO = dma_sync("SO", at_dump, bbv(OFF_ATT, 16384), [("V", C_V_END)])
            dump_barrier()

    D_T_END = D_V_END = D_SO_END = 0
    if "D" in phases:
        att_view = bbv(OFF_ATT, 16384)
        d_evac = {}
        oq_done = {}
        lg_woq = {}

        def load_woq(oq, w):
            lg_woq[oq] = dma_cast(
                f"LWd{oq % 2}",
                bbv(OFF_WOQ + (oq % 2) * 8192, 8192), di["wo_r"][oq], w)

        if dump:
            _vid3 = pb.emit(VEC, lambda e: e.tensor_copy(
                srlb[:], ident_bf[:]), [("T", pb.cnt["T"]), ("V", pb.cnt["V"])],
                inc="V")
            dma_sync("SO", id_dump2, srlb[:], [("V", _vid3)])
            dump_barrier()
        load_woq(0, [("T", C_T_END)])
        load_woq(1, [])
        for oq in range(4 if not CFG.get("skip_dmm") else 0):
            woq = bbv(OFF_WOQ + (oq % 2) * 8192, 8192)
            for quad in range(2):
                for dc in range(16):
                    for tl in range(4):
                        tile = quad * 4 + tl
                        w = []
                        if dc == 0 and tl == 0:
                            w = [(f"LWd{oq % 2}", lg_woq[oq]), ("V", C_V_END)]
                            pk = (oq, 0, 3) if quad == 1 else (oq - 1, 1, 3)
                            if pk in d_evac:
                                w.append(("V", d_evac[pk]))
                        pb.emit(TEN, lambda e, dc=dc, tile=tile, tl=tl, wq=woq:
                                e.matmul(
                            P[tl][:], att_view[:, dc * 1024 + tile * 128:
                                               dc * 1024 + (tile + 1) * 128],
                            wq[:, dc * 512:(dc + 1) * 512],
                            start=(dc == 0), stop=(dc == 15)),
                            w, inc="T" if dc == 15 else None)
                dmm = pb.cnt["T"]
                oq_done[oq] = dmm
                for tl in range(4):
                    tile = quad * 4 + tl
                    if CFG.get("evac_alt"):
                        d_evac[(oq, quad, tl)] = pb.emit(
                            VEC, lambda e, tl=tl: e.tensor_copy(
                                srlb[:], P[tl][:, 0:128]),
                            [("T", dmm)], inc="V")
                    else:
                        d_evac[(oq, quad, tl)] = pb.emit(
                            VEC, lambda e, tile=tile, oq=oq, tl=tl: e.tensor_copy(
                                fbv(OFF_ACC + tile * 2048 + oq * 512, 512), P[tl][:]),
                            [("T", dmm)], inc="V")
            if oq + 2 < 4:
                load_woq(oq + 2, [("T", oq_done[oq])])
        if dump:
            _vid2 = pb.emit(VEC, lambda e: e.tensor_copy(
                srlb[:], ident_bf[:]), [("T", pb.cnt["T"]), ("V", pb.cnt["V"])],
                inc="V")
            dma_sync("SO", id_dump[:, 128:256], srlb[:], [("V", _vid2)])
            dump_barrier()
        dcast = {}
        dtrans = {}
        st_res = {}
        for tile in range(NT_OWN if not CFG.get("skip_dmm") else 0):
            xslot = fbv(OFF_XF + (tile % 2) * 2048, 2048)
            w = [("V", d_evac[(3, tile // 4, tile % 4)])]
            if tile >= 2:
                w.append(("S", dcast[tile - 2]))
                w.append(("SO", pb.cnt["SO"]))
            lh_x = dma_sync(f"LX{tile % 2}", xslot, di["x_own"].rearrange(
                "(t p) h -> t p h", p=128)[tile], w)
            x2ap = fbv(OFF_ACC + tile * 2048, 2048)
            v_x2 = pb.emit(VEC, lambda e, a=x2ap, xs=xslot: e.tensor_add(a, a, xs),
                           [(f"LX{tile % 2}", lh_x)], inc="V")
            dma_sync("SO", x2_scr.rearrange("(t p) h -> t p h", p=128)[tile], x2ap,
                     [("V", v_x2)])
            xlslot = bbv(OFF_XLN_DE + (tile % 2) * 2048, 2048)
            wv = [("V", v_x2)]
            if tile >= 2:
                wv.append(("T", dtrans[tile - 2]))
                wv.append(("S", dcast[tile - 2]))
            dcast[tile] = emit_ln(x2ap, xlslot, xslot, wv, tile % 2)
            st_res[tile] = dma_sync(
                "SO", res_scr.rearrange("(t p) h -> t p h", p=128)[tile], xslot,
                [("S", dcast[tile])])
            wtr = [("S", dcast[tile])]
            if tile == 0 and dump and "B" in phases and QDUMP_SO:
                wtr.append(("SO", QDUMP_SO))
            emit_transpose16(lambda fc, xl=xlslot: xl[:, fc * 128:(fc + 1) * 128],
                             OFF_XTT + tile * 2048, wtr)
            dtrans[tile] = pb.cnt["T"]
        if dump and not CFG.get("skip_dmm"):
            dma_sync("SO", xtt_dump, bbv(OFF_XTT, 16384),
                     [("T", pb.cnt["T"]), ("V", pb.cnt["V"])])
            dump_barrier()
        D_T_END = pb.cnt["T"]
        D_V_END = pb.cnt["V"]
        D_SO_END = pb.cnt["SO"]

    if "E" in phases:
        xtt_view = bbv(OFF_XTT, 16384).rearrange(
            "p (tile fc tl) -> p tile fc tl", tile=8, fc=16)

        def xtt_rhs(fc, th):
            return xtt_view[:, th * 4:(th + 1) * 4, fc, :]

        prev_iter_T = D_T_END
        prev_iter_V = D_V_END
        prev_iter_SO = D_SO_END
        for j in range(2):
            w_gate_done = {}
            for tile in range(NT_OWN):
                if tile == 0:
                    w = [("T", prev_iter_T), ("V", prev_iter_V), ("LB_G", LG_WGC)]
                else:
                    w = [("V", w_gate_done[tile - 1])]
                for fc in range(16):
                    pb.emit(TEN, lambda e, fc=fc, tile=tile, j=j: e.matmul(
                        P[4][:, 0:8], xtt_view[:, tile, fc, :],
                        wg_bf[:, j * 128 + fc * 8:j * 128 + (fc + 1) * 8],
                        start=(fc == 0), stop=False),
                        w if fc == 0 else ())
                t_g = pb.emit(TEN, lambda e, j=j: e.matmul(
                    P[4][:, 0:8], onesrow_f32[:], bg_sb[0:1, j * 8:(j + 1) * 8],
                    start=False, stop=True), [("LB_H", LH_BIAS)], inc="T")
                o = 100
                pb.emit(VEC, lambda e: e.tensor_copy(sml[:, o:o + 8], P[4][:, 0:8]),
                        [("T", t_g)])
                vfill(2)
                pb.emit(VEC, lambda e: e.tensor_reduce(
                    sml[:, o + 8:o + 9], sml[:, o:o + 8], AX, ALU.min))
                vfill(2)
                pb.emit(VEC, lambda e: e.tensor_scalar(
                    sml[:, o + 10:o + 18], sml[:, o:o + 8], sml[:, o + 8:o + 9],
                    None, ALU.is_equal))
                vfill(2)
                pb.emit(VEC, lambda e: e.scalar_tensor_tensor(
                    sml[:, o + 18:o + 26], sml[:, o + 10:o + 18], c1e30[:, 0:1],
                    sml[:, o:o + 8], ALU.mult, ALU.add))
                vfill(2)
                pb.emit(VEC, lambda e: e.tensor_reduce(
                    sml[:, o + 9:o + 10], sml[:, o + 18:o + 26], AX, ALU.min))
                vfill(2)
                pb.emit(VEC, lambda e: e.tensor_scalar(
                    sml[:, o + 10:o + 18], sml[:, o:o + 8], sml[:, o + 9:o + 10],
                    None, ALU.is_gt))
                vfill(2)
                pb.emit(VEC, lambda e: e.scalar_tensor_tensor(
                    sml[:, o + 18:o + 26], sml[:, o:o + 8], c1e4[:, 0:1],
                    sml[:, o + 10:o + 18], ALU.add, ALU.mult))
                vfill(2)
                v_m = pb.emit(VEC, lambda e: e.reduce_max(
                    sml[:, o + 8:o + 9], sml[:, o + 18:o + 26], AX, negate=True),
                    inc="V")
                s_e = pb.emit(SCA, lambda e: e.activation(
                    sml[:, o + 26:o + 34], sml[:, o + 18:o + 26], ACT.Exp,
                    bias=sml[:, o + 8:o + 9]), [("V", v_m)], inc="S")
                pb.emit(VEC, lambda e: e.reduce_sum(
                    sml[:, o + 9:o + 10], sml[:, o + 26:o + 34], AX), [("S", s_e)])
                vfill(2)
                pb.emit(VEC, lambda e: e.reciprocal(
                    sml[:, o + 8:o + 9], sml[:, o + 9:o + 10]))
                vfill(2)
                v_w = pb.emit(VEC, lambda e, tile=tile: e.tensor_scalar_mul(
                    w_all[:, tile * 8:(tile + 1) * 8], sml[:, o + 26:o + 34],
                    sml[:, o + 8:o + 9]), inc="V")
                t_wt = pb.emit(TEN, lambda e, tile=tile: e.transpose(
                    P[5][0:8, 0:128], w_all[:, tile * 8:(tile + 1) * 8],
                    ident_f32[:]), [("V", v_w)], inc="T")
                w_gate_done[tile] = pb.emit(VEC, lambda e, tile=tile: e.tensor_copy(
                    wcomb[0:6, tile * 128:(tile + 1) * 128], P[5][0:6, 0:128]),
                    [("T", t_wt)], inc="V")
            acc_init_done = {}
            prev_ai = None
            for tile in range(NT_OWN):
                for hc in range(4):
                    w = [("V", w_gate_done[tile])]
                    if tile == 0 and hc == 0:
                        w.append(("SO", prev_iter_SO))
                        w.append(("LB_G", LG_BDN))
                    if prev_ai:
                        w.append(("V", prev_ai))
                    t_ai = pb.emit(TEN, lambda e, tile=tile, hc=hc: e.matmul(
                        P[6][:], wcomb[0:7, tile * 128:(tile + 1) * 128],
                        bdnsd_sb[0:7, hc * 512:(hc + 1) * 512],
                        start=True, stop=True), w, inc="T")
                    prev_ai = pb.emit(
                        VEC, lambda e, tile=tile, hc=hc: e.tensor_copy(
                            fbv(OFF_ACC + tile * 2048 + hc * 512, 512), P[6][:]),
                        [("T", t_ai)], inc="V")
            V_ACCINIT = prev_ai
            DSO = None
            if dump and j == 0:
                dma_sync("SO", w_dump, w_all[:], [("V", V_ACCINIT)])
                DSO = dma_sync("SO", accd[0], fbv(OFF_ACC, 16384),
                               [("V", V_ACCINIT)])
                dump_barrier()

            sh_done = {}
            lg_su = {}

            def load_su(oc_h, w):
                oc, hh = oc_h
                lg_su[oc_h] = dma_cast(
                    f"LWd{hh}",
                    bbv(OFF_R4 + hh * 1024, 1024),
                    di["wsu_r2"][oc][:, hh * 1024:(hh + 1) * 1024], w)

            wsu0 = [("T", prev_iter_T)]
            if dump and "C" in phases and ATD_SO:
                wsu0.append(("SO", ATD_SO))
            load_su((0, 0), wsu0)
            load_su((0, 1), [])
            lg_sa = dma_cast(
                "LB_G", bbv(OFF_R4 + 2048, 8192), di["wsa_r2"],
                [("T", prev_iter_T)])
            for oc in range(16):
                bank = P[oc % 2]
                s_prev_th = None
                for th in range(2):
                    w = []
                    if th == 0:
                        w = [("LWd0", lg_su[(oc, 0)]), ("LWd1", lg_su[(oc, 1)]),
                             ("V", V_ACCINIT)]
                        if oc >= 1:
                            w.append(("S", sh_done[oc - 1]))
                    else:
                        w = [("S", s_prev_th)]
                    for fc in range(16):
                        sl = bbv(OFF_R4 + (fc // 8) * 1024, 1024)
                        pb.emit(TEN, lambda e, fc=fc, th=th, bank=bank, sl=sl:
                                e.matmul(
                            bank[:], sl[:, (fc % 8) * 128:(fc % 8 + 1) * 128],
                            xtt_rhs(fc, th), start=(fc == 0), stop=(fc == 15)),
                            w if fc == 0 else (), inc="T" if fc == 15 else None)
                    t_sh = pb.cnt["T"]
                    s_prev_th = pb.emit(SCA, lambda e, oc=oc, th=th, bank=bank:
                                        e.activation(
                        bbv(OFF_R2 + oc * 1024 + th * 512, 512), bank[:],
                        ACT.Identity, bias=bsu_sb[:, oc:oc + 1]),
                        [("T", t_sh)], inc="S")
                sh_done[oc] = s_prev_th
                if oc + 1 < 16:
                    load_su((oc + 1, 0), [("S", s_prev_th)])
                    load_su((oc + 1, 1), [])
            S_SH_END = pb.cnt["S"]
            if dump and j == 0:
                dma_sync("SO", sh_dump, bbv(OFF_R2, 16384), [("S", S_SH_END)])
                dump_barrier()
            wsa_sb = bbv(OFF_R4 + 2048, 8192)
            z_done = {}
            prev_zmul = 0
            for g2 in range(2):
                for e2c in range(8):
                    zc = g2 * 8 + e2c
                    bank = P[zc % 2]
                    v_prev_th = None
                    for th in range(2):
                        w = []
                        if th == 0:
                            if zc == 0:
                                w = [("LB_G", lg_sa), ("S", S_SH_END)]
                            elif zc >= 2:
                                w = [("V", z_done[zc - 2])]
                        else:
                            w = [("V", v_prev_th)]
                        for ec in range(8):
                            pb.emit(TEN, lambda e, ec=ec, e2c=e2c, g2=g2, th=th,
                                    bank=bank: e.matmul(
                                bank[:],
                                wsa_sb[:, ec * 1024 + e2c * 128:ec * 1024 + (e2c + 1) * 128],
                                bbv(OFF_R2 + (g2 * 8 + ec) * 1024 + th * 512, 512),
                                start=(ec == 0), stop=(ec == 7)),
                                w if ec == 0 else (), inc="T" if ec == 7 else None)
                        t_a = pb.cnt["T"]
                        ws = [("T", t_a)]
                        if prev_zmul:
                            ws.append(("V", prev_zmul))
                        s_si = pb.emit(SCA, lambda e, e2c=e2c, bank=bank:
                                       e.activation(
                            tmp_bf[:], bank[:], ACT.Silu,
                            bias=bsa_sb[:, e2c:e2c + 1]), ws, inc="S")
                        v_prev_th = pb.emit(VEC, lambda e, zc=zc, th=th: e.tensor_mul(
                            bbv(OFF_R3 + zc * 1024 + th * 512, 512), tmp_bf[:],
                            bbv(OFF_R2 + zc * 1024 + th * 512, 512)),
                            [("S", s_si)], inc="V")
                        prev_zmul = v_prev_th
                    z_done[zc] = v_prev_th
            V_Z_END = pb.cnt["V"]
            if dump and j == 0:
                ZSO = dma_sync("SO", z_dump, bbv(OFF_R3, 16384), [("V", V_Z_END)])
                dump_barrier()
            sd_evac = {}
            lg_sd = {}

            def load_sd(hq, w):
                lg_sd[hq] = dma_cast(
                    f"LWd{hq % 2}",
                    bbv(OFF_R2 + (hq % 2) * 8192, 8192), di["wsd_r2"][hq], w)

            w0 = [("V", V_Z_END), ("S", S_SH_END)]
            if DSO:
                w0.append(("SO", DSO))
            load_sd(0, w0)
            load_sd(1, [])
            for hq in range(4):
                wsdq = bbv(OFF_R2 + (hq % 2) * 8192, 8192)
                for quad in range(2):
                    for zc in range(16):
                        for tl in range(4):
                            tile = quad * 4 + tl
                            w = []
                            if zc == 0 and tl == 0:
                                w = [(f"LWd{hq % 2}", lg_sd[hq]), ("V", V_Z_END)]
                                pk = (hq, 0, 3) if quad == 1 else (hq - 1, 1, 3)
                                if pk in sd_evac:
                                    w.append(("V", sd_evac[pk]))
                            pb.emit(TEN, lambda e, zc=zc, tile=tile, tl=tl,
                                    wq=wsdq: e.matmul(
                                P[tl][:],
                                bbv(OFF_R3 + zc * 1024 + tile * 128, 128),
                                wq[:, zc * 512:(zc + 1) * 512],
                                start=(zc == 0), stop=(zc == 15)),
                                w, inc="T" if zc == 15 else None)
                    t_sd = pb.cnt["T"]
                    for tl in range(4):
                        tile = quad * 4 + tl
                        sd_evac[(hq, quad, tl)] = pb.emit(
                            VEC, lambda e, tile=tile, hq=hq, tl=tl: e.tensor_add(
                                fbv(OFF_ACC + tile * 2048 + hq * 512, 512),
                                fbv(OFF_ACC + tile * 2048 + hq * 512, 512),
                                P[tl][:]), [("T", t_sd)], inc="V")
                if hq + 2 < 4:
                    load_sd(hq + 2, [("T", pb.cnt["T"])])
            V_SD_END = pb.cnt["V"]
            DSO2 = None
            if dump and j == 0:
                DSO2 = dma_sync("SO", accd[1], fbv(OFF_ACC, 16384),
                                [("V", V_SD_END)])
                dump_barrier()

            ex_evac_last = V_SD_END
            prev_h_readers_T = pb.cnt["T"]
            for n in range(NE):
                h_done = {}
                lg_up = {}

                def load_up(ec, w, n=n):
                    lg_up[ec] = dma_cast(
                        f"LWa{ec % 2}",
                        bbv(OFF_R4 + (ec % 2) * 2048, 2048),
                        di["wup_r"][n][ec], w)

                w0 = [("T", prev_h_readers_T), ("V", ex_evac_last)]
                if n == 0 and DSO2:
                    w0.append(("SO", DSO2))
                load_up(0, w0)
                load_up(1, [])
                for ec in range(8):
                    bank = P[ec % 2]
                    slab = bbv(OFF_R4 + (ec % 2) * 2048, 2048)
                    s_prev_th = None
                    for th in range(2):
                        w = []
                        if th == 0:
                            w = [(f"LWa{ec % 2}", lg_up[ec])]
                            if ec == 0:
                                w.append(("V", ex_evac_last))
                            if ec >= 2:
                                w.append(("S", h_done[ec - 2]))
                        else:
                            w = [("S", s_prev_th)]
                        for fc in range(16):
                            pb.emit(TEN, lambda e, fc=fc, th=th, bank=bank,
                                    sl=slab: e.matmul(
                                bank[:], sl[:, fc * 128:(fc + 1) * 128],
                                xtt_rhs(fc, th), start=(fc == 0), stop=(fc == 15)),
                                w if fc == 0 else (), inc="T" if fc == 15 else None)
                        t_h = pb.cnt["T"]
                        s_prev_th = pb.emit(SCA, lambda e, n=n, ec=ec, th=th,
                                            bank=bank: e.activation(
                            bbv(OFF_R2 + ec * 1024 + th * 512, 512), bank[:],
                            ACT.Identity, bias=bup_sb[:, n * 8 + ec:n * 8 + ec + 1]),
                            [("T", t_h)], inc="S")
                    h_done[ec] = s_prev_th
                    if ec + 2 < 8:
                        load_up(ec + 2, [("T", pb.cnt["T"])])
                S_H_END = pb.cnt["S"]
                prev_h_readers_T = pb.cnt["T"]
                u_done = {}
                prev_umul = 0
                lg_sw = {}

                def load_sw(e2c, w, n=n):
                    lg_sw[e2c] = dma_cast(
                        f"LWb{e2c % 2}",
                        bbv(OFF_R4 + 4096 + (e2c % 2) * 1024, 1024),
                        di["wsw_r"][n][e2c], w)

                load_sw(0, [("V", ex_evac_last)])
                load_sw(1, [])
                for e2c in range(8):
                    bank = P[2 + (e2c % 2)]
                    slab = bbv(OFF_R4 + 4096 + (e2c % 2) * 1024, 1024)
                    v_prev_th = None
                    for th in range(2):
                        w = []
                        if th == 0:
                            w = [(f"LWb{e2c % 2}", lg_sw[e2c]), ("S", S_H_END)]
                            if e2c >= 2:
                                w.append(("V", u_done[e2c - 2]))
                        else:
                            w = [("V", v_prev_th)]
                        for ec in range(8):
                            pb.emit(TEN, lambda e, ec=ec, th=th, bank=bank,
                                    sl=slab: e.matmul(
                                bank[:], sl[:, ec * 128:(ec + 1) * 128],
                                bbv(OFF_R2 + ec * 1024 + th * 512, 512),
                                start=(ec == 0), stop=(ec == 7)),
                                w if ec == 0 else (), inc="T" if ec == 7 else None)
                        t_u = pb.cnt["T"]
                        ws = [("T", t_u)]
                        if prev_umul:
                            ws.append(("V", prev_umul))
                        s_si = pb.emit(SCA, lambda e, n=n, e2c=e2c, bank=bank:
                                       e.activation(
                            tmp_bf[:], bank[:], ACT.Silu,
                            bias=bsw_sb[:, n * 8 + e2c:n * 8 + e2c + 1]),
                            ws, inc="S")
                        v_prev_th = pb.emit(VEC, lambda e, e2c=e2c, th=th:
                                            e.tensor_mul(
                            bbv(OFF_R3 + e2c * 1024 + th * 512, 512), tmp_bf[:],
                            bbv(OFF_R2 + e2c * 1024 + th * 512, 512)),
                            [("S", s_si)], inc="V")
                        prev_umul = v_prev_th
                    u_done[e2c] = v_prev_th
                    if e2c + 2 < 8:
                        load_sw(e2c + 2, [("T", pb.cnt["T"])])
                V_U_END = pb.cnt["V"]
                dn_evac = {}
                lg_dn = {}

                def load_dn(hq, w, n=n):
                    lg_dn[hq] = dma_cast(
                        f"LWc{hq % 2}",
                        bbv(OFF_R2 + (hq % 2) * 4096, 4096),
                        di["wdn_r"][n][hq], w)

                load_dn(0, [("V", V_U_END)])
                load_dn(1, [("V", V_U_END)])
                for hq in range(4):
                    wdnq = bbv(OFF_R2 + (hq % 2) * 4096, 4096)
                    for quad in range(2):
                        for ec in range(8):
                            for tl in range(4):
                                tile = quad * 4 + tl
                                w = []
                                if ec == 0 and tl == 0:
                                    w = [(f"LWc{hq % 2}", lg_dn[hq]), ("V", V_U_END)]
                                    pk = (hq, 0, 3) if quad == 1 else (hq - 1, 1, 3)
                                    if pk in dn_evac:
                                        w.append(("V", dn_evac[pk]))
                                pb.emit(TEN, lambda e, ec=ec, tile=tile, tl=tl,
                                        wq=wdnq: e.matmul(
                                    P[tl][:],
                                    bbv(OFF_R3 + ec * 1024 + tile * 128, 128),
                                    wq[:, ec * 512:(ec + 1) * 512],
                                    start=(ec == 0), stop=(ec == 7)),
                                    w, inc="T" if ec == 7 else None)
                        t_dn = pb.cnt["T"]
                        for tl in range(4):
                            tile = quad * 4 + tl
                            dn_evac[(hq, quad, tl)] = pb.emit(
                                VEC, lambda e, tile=tile, hq=hq, n=n, tl=tl:
                                e.scalar_tensor_tensor(
                                    fbv(OFF_ACC + tile * 2048 + hq * 512, 512),
                                    P[tl][:],
                                    w_all[:, tile * 8 + n:tile * 8 + n + 1],
                                    fbv(OFF_ACC + tile * 2048 + hq * 512, 512),
                                    ALU.mult, ALU.add), [("T", t_dn)], inc="V")
                    if hq + 2 < 4:
                        load_dn(hq + 2, [("T", pb.cnt["T"])])
                ex_evac_last = pb.cnt["V"]
                prev_h_readers_T = pb.cnt["T"]

            DSO3 = None
            if dump and j == 0:
                DSO3 = dma_sync("SO", accd[2], fbv(OFF_ACC, 16384),
                                [("V", ex_evac_last)])
                dump_barrier()
            comb_v = {}
            comb_trans = {}
            comb_st = {}
            for tile in range(NT_OWN):
                xslot = fbv(OFF_XF + (tile % 2) * 2048, 2048)
                w = [("V", ex_evac_last), ("SO", D_SO_END)]
                if DSO3:
                    w.append(("SO", DSO3))
                if tile >= 2:
                    w.append(("V", comb_v[tile - 2]))
                    w.append(("SO", pb.cnt["SO"]))
                lh_res = dma_sync(f"LX{tile % 2}", xslot, res_scr.rearrange(
                    "(t p) h -> t p h", p=128)[tile], w)
                accap = fbv(OFF_ACC + tile * 2048, 2048)
                v_add = pb.emit(VEC, lambda e, a=accap, xs=xslot: e.tensor_add(
                    a, a, xs), [(f"LX{tile % 2}", lh_res)], inc="V")
                if j == 0:
                    xlslot = bbv(OFF_XLN_DE + (tile % 2) * 2048, 2048)
                    w = [("V", v_add)]
                    if tile >= 2:
                        w.append(("T", comb_trans[tile - 2]))
                    s_c = pb.emit(SCA, lambda e, a=accap, xl=xlslot:
                                  e.activation(xl, a, ACT.Copy), w, inc="S")
                    comb_v[tile] = emit_transpose16(
                        lambda fc, xl=xlslot: xl[:, fc * 128:(fc + 1) * 128],
                        OFF_XTT + tile * 2048, [("S", s_c)])
                    comb_trans[tile] = pb.cnt["T"]
                else:
                    lh_x2 = dma_sync(f"LX{tile % 2}", xslot, x2_scr.rearrange(
                        "(t p) h -> t p h", p=128)[tile], [("V", v_add)])
                    v_f = pb.emit(VEC, lambda e, a=accap, xs=xslot: e.tensor_add(
                        a, a, xs), [(f"LX{tile % 2}", lh_x2)], inc="V")
                    comb_v[tile] = v_f
                    comb_st[tile] = dma_sync(
                        "SO", out_own.rearrange("(t p) h -> t p h", p=128)[tile],
                        accap, [("V", v_f)])
            prev_iter_T = pb.cnt["T"]
            prev_iter_V = pb.cnt["V"]
            prev_iter_SO = pb.cnt["SO"]

    pb.wait("sync", [(s, pb.cnt[s]) for s in SYNC_SEMS if pb.cnt[s]])
    pb.wait("gpsimd", [(s, pb.cnt[s]) for s in GPS_SEMS if pb.cnt[s]])
    pb.run()
    return nc


def _own_tiles(i):
    return [4 * s + i if s < 4 else 4 * s + 3 - i for s in range(8)]


def _make_cmask(i):
    m = np.zeros((2, 128, 512), np.float32)
    tri = (np.arange(128)[:, None] <= np.arange(128)[None, :]).astype(np.float32)
    for side, r in enumerate((i, 3 - i)):
        for jj in range(4):
            blk = m[side, :, jj * 128:(jj + 1) * 128]
            if jj < r:
                blk[:] = 1.0
            elif jj == r:
                blk[:] = tri
    return m


def make_in_maps(x, w_qkv, w_o, Wg, bg, Wup, bup, Wsw, bsw, Wdn, bdn,
                 Wsu, bsu, Wsa, bsa, Wsd, bsd):
    f = lambda a: np.ascontiguousarray(np.asarray(a, dtype=np.float32))
    x = f(x); w_qkv = f(w_qkv)
    wqT = np.ascontiguousarray(w_qkv[0:H].T)
    wkT = np.ascontiguousarray(w_qkv[H:2 * H].T)
    wvT = np.ascontiguousarray(w_qkv[2 * H:3 * H].T)
    woT = np.ascontiguousarray(f(w_o).T)
    c = np.ascontiguousarray
    shared = dict(
        wq_r=c(wqT.reshape(16, 128, 16, 128).transpose(2, 1, 0, 3).reshape(16, 128, 2048)),
        wk_r=c(wkT.reshape(16, 128, 2048).transpose(1, 0, 2).reshape(128, 32768)),
        wv_r=c(wvT.reshape(16, 128, 2048).transpose(1, 0, 2).reshape(128, 32768)),
        wo_r=c(woT.reshape(16, 128, 4, 512).transpose(2, 1, 0, 3).reshape(4, 128, 8192)),
        wg_in=c(f(Wg).reshape(2, 16, 128, 8).transpose(2, 0, 1, 3).reshape(128, 256)),
        bg_in=f(bg).reshape(1, 16),
        wup_r=c(f(Wup).reshape(6, 16, 128, 8, 128).transpose(0, 3, 2, 1, 4).reshape(6, 8, 128, 2048)),
        wsw_r=c(f(Wsw).reshape(6, 8, 128, 8, 128).transpose(0, 3, 2, 1, 4).reshape(6, 8, 128, 1024)),
        wdn_r=c(f(Wdn).reshape(6, 8, 128, 4, 512).transpose(0, 3, 2, 1, 4).reshape(6, 4, 128, 4096)),
        wsu_r2=c(f(Wsu).reshape(16, 128, 16, 128).transpose(2, 1, 0, 3).reshape(16, 128, 2048)),
        wsa_r2=c(f(Wsa).reshape(8, 128, 1024).transpose(1, 0, 2).reshape(128, 8192)),
        wsd_r2=c(f(Wsd).reshape(16, 128, 4, 512).transpose(2, 1, 0, 3).reshape(4, 128, 8192)),
        bup_r=np.ascontiguousarray(
            f(bup).reshape(6, 8, 128).transpose(2, 0, 1).reshape(128, 48)),
        bsw_r=np.ascontiguousarray(
            f(bsw).reshape(6, 8, 128).transpose(2, 0, 1).reshape(128, 48)),
        bsu_r=np.ascontiguousarray(f(bsu).reshape(16, 128).T),
        bsa_r=np.ascontiguousarray(f(bsa).reshape(8, 128).T),
        bdnsd=np.ascontiguousarray(
            np.concatenate([f(bdn), f(bsd)[None]], 0)),
    )
    in_maps = []
    for c in range(8):
        b, i = c // 4, c % 4
        m = dict(shared)
        m["x_own"] = np.ascontiguousarray(
            x[b].reshape(NT_SEQ, 128, H)[_own_tiles(i)].reshape(1024, H))
        m["x_seq"] = x[b]
        m["cmask"] = _make_cmask(i)
        in_maps.append(m)
    return in_maps


def kernel(L0_attn_hidden_states, L0_attn_w_qkv, L0_attn_w_o,
           L0_ffn_Wg, L0_ffn_bg, L0_ffn_Wup, L0_ffn_bup, L0_ffn_Wsw, L0_ffn_bsw,
           L0_ffn_Wdn, L0_ffn_bdn, L0_ffn_Wsu, L0_ffn_bsu, L0_ffn_Wsa,
           L0_ffn_bsa, L0_ffn_Wsd, L0_ffn_bsd):
    in_maps = make_in_maps(L0_attn_hidden_states, L0_attn_w_qkv, L0_attn_w_o,
                           L0_ffn_Wg, L0_ffn_bg, L0_ffn_Wup, L0_ffn_bup,
                           L0_ffn_Wsw, L0_ffn_bsw, L0_ffn_Wdn, L0_ffn_bdn,
                           L0_ffn_Wsu, L0_ffn_bsu, L0_ffn_Wsa, L0_ffn_bsa,
                           L0_ffn_Wsd, L0_ffn_bsd)
    nc = build()
    res = run_bass_kernel_spmd(nc, in_maps, core_ids=list(range(8)))
    out = np.empty((2, SEQ, H), np.float32)
    for c in range(8):
        b, i = c // 4, c % 4
        oc = np.asarray(res.results[c]["out_own"]).reshape(NT_OWN, 128, H)
        for s, t in enumerate(_own_tiles(i)):
            out[b, t * 128:(t + 1) * 128, :] = oc[s]
    return out


# version 06
# speedup vs baseline: 1.0772x; 1.0029x over previous
import math
import numpy as np
import concourse.bass as bass
import concourse.mybir as mybir
from concourse.bass_utils import run_bass_kernel_spmd

F32 = mybir.dt.float32
BF16 = mybir.dt.bfloat16
AX = mybir.AxisListType.X
ALU = mybir.AluOpType
ACT = mybir.ActivationFunctionType

H = 2048
SEQ = 4096
NT_SEQ = 32
NT_OWN = 8
NE = 6
EX = 1024
C_SHIFT = 10.0
SCALE = float(np.float32(1.0) / np.sqrt(np.float32(128.0)))
EPS = 1e-5

CFG = dict(heads=16, dump=False, phases="ABCDEF")
ENGINES = ("sync", "tensor", "vector", "scalar", "gpsimd")


class PB:
    def __init__(self, nc):
        self.nc = nc
        self.q = {e: [] for e in ENGINES}
        self.sem = {}
        self.cnt = {}

    def new_sem(self, name):
        self.sem[name] = self.nc.alloc_semaphore(name)
        self.cnt[name] = 0

    def emit(self, eng, fn, waits=(), inc=None, inc_n=1):
        tgt = None
        if inc is not None:
            self.cnt[inc] += inc_n
            tgt = self.cnt[inc]
        self.q[eng].append((fn, tuple(waits), inc, inc_n))
        return tgt

    def wait(self, eng, waits):
        self.q[eng].append((None, tuple(waits), None, 0))

    def run(self):
        sems = self.sem

        def body(eng_name):
            def _(eng_obj):
                for fn, waits, inc, inc_n in self.q[eng_name]:
                    for s, v in waits:
                        if v and v > 0:
                            eng_obj.wait_ge(sems[s], v)
                    if fn is not None:
                        inst = fn(eng_obj)
                        if inc is not None:
                            inst.then_inc(sems[inc], inc_n)
            return _

        with self.nc.Block() as blk:
            blk.sync(body("sync"))
            blk.tensor(body("tensor"))
            blk.vector(body("vector"))
            blk.scalar(body("scalar"))
            blk.gpsimd(body("gpsimd"))


def build():
    heads = CFG["heads"]
    dump = CFG["dump"]
    phases = CFG["phases"]

    nc = bass.Bass("TRN2", target_bir_lowering=False,
                   detect_race_conditions=False,
                   dynamic_dma_scratch_size=16384)

    di = {}

    def inp(name, shape):
        di[name] = nc.dram_tensor(name, list(shape), F32, kind="ExternalInput").ap()

    inp("x_own", (1024, H))
    inp("x_seq", (SEQ, H))
    inp("wq_r", (16, 128, 2048)); inp("wk_r", (128, 32768)); inp("wv_r", (128, 32768))
    inp("wo_r", (4, 128, 8192))
    inp("wg_in", (128, 256)); inp("bg_in", (1, 16))
    inp("wup_r", (NE, 8, 128, 2048)); inp("wsw_r", (NE, 8, 128, 1024))
    inp("wdn_r", (NE, 4, 128, 4096))
    inp("wsu_r2", (16, 128, 2048)); inp("wsa_r2", (128, 8192))
    inp("wsd_r2", (4, 128, 8192))
    inp("bup_r", (128, 48)); inp("bsw_r", (128, 48)); inp("bsu_r", (128, 16))
    inp("bsa_r", (128, 8)); inp("bdnsd", (7, H))
    inp("cmask", (2, 128, 512))

    out_own = nc.dram_tensor("out_own", [1024, H], F32, kind="ExternalOutput").ap()
    skind = "ExternalOutput" if dump else "Internal"
    kT_scr = nc.dram_tensor("kT_scr", [H, SEQ], BF16, kind=skind).ap()
    vT_scr = nc.dram_tensor("vT_scr", [SEQ, H], BF16, kind=skind).ap()
    xlt_scr = nc.dram_tensor("xlt_scr", [NT_SEQ, 128, H], BF16, kind=skind).ap()
    res_scr = nc.dram_tensor("res_scr", [1024, H], F32, kind=skind).ap()
    x2_scr = nc.dram_tensor("x2_scr", [1024, H], F32, kind=skind).ap()
    if dump:
        q_dump = nc.dram_tensor("q_dump", [128, 16384], BF16, kind="ExternalOutput").ap()
        at_dump = nc.dram_tensor("at_dump", [128, 16384], BF16, kind="ExternalOutput").ap()
        accd = [nc.dram_tensor(f"acc_dump{k}", [128, 16384], F32,
                               kind="ExternalOutput").ap() for k in range(3)]
        w_dump = nc.dram_tensor("w_dump", [128, 64], F32, kind="ExternalOutput").ap()
        sh_dump = nc.dram_tensor("sh_dump", [128, 16384], BF16, kind="ExternalOutput").ap()
        z_dump = nc.dram_tensor("z_dump", [128, 16384], BF16, kind="ExternalOutput").ap()
        sml_dump = nc.dram_tensor("sml_dump", [128, 128], F32, kind="ExternalOutput").ap()
        x_dump = nc.dram_tensor("x_dump", [4, 128, 2048], F32, kind="ExternalOutput").ap()
        xltb_dump = nc.dram_tensor("xltb_dump", [128, 16384], BF16, kind="ExternalOutput").ap()
        id_dump = nc.dram_tensor("id_dump", [128, 256], F32, kind="ExternalOutput").ap()
        xtt_dump = nc.dram_tensor("xtt_dump", [128, 16384], BF16, kind="ExternalOutput").ap()
        id_dump2 = nc.dram_tensor("id_dump2", [128, 128], F32, kind="ExternalOutput").ap()
        x_dump_v = [x_dump[k] for k in range(4)]

    sb = nc.alloc_sbuf_tensor
    ident_bf = sb("ident_bf", [128, 128], BF16)
    ident_f32 = sb("ident_f32", [128, 128], F32)
    onescol_bf = sb("onescol_bf", [128, 1], BF16)
    onesrow_f32 = sb("onesrow_f32", [1, 128], F32)
    negC = sb("negC", [128, 1], F32)
    c1e30 = sb("c1e30", [128, 1], F32)
    c1e4 = sb("c1e4", [128, 1], F32)
    wg_bf = sb("wg_bf", [128, 256], BF16)
    bg_sb = sb("bg_sb", [1, 16], F32)
    bup_sb = sb("bup_sb", [128, 48], F32)
    bsw_sb = sb("bsw_sb", [128, 48], F32)
    bsu_sb = sb("bsu_sb", [128, 16], F32)
    bsa_sb = sb("bsa_sb", [128, 8], F32)
    bdnsd_sb = sb("bdnsd_sb", [8, H], BF16)
    wcomb = sb("wcomb", [8, 1024], BF16)
    w_all = sb("w_all", [128, 64], F32)
    srl = sb("srl", [1, 128], F32)
    srlb = sb("srlb", [128, 128], F32)
    sml = sb("sml", [128, 160], F32)
    smlsnap = sb("smlsnap", [128, 128], F32) if dump else None
    tmp_bf = sb("tmp_bf", [128, 512], BF16)

    NBF = 59392
    BB = sb("BB", [128, NBF], BF16)
    NF = 20480
    FB = sb("FB", [128, NF], F32)
    assert nc.sbuf_bytes_remaining >= 0

    def bbv(off, n):
        return BB[:, off:off + n]

    def fbv(off, n):
        return FB[:, off:off + n]

    OFF_WKT = 0
    OFF_XLTA = 32768
    OFF_XLTR = 40960
    OFF_KVO = 49152
    OFF_Q = 0
    OFF_XLTB = 16384
    OFF_ATT = 16384
    OFF_KVH = 32768
    OFF_PT = 50176
    OFF_CMASK = 54272
    OFF_WSLAB = 50176
    OFF_XTT = 0
    OFF_WOQ = 32768
    OFF_R2 = 16384
    OFF_R3 = 32768
    OFF_R4 = 49152

    OFF_XLN_AB = 45056
    OFF_XLN_DE = 53248
    OFF_ACC = 0
    OFF_XF = 16384
    OFF_XA4 = 8192

    P = [nc.alloc_psum_tensor(f"P{i}", [128, 512], F32) for i in range(7)]
    PB7 = nc.alloc_psum_tensor("PB7", [128, 1024], BF16)

    pb = PB(nc)
    SYNC_SEMS = ["LB_H", "LX0", "LX1", "LX2", "LX3", "LK0", "LK1",
                 "SKV0", "SKV1", "SX0", "SX1", "SX2", "SX3", "SO"]
    GPS_SEMS = ["LB_G", "LWa0", "LWa1", "LWb0", "LWb1", "LWc0", "LWc1",
                "LWd0", "LWd1"]
    for s in SYNC_SEMS + GPS_SEMS + ["T", "S", "V", "G"]:
        pb.new_sem(s)
    sem_nums = [h.num for h in pb.sem.values()]
    clear_range = range(min(sem_nums), max(sem_nums) + 1)
    if CFG.get("preamble", True):
        nc.gpsimd.sem_clear(clear_range)
        nc.all_engine_barrier()

    def dma_sync(sem, dst, src, waits=()):
        return pb.emit("sync", lambda e, d=dst, s=src: e.dma_start(out=d, in_=s),
                       waits, sem, 16)

    def dma_cast(sem, dst, src, waits=()):
        return pb.emit("gpsimd", lambda e, d=dst, s=src: e.dma_start(out=d, in_=s),
                       waits, sem, 16)

    def dump_barrier():
        v = pb.cnt["SO"]
        for eng in ENGINES:
            pb.wait(eng, [("SO", v)])

    TEN, VEC, SCA = "tensor", "vector", "scalar"
    GPS = "gpsimd"

    pb.emit(GPS, lambda e: e.memset(ident_bf[:], 0.0))
    pb.emit(GPS, lambda e: e.affine_select(
        out=ident_bf[:], in_=ident_bf[:], compare_op=ALU.not_equal, fill=1.0,
        base=0, pattern=[[-1, 128]], channel_multiplier=1))
    pb.emit(GPS, lambda e: e.memset(ident_f32[:], 0.0))
    pb.emit(GPS, lambda e: e.affine_select(
        out=ident_f32[:], in_=ident_f32[:], compare_op=ALU.not_equal, fill=1.0,
        base=0, pattern=[[-1, 128]], channel_multiplier=1))
    pb.emit(GPS, lambda e: e.memset(onescol_bf[:], 1.0))
    pb.emit(GPS, lambda e: e.memset(onesrow_f32[:], 1.0))
    pb.emit(GPS, lambda e: e.memset(negC[:], -C_SHIFT))
    pb.emit(GPS, lambda e: e.memset(sml[:, 144:160], 0.0))
    pb.emit(GPS, lambda e: e.memset(c1e30[:], 1.0e30))
    pb.emit(GPS, lambda e: e.memset(c1e4[:], 1.0e4))
    G_CONST = pb.emit(GPS, lambda e: e.memset(wcomb[:], 1.0), inc="G")

    cmask_bf = BB[:, OFF_CMASK:OFF_CMASK + 1024]
    dma_cast("LB_G", cmask_bf.rearrange("p (i j) -> p i j", i=2),
             di["cmask"].rearrange("i p j -> p i j"))
    LG_WGC = dma_cast("LB_G", wg_bf[:], di["wg_in"])
    LH_BIAS = 0
    for name, dst in (("bg_in", bg_sb[:]), ("bup_r", bup_sb[:]),
                      ("bsw_r", bsw_sb[:]), ("bsu_r", bsu_sb[:]),
                      ("bsa_r", bsa_sb[:])):
        LH_BIAS = dma_sync("LB_H", dst, di[name])
    LG_BDN = dma_cast("LB_G", bdnsd_sb[0:7, :], di["bdnsd"])

    wkT_sb = bbv(OFF_WKT, 32768)
    LG_WKT = 0
    if "A" in phases:
        for q4 in range(4):
            LG_WKT = dma_cast(
                "LB_G", bbv(OFF_WKT + q4 * 8192, 8192),
                di["wk_r"][:, q4 * 8192:(q4 + 1) * 8192])

    state = dict(pb7_free=0)

    LNDBG = dict(tile=None)
    fill_state = dict(k=0)

    def vfill(n=2):
        for _ in range(n):
            k = fill_state["k"] = (fill_state["k"] + 1) % 4
            pb.emit(VEC, lambda e, k=k: e.tensor_copy(
                sml[:, 151 + k:152 + k], sml[:, 150:151]))

    def emit_ln(xap, out_bf_ap, out_f32_ap, waits_v, par):
        o = par * 64
        for c in range(4):
            pb.emit(VEC, lambda e, c=c, o=o: e.bn_stats(
                sml[:, o + 6 * c:o + 6 * c + 6], xap[:, 512 * c:512 * (c + 1)]),
                waits_v if c == 0 else ())
        for c in range(4):
            pb.emit(VEC, lambda e, c=c, o=o: e.bn_aggr(
                sml[:, o + 24 + 2 * c:o + 26 + 2 * c], sml[:, o + 6 * c:o + 6 * c + 6]))
        mview = sml[:, o + 24:o + 32].rearrange("p (c t) -> p c t", t=2)[:, :, 0]
        vview = sml[:, o + 24:o + 32].rearrange("p (c t) -> p c t", t=2)[:, :, 1]
        vfill()
        pb.emit(VEC, lambda e: e.reduce_sum(sml[:, o + 32:o + 33], mview, AX))
        pb.emit(VEC, lambda e: e.tensor_mul(sml[:, o + 36:o + 40], mview, mview))
        pb.emit(VEC, lambda e: e.reduce_sum(sml[:, o + 41:o + 42], vview, AX))
        pb.emit(VEC, lambda e: e.tensor_scalar_mul(
            sml[:, o + 33:o + 34], sml[:, o + 32:o + 33], -0.25))
        vfill(1)
        pb.emit(VEC, lambda e: e.reduce_sum(
            sml[:, o + 40:o + 41], sml[:, o + 36:o + 40], AX))
        pb.emit(VEC, lambda e: e.tensor_mul(
            sml[:, o + 44:o + 45], sml[:, o + 33:o + 34], sml[:, o + 33:o + 34]))
        vfill(1)
        pb.emit(VEC, lambda e: e.tensor_add(
            sml[:, o + 42:o + 43], sml[:, o + 40:o + 41], sml[:, o + 41:o + 42]))
        vfill(2)
        pb.emit(VEC, lambda e: e.tensor_scalar_mul(
            sml[:, o + 43:o + 44], sml[:, o + 42:o + 43], 0.25))
        vfill(2)
        pb.emit(VEC, lambda e: e.tensor_sub(
            sml[:, o + 45:o + 46], sml[:, o + 43:o + 44], sml[:, o + 44:o + 45]))
        vfill(2)
        v_stats = pb.emit(VEC, lambda e: e.tensor_scalar_add(
            sml[:, o + 46:o + 47], sml[:, o + 45:o + 46], EPS), inc="V")
        pb.emit(SCA, lambda e: e.activation(
            sml[:, o + 47:o + 48], sml[:, o + 46:o + 47], ACT.Ln), [("V", v_stats)])
        s_r = pb.emit(SCA, lambda e: e.activation(
            sml[:, o + 48:o + 49], sml[:, o + 47:o + 48], ACT.Exp, scale=-0.5),
            inc="S")
        v_negmr = pb.emit(VEC, lambda e: e.tensor_mul(
            sml[:, o + 49:o + 50], sml[:, o + 33:o + 34],
            sml[:, o + 48:o + 49]), [("S", s_r)], inc="V")
        if dump and LNDBG["tile"] is not None:
            tdbg = LNDBG["tile"]
            vfill(2)
            pb.emit(VEC, lambda e, tdbg=tdbg: e.tensor_copy(
                smlsnap[:, tdbg * 4:tdbg * 4 + 2], sml[:, o + 48:o + 50]))
            pb.emit(VEC, lambda e, tdbg=tdbg: e.tensor_copy(
                smlsnap[:, tdbg * 4 + 2:tdbg * 4 + 4], sml[:, o + 45:o + 47]))
            if 2 <= tdbg <= 5:
                dma_sync("SO", x_dump_v[tdbg - 2], xap, [("V", pb.cnt["V"])])
            LNDBG["tile"] = None
        s_cast = pb.emit(SCA, lambda e: e.activation(
            out_bf_ap, xap, ACT.Identity, bias=sml[:, o + 49:o + 50],
            scale=sml[:, o + 48:o + 49]), [("V", v_negmr)], inc="S")
        if out_f32_ap is not None:
            s_cast = pb.emit(SCA, lambda e: e.activation(
                out_f32_ap, xap, ACT.Identity, bias=sml[:, o + 49:o + 50],
                scale=sml[:, o + 48:o + 49]), inc="S")
        return s_cast

    def emit_transpose16(src_fn, dst_off, waits_t):
        vlast = None
        for r in range(2):
            for k in range(8):
                fc = r * 8 + k
                w = []
                if k == 0:
                    w = list(waits_t) if r == 0 else []
                    w.append(("V", state["pb7_free"] if r == 0 else vlast))
                pb.emit(TEN, lambda e, fc=fc, k=k: e.transpose(
                    PB7[:, k * 128:(k + 1) * 128], src_fn(fc), ident_bf[:]),
                    w, inc="T" if k == 7 else None)
            t_done = pb.cnt["T"]
            vlast = pb.emit(VEC, lambda e, r=r: e.tensor_copy(
                bbv(dst_off + r * 1024, 1024), PB7[:]), [("T", t_done)], inc="V")
        state["pb7_free"] = vlast
        return vlast

    ST_XLT = {}
    A1_T_END = A2_T_END = 0
    if "A" in phases:
        xcast = {}
        ttrans = {}
        vcopy = {}
        kevac = {}
        kst = {}
        kmm_last = {}
        A_ST = CFG.get("a_st", 8)
        for st in range(A_ST):
            for tt in range(4):
                t = 4 * st + tt
                xslot = fbv(OFF_XA4 + (t % 4) * 2048, 2048)
                w = [("S", xcast[t - 4])] if t >= 4 else []
                if dump and t >= 4:
                    w.append(("SO", pb.cnt["SO"]))
                lh_x = dma_sync(f"LX{t % 4}", xslot, di["x_seq"].rearrange(
                    "(t p) h -> t p h", p=128)[t], w)
                wv = [(f"LX{t % 4}", lh_x)]
                if t >= 2:
                    wv.append(("T", ttrans[t - 2]))
                    wv.append(("S", xcast[t - 2]))
                xlslot = bbv(OFF_XLN_AB + (t % 2) * 2048, 2048)
                LNDBG["tile"] = t
                xcast[t] = emit_ln(xslot, xlslot, None, wv, t % 2)
                wt = [("S", xcast[t]), ("G", G_CONST)]
                if st >= 1:
                    wt.append((f"SX{tt}", ST_XLT[4 * (st - 1) + tt]))
                vcopy[t] = emit_transpose16(
                    lambda fc, xl=xlslot: xl[:, fc * 128:(fc + 1) * 128],
                    OFF_XLTA + tt * 2048, wt)
                ttrans[t] = pb.cnt["T"]
                ST_XLT[t] = dma_sync(f"SX{tt}", xlt_scr[t],
                                     bbv(OFF_XLTA + tt * 2048, 2048),
                                     [("V", vcopy[t])])
            xlt_view = bbv(OFF_XLTA, 8192).rearrange(
                "p (tile fc tl) -> p tile fc tl", tile=4, fc=16)
            for dc in range(16):
                bank = P[dc % 2]
                if dc < 2:
                    w = [("LB_G", LG_WKT), ("V", vcopy[4 * st + 3])]
                else:
                    w = [("V", kevac[(st, dc - 2)])]
                for fc in range(16):
                    pb.emit(TEN, lambda e, dc=dc, fc=fc, bank=bank, xv=xlt_view:
                            e.matmul(
                        bank[:], wkT_sb[:, fc * 2048 + dc * 128:fc * 2048 + (dc + 1) * 128],
                        xv[:, :, fc, :], start=(fc == 0), stop=(fc == 15)),
                        w if fc == 0 else (), inc="T" if fc == 15 else None)
                kmm = pb.cnt["T"]
                kmm_last[st] = kmm
                kv_slot = bbv(OFF_KVO + (dc % 2) * 512, 512)
                w2 = [("T", kmm)]
                prevkey = (st, dc - 2) if dc >= 2 else (st - 1, dc + 14)
                if prevkey in kst:
                    w2.append((f"SKV{dc % 2}", kst[prevkey]))
                kevac[(st, dc)] = pb.emit(
                    VEC, lambda e, bank=bank, kv=kv_slot: e.tensor_copy(kv, bank[:]),
                    w2, inc="V")
                kst[(st, dc)] = dma_sync(
                    f"SKV{dc % 2}",
                    kT_scr[dc * 128:(dc + 1) * 128, st * 512:(st + 1) * 512],
                    kv_slot, [("V", kevac[(st, dc)])])
        if dump:
            dma_sync("SO", sml_dump, smlsnap[:], [("V", pb.cnt["V"])])
            dump_barrier()
        A1_T_END = pb.cnt["T"]

        LG_WVT = 0
        pb.wait(GPS, [("T", A1_T_END)])
        for q4 in range(4):
            LG_WVT = dma_cast(
                "LB_G", bbv(OFF_WKT + q4 * 8192, 8192),
                di["wv_r"][:, q4 * 8192:(q4 + 1) * 8192])
        wvT_sb = bbv(OFF_WKT, 32768)
        vevac = {}
        vst = {}
        vmm_last = {}
        for t in range(4 * A_ST):
            slot = bbv(OFF_XLTR + (t % 2) * 2048, 2048)
            w = [(f"SX{t % 4}", ST_XLT[t])]
            if t >= 2:
                w.append(("T", vmm_last[t - 2]))
            xltld = dma_sync(f"LX{t % 2}", slot, xlt_scr[t], w)
            for dvg in range(4):
                bank = P[dvg % 2]
                if dvg < 2:
                    w = [(f"LX{t % 2}", xltld), ("LB_G", LG_WVT)]
                    prevkey = (t - 1, dvg + 2)
                else:
                    w = []
                    prevkey = (t, dvg - 2)
                if prevkey in vevac:
                    w.append(("V", vevac[prevkey]))
                for fc in range(16):
                    pb.emit(TEN, lambda e, fc=fc, dvg=dvg, bank=bank, sl=slot:
                            e.matmul(
                        bank[:], sl[:, fc * 128:(fc + 1) * 128],
                        wvT_sb[:, fc * 2048 + dvg * 512:fc * 2048 + (dvg + 1) * 512],
                        start=(fc == 0), stop=(fc == 15)),
                        w if fc == 0 else (), inc="T" if fc == 15 else None)
                vmm = pb.cnt["T"]
                vmm_last[t] = vmm
                kv_slot = bbv(OFF_KVO + (dvg % 2) * 512, 512)
                w2 = [("T", vmm)]
                prevkey = (t, dvg - 2) if dvg >= 2 else (t - 1, dvg + 2)
                if prevkey in vst:
                    w2.append((f"SKV{dvg % 2}", vst[prevkey]))
                vevac[(t, dvg)] = pb.emit(
                    VEC, lambda e, bank=bank, kv=kv_slot: e.tensor_copy(kv, bank[:]),
                    w2, inc="V")
                vst[(t, dvg)] = dma_sync(
                    f"SKV{dvg % 2}",
                    vT_scr[t * 128:(t + 1) * 128, dvg * 512:(dvg + 1) * 512],
                    kv_slot, [("V", vevac[(t, dvg)])])
        A2_T_END = pb.cnt["T"]

    B_V_END = 0
    if "B" in phases:
        if dump:
            _vid = pb.emit(VEC, lambda e: e.tensor_copy(
                srlb[:], ident_bf[:]), [("T", A2_T_END)], inc="V")
            dma_sync("SO", id_dump[:, 0:128], srlb[:], [("V", _vid)])
            dump_barrier()
        bcast = {}
        btrans = {}
        for ot in range(NT_OWN):
            xslot = fbv(OFF_XF + (ot % 2) * 2048, 2048)
            w = [("S", bcast[ot - 2])] if ot >= 2 else []
            lh_x = dma_sync(f"LX{ot % 2}", xslot, di["x_own"].rearrange(
                "(t p) h -> t p h", p=128)[ot], w)
            wv = [(f"LX{ot % 2}", lh_x)]
            if ot == 0:
                wv.append(("T", A2_T_END))
            if ot >= 2:
                wv.append(("T", btrans[ot - 2]))
                wv.append(("S", bcast[ot - 2]))
            xlslot = bbv(OFF_XLN_AB + (ot % 2) * 2048, 2048)
            if dump and "A" not in phases:
                LNDBG["tile"] = ot
            bcast[ot] = emit_ln(xslot, xlslot, None, wv, ot % 2)
            wt = [("S", bcast[ot]), ("G", G_CONST), ("T", A2_T_END)]
            emit_transpose16(lambda fc, xl=xlslot: xl[:, fc * 128:(fc + 1) * 128],
                             OFF_XLTB + ot * 2048, wt)
            btrans[ot] = pb.cnt["T"]
        v_xltb = pb.cnt["V"]
        XLTB_SO = None
        if dump:
            XLTB_SO = dma_sync("SO", xltb_dump, bbv(OFF_XLTB, 16384),
                               [("V", v_xltb)])
            dump_barrier()
        skip_q = bool(CFG.get("noq"))
        xltb_view = bbv(OFF_XLTB, 16384).rearrange(
            "p (tile fc tl) -> p tile fc tl", tile=8, fc=16)
        qevac = {}
        qmm_done = {}
        lg_wq = {}

        def load_wq(dc, w):
            lg_wq[dc] = dma_cast(
                f"LWd{dc % 2}",
                bbv(OFF_WSLAB + (dc % 2) * 2048, 2048), di["wq_r"][dc], w)

        if not skip_q:
            wq0w = [("T", A2_T_END)]
            if CFG.get("serial_wq"):
                wq0w.append(("V", v_xltb))
            load_wq(0, wq0w)
            load_wq(1, [])
        for dc in range(16 if not skip_q else 0):
            slab = bbv(OFF_WSLAB + (dc % 2) * 2048, 2048)
            for tq in range(2):
                bank = P[tq]
                w = []
                if tq == 0:
                    w = [(f"LWd{dc % 2}", lg_wq[dc]), ("V", v_xltb)]
                if (dc - 1, tq) in qevac:
                    w.append(("V", qevac[(dc - 1, tq)]))
                for fc in range(16):
                    pb.emit(TEN, lambda e, fc=fc, tq=tq, bank=bank, sl=slab: e.matmul(
                        bank[:], sl[:, fc * 128:(fc + 1) * 128],
                        xltb_view[:, tq * 4:(tq + 1) * 4, fc, :],
                        start=(fc == 0), stop=(fc == 15)),
                        w if fc == 0 else (), inc="T" if fc == 15 else None)
                qmm_done[dc] = pb.cnt["T"]
                qevac[(dc, tq)] = pb.emit(
                    VEC, lambda e, dc=dc, tq=tq, bank=bank: e.tensor_copy(
                        bbv(OFF_Q + dc * 1024 + tq * 512, 512), bank[:]),
                    [("T", qmm_done[dc])], inc="V")
            if dc + 2 < 16:
                load_wq(dc + 2, [("T", qmm_done[dc])])
        if dump and "A" not in phases:
            dma_sync("SO", sml_dump, smlsnap[:], [("V", pb.cnt["V"])])
        B_V_END = pb.cnt["V"]
        QDUMP_SO = None
        if dump:
            QDUMP_SO = dma_sync("SO", q_dump, bbv(OFF_Q, 16384), [("V", B_V_END)])
            dump_barrier()

    C_V_END = C_T_END = 0
    if "C" in phases:
        kvload = {}
        head_done = {}
        prev_evac = B_V_END
        prev_l_user = 0
        prev_rlb_user = 0
        bank_free_S = {0: 0, 1: 0}
        for h in range(heads):
            buf = h % 2
            koff = OFF_KVH + buf * 8192
            voff = OFF_KVH + buf * 8192 + 4096
            w = [("V", B_V_END)] if h < 2 else [("T", head_done[h - 2])]
            dma_sync(f"LK{h % 2}", bbv(koff, 4096),
                     kT_scr[h * 128:(h + 1) * 128, :], w)
            for vh in range(2):
                kvload[h] = dma_sync(
                    f"LK{h % 2}",
                    bbv(voff + vh * 2048, 2048).rearrange("p (kt d) -> p kt d", kt=16),
                    vT_scr.rearrange("(kt p) d -> p kt d", p=128)[
                        :, vh * 16:(vh + 1) * 16, h * 128:(h + 1) * 128])
            for s in range(NT_OWN):
                ngr = s + 1
                exp_done = {}
                av_done = {}
                smm = {}

                def emit_st(g):
                    bank = P[g % 2]
                    w = []
                    if g == 0:
                        w = [(f"LK{h % 2}", kvload[h])]
                        if h == 0 and s == 0:
                            w.append(("G", G_CONST))
                    if g >= 2:
                        w.append(("S", exp_done[g - 2]))
                    elif bank_free_S[g % 2]:
                        w.append(("S", bank_free_S[g % 2]))
                    for jj in range(4):
                        kt = 4 * g + jj
                        pb.emit(TEN, lambda e, kt=kt, jj=jj, bank=bank, ko=koff,
                                h=h, s=s: e.matmul(
                            bank[:, jj * 128:(jj + 1) * 128],
                            bbv(ko + kt * 128, 128),
                            bbv(OFF_Q + h * 1024 + s * 128, 128),
                            start=True, stop=True),
                            w if jj == 0 else (), inc="T" if jj == 3 else None)
                    smm[g] = pb.cnt["T"]

                emit_st(0)
                for g in range(ngr):
                    bank = P[g % 2]
                    if g + 1 < ngr:
                        emit_st(g + 1)
                    wexp = [("T", smm[g])]
                    if g >= 2:
                        wexp.append(("T", av_done[g - 2]))
                    ptap = bbv(OFF_PT + (g % 2) * 512, 512)
                    exp_done[g] = pb.emit(SCA, lambda e, bank=bank, pt=ptap:
                                          e.activation(
                        pt, bank[:], ACT.Exp, bias=negC[:, 0:1], scale=SCALE),
                        wexp, inc="S")
                    bank_free_S[g % 2] = exp_done[g]
                    gate_sem = ("S", exp_done[g])
                    if g == s:
                        side = 0 if s < 4 else 1
                        vm = pb.emit(VEC, lambda e, pt=ptap, side=side: e.tensor_mul(
                            pt, pt, BB[:, OFF_CMASK + side * 512:
                                       OFF_CMASK + (side + 1) * 512]),
                            [("S", exp_done[g]), ("LB_G", LG_WGC)], inc="V")
                        gate_sem = ("V", vm)
                    w = [gate_sem]
                    if g == 0:
                        w.append(("V", prev_evac))
                        if prev_l_user:
                            w.append(("V", prev_l_user))
                    first = (g == 0)
                    last = (g == ngr - 1)
                    for jj in range(4):
                        kt = 4 * g + jj
                        pb.emit(TEN, lambda e, kt=kt, jj=jj, pt=ptap, vo=voff,
                                fi=first, la=last: e.matmul(
                            P[2][:, 0:128], bbv(vo + kt * 128, 128),
                            pt[:, jj * 128:(jj + 1) * 128],
                            start=(fi and jj == 0), stop=(la and jj == 3)),
                            w if jj == 0 else ())
                    for jj in range(4):
                        pb.emit(TEN, lambda e, jj=jj, pt=ptap, fi=first, la=last:
                                e.matmul(
                            P[3][0:1, 0:128], onescol_bf[:],
                            pt[:, jj * 128:(jj + 1) * 128],
                            start=(fi and jj == 0), stop=(la and jj == 3)),
                            (), inc="T" if jj == 3 else None)
                    av_done[g] = pb.cnt["T"]
                v_rl = pb.emit(VEC, lambda e: e.reciprocal(srl[:], P[3][0:1, 0:128]),
                               [("T", av_done[ngr - 1])], inc="V")
                prev_l_user = v_rl
                w = [("V", v_rl)]
                if prev_rlb_user:
                    w.append(("V", prev_rlb_user))
                t_rlb = pb.emit(TEN, lambda e: e.matmul(
                    P[4][:, 0:128], onesrow_f32[:], srl[:], start=True, stop=True),
                    w, inc="T")
                v_cp = pb.emit(VEC, lambda e: e.tensor_copy(srlb[:], P[4][:, 0:128]),
                               [("T", t_rlb)], inc="V")
                vfill(2)
                wev = [("V", v_cp)]
                if h == 0 and s == 0 and dump and "B" in phases and XLTB_SO:
                    wev.append(("SO", XLTB_SO))
                prev_evac = pb.emit(VEC, lambda e, h=h, s=s: e.tensor_mul(
                    bbv(OFF_ATT + h * 1024 + s * 128, 128), P[2][:, 0:128], srlb[:]),
                    wev, inc="V")
                prev_rlb_user = prev_evac
            head_done[h] = pb.cnt["T"]
        C_V_END = pb.cnt["V"]
        C_T_END = pb.cnt["T"]
        ATD_SO = None
        if dump:
            ATD_SO = dma_sync("SO", at_dump, bbv(OFF_ATT, 16384), [("V", C_V_END)])
            dump_barrier()

    D_T_END = D_V_END = D_SO_END = 0
    if "D" in phases:
        att_view = bbv(OFF_ATT, 16384)
        d_evac = {}
        oq_done = {}
        lg_woq = {}

        def load_woq(oq, w):
            lg_woq[oq] = dma_cast(
                f"LWd{oq % 2}",
                bbv(OFF_WOQ + (oq % 2) * 8192, 8192), di["wo_r"][oq], w)

        if dump:
            _vid3 = pb.emit(VEC, lambda e: e.tensor_copy(
                srlb[:], ident_bf[:]), [("T", pb.cnt["T"]), ("V", pb.cnt["V"])],
                inc="V")
            dma_sync("SO", id_dump2, srlb[:], [("V", _vid3)])
            dump_barrier()
        load_woq(0, [("T", C_T_END)])
        load_woq(1, [])
        for oq in range(4 if not CFG.get("skip_dmm") else 0):
            woq = bbv(OFF_WOQ + (oq % 2) * 8192, 8192)
            for quad in range(2):
                for dc in range(16):
                    for tl in range(4):
                        tile = quad * 4 + tl
                        w = []
                        if dc == 0 and tl == 0:
                            w = [(f"LWd{oq % 2}", lg_woq[oq]), ("V", C_V_END)]
                            pk = (oq, 0, 3) if quad == 1 else (oq - 1, 1, 3)
                            if pk in d_evac:
                                w.append(("V", d_evac[pk]))
                        pb.emit(TEN, lambda e, dc=dc, tile=tile, tl=tl, wq=woq:
                                e.matmul(
                            P[tl][:], att_view[:, dc * 1024 + tile * 128:
                                               dc * 1024 + (tile + 1) * 128],
                            wq[:, dc * 512:(dc + 1) * 512],
                            start=(dc == 0), stop=(dc == 15)),
                            w, inc="T" if dc == 15 else None)
                dmm = pb.cnt["T"]
                oq_done[oq] = dmm
                for tl in range(4):
                    tile = quad * 4 + tl
                    if CFG.get("evac_alt"):
                        d_evac[(oq, quad, tl)] = pb.emit(
                            VEC, lambda e, tl=tl: e.tensor_copy(
                                srlb[:], P[tl][:, 0:128]),
                            [("T", dmm)], inc="V")
                    else:
                        d_evac[(oq, quad, tl)] = pb.emit(
                            VEC, lambda e, tile=tile, oq=oq, tl=tl: e.tensor_copy(
                                fbv(OFF_ACC + tile * 2048 + oq * 512, 512), P[tl][:]),
                            [("T", dmm)], inc="V")
            if oq + 2 < 4:
                load_woq(oq + 2, [("T", oq_done[oq])])
        if dump:
            _vid2 = pb.emit(VEC, lambda e: e.tensor_copy(
                srlb[:], ident_bf[:]), [("T", pb.cnt["T"]), ("V", pb.cnt["V"])],
                inc="V")
            dma_sync("SO", id_dump[:, 128:256], srlb[:], [("V", _vid2)])
            dump_barrier()
        dcast = {}
        dtrans = {}
        st_res = {}
        for tile in range(NT_OWN if not CFG.get("skip_dmm") else 0):
            xslot = fbv(OFF_XF + (tile % 2) * 2048, 2048)
            w = [("V", d_evac[(3, tile // 4, tile % 4)])]
            if tile >= 2:
                w.append(("S", dcast[tile - 2]))
                w.append(("SO", pb.cnt["SO"]))
            lh_x = dma_sync(f"LX{tile % 2}", xslot, di["x_own"].rearrange(
                "(t p) h -> t p h", p=128)[tile], w)
            x2ap = fbv(OFF_ACC + tile * 2048, 2048)
            v_x2 = pb.emit(VEC, lambda e, a=x2ap, xs=xslot: e.tensor_add(a, a, xs),
                           [(f"LX{tile % 2}", lh_x)], inc="V")
            dma_sync("SO", x2_scr.rearrange("(t p) h -> t p h", p=128)[tile], x2ap,
                     [("V", v_x2)])
            xlslot = bbv(OFF_XLN_DE + (tile % 2) * 2048, 2048)
            wv = [("V", v_x2)]
            if tile >= 2:
                wv.append(("T", dtrans[tile - 2]))
                wv.append(("S", dcast[tile - 2]))
            dcast[tile] = emit_ln(x2ap, xlslot, xslot, wv, tile % 2)
            st_res[tile] = dma_sync(
                "SO", res_scr.rearrange("(t p) h -> t p h", p=128)[tile], xslot,
                [("S", dcast[tile])])
            wtr = [("S", dcast[tile])]
            if tile == 0 and dump and "B" in phases and QDUMP_SO:
                wtr.append(("SO", QDUMP_SO))
            emit_transpose16(lambda fc, xl=xlslot: xl[:, fc * 128:(fc + 1) * 128],
                             OFF_XTT + tile * 2048, wtr)
            dtrans[tile] = pb.cnt["T"]
        if dump and not CFG.get("skip_dmm"):
            dma_sync("SO", xtt_dump, bbv(OFF_XTT, 16384),
                     [("T", pb.cnt["T"]), ("V", pb.cnt["V"])])
            dump_barrier()
        D_T_END = pb.cnt["T"]
        D_V_END = pb.cnt["V"]
        D_SO_END = pb.cnt["SO"]

    if "E" in phases:
        xtt_view = bbv(OFF_XTT, 16384).rearrange(
            "p (tile fc tl) -> p tile fc tl", tile=8, fc=16)

        def xtt_rhs(fc, th):
            return xtt_view[:, th * 4:(th + 1) * 4, fc, :]

        prev_iter_T = D_T_END
        prev_iter_V = D_V_END
        prev_iter_SO = D_SO_END
        for j in range(2):
            w_gate_done = {}
            for tile in range(NT_OWN):
                if tile == 0:
                    w = [("T", prev_iter_T), ("V", prev_iter_V), ("LB_G", LG_WGC)]
                else:
                    w = [("V", w_gate_done[tile - 1])]
                for fc in range(16):
                    pb.emit(TEN, lambda e, fc=fc, tile=tile, j=j: e.matmul(
                        P[4][:, 0:8], xtt_view[:, tile, fc, :],
                        wg_bf[:, j * 128 + fc * 8:j * 128 + (fc + 1) * 8],
                        start=(fc == 0), stop=False),
                        w if fc == 0 else ())
                t_g = pb.emit(TEN, lambda e, j=j: e.matmul(
                    P[4][:, 0:8], onesrow_f32[:], bg_sb[0:1, j * 8:(j + 1) * 8],
                    start=False, stop=True), [("LB_H", LH_BIAS)], inc="T")
                o = 100
                pb.emit(VEC, lambda e: e.tensor_copy(sml[:, o:o + 8], P[4][:, 0:8]),
                        [("T", t_g)])
                vfill(2)
                pb.emit(VEC, lambda e: e.tensor_reduce(
                    sml[:, o + 8:o + 9], sml[:, o:o + 8], AX, ALU.min))
                vfill(2)
                pb.emit(VEC, lambda e: e.tensor_scalar(
                    sml[:, o + 10:o + 18], sml[:, o:o + 8], sml[:, o + 8:o + 9],
                    None, ALU.is_equal))
                vfill(2)
                pb.emit(VEC, lambda e: e.scalar_tensor_tensor(
                    sml[:, o + 18:o + 26], sml[:, o + 10:o + 18], c1e30[:, 0:1],
                    sml[:, o:o + 8], ALU.mult, ALU.add))
                vfill(2)
                pb.emit(VEC, lambda e: e.tensor_reduce(
                    sml[:, o + 9:o + 10], sml[:, o + 18:o + 26], AX, ALU.min))
                vfill(2)
                pb.emit(VEC, lambda e: e.tensor_scalar(
                    sml[:, o + 10:o + 18], sml[:, o:o + 8], sml[:, o + 9:o + 10],
                    None, ALU.is_gt))
                vfill(2)
                pb.emit(VEC, lambda e: e.scalar_tensor_tensor(
                    sml[:, o + 18:o + 26], sml[:, o:o + 8], c1e4[:, 0:1],
                    sml[:, o + 10:o + 18], ALU.add, ALU.mult))
                vfill(2)
                v_m = pb.emit(VEC, lambda e: e.reduce_max(
                    sml[:, o + 8:o + 9], sml[:, o + 18:o + 26], AX, negate=True),
                    inc="V")
                s_e = pb.emit(SCA, lambda e: e.activation(
                    sml[:, o + 26:o + 34], sml[:, o + 18:o + 26], ACT.Exp,
                    bias=sml[:, o + 8:o + 9]), [("V", v_m)], inc="S")
                pb.emit(VEC, lambda e: e.reduce_sum(
                    sml[:, o + 9:o + 10], sml[:, o + 26:o + 34], AX), [("S", s_e)])
                vfill(2)
                pb.emit(VEC, lambda e: e.reciprocal(
                    sml[:, o + 8:o + 9], sml[:, o + 9:o + 10]))
                vfill(2)
                v_w = pb.emit(VEC, lambda e, tile=tile: e.tensor_scalar_mul(
                    w_all[:, tile * 8:(tile + 1) * 8], sml[:, o + 26:o + 34],
                    sml[:, o + 8:o + 9]), inc="V")
                t_wt = pb.emit(TEN, lambda e, tile=tile: e.transpose(
                    P[5][0:8, 0:128], w_all[:, tile * 8:(tile + 1) * 8],
                    ident_f32[:]), [("V", v_w)], inc="T")
                w_gate_done[tile] = pb.emit(VEC, lambda e, tile=tile: e.tensor_copy(
                    wcomb[0:6, tile * 128:(tile + 1) * 128], P[5][0:6, 0:128]),
                    [("T", t_wt)], inc="V")
            acc_init_done = {}
            prev_ai = None
            for tile in range(NT_OWN):
                for hc in range(4):
                    w = [("V", w_gate_done[tile])]
                    if tile == 0 and hc == 0:
                        w.append(("SO", prev_iter_SO))
                        w.append(("LB_G", LG_BDN))
                    if prev_ai:
                        w.append(("V", prev_ai))
                    t_ai = pb.emit(TEN, lambda e, tile=tile, hc=hc: e.matmul(
                        P[6][:], wcomb[0:7, tile * 128:(tile + 1) * 128],
                        bdnsd_sb[0:7, hc * 512:(hc + 1) * 512],
                        start=True, stop=True), w, inc="T")
                    prev_ai = pb.emit(
                        VEC, lambda e, tile=tile, hc=hc: e.tensor_copy(
                            fbv(OFF_ACC + tile * 2048 + hc * 512, 512), P[6][:]),
                        [("T", t_ai)], inc="V")
            V_ACCINIT = prev_ai
            DSO = None
            if dump and j == 0:
                dma_sync("SO", w_dump, w_all[:], [("V", V_ACCINIT)])
                DSO = dma_sync("SO", accd[0], fbv(OFF_ACC, 16384),
                               [("V", V_ACCINIT)])
                dump_barrier()

            sh_done = {}
            lg_su = {}

            def load_su(oc_h, w):
                oc, hh = oc_h
                lg_su[oc_h] = dma_cast(
                    f"LWd{hh}",
                    bbv(OFF_R4 + hh * 1024, 1024),
                    di["wsu_r2"][oc][:, hh * 1024:(hh + 1) * 1024], w)

            wsu0 = [("T", prev_iter_T)]
            if dump and "C" in phases and ATD_SO:
                wsu0.append(("SO", ATD_SO))
            load_su((0, 0), wsu0)
            load_su((0, 1), [])
            lg_sa = dma_cast(
                "LB_G", bbv(OFF_R4 + 2048, 8192), di["wsa_r2"],
                [("T", prev_iter_T)])
            for oc in range(16):
                bank = P[oc % 2]
                s_prev_th = None
                for th in range(2):
                    w = []
                    if th == 0:
                        w = [("LWd0", lg_su[(oc, 0)]), ("LWd1", lg_su[(oc, 1)]),
                             ("V", V_ACCINIT)]
                        if oc >= 1:
                            w.append(("S", sh_done[oc - 1]))
                    else:
                        w = [("S", s_prev_th)]
                    for fc in range(16):
                        sl = bbv(OFF_R4 + (fc // 8) * 1024, 1024)
                        pb.emit(TEN, lambda e, fc=fc, th=th, bank=bank, sl=sl:
                                e.matmul(
                            bank[:], sl[:, (fc % 8) * 128:(fc % 8 + 1) * 128],
                            xtt_rhs(fc, th), start=(fc == 0), stop=(fc == 15)),
                            w if fc == 0 else (), inc="T" if fc == 15 else None)
                    t_sh = pb.cnt["T"]
                    s_prev_th = pb.emit(SCA, lambda e, oc=oc, th=th, bank=bank:
                                        e.activation(
                        bbv(OFF_R2 + oc * 1024 + th * 512, 512), bank[:],
                        ACT.Identity, bias=bsu_sb[:, oc:oc + 1]),
                        [("T", t_sh)], inc="S")
                sh_done[oc] = s_prev_th
                if oc + 1 < 16:
                    load_su((oc + 1, 0), [("S", s_prev_th)])
                    load_su((oc + 1, 1), [])
            S_SH_END = pb.cnt["S"]
            if dump and j == 0:
                dma_sync("SO", sh_dump, bbv(OFF_R2, 16384), [("S", S_SH_END)])
                dump_barrier()
            wsa_sb = bbv(OFF_R4 + 2048, 8192)
            z_done = {}
            z_sil = {}
            for g2 in range(2):
                for e2c in range(8):
                    zc = g2 * 8 + e2c
                    bank = P[zc % 2]
                    s_prev_th = None
                    for th in range(2):
                        w = []
                        if th == 0:
                            if zc == 0:
                                w = [("LB_G", lg_sa), ("S", S_SH_END)]
                            elif zc >= 2:
                                w = [("S", z_sil[zc - 2])]
                        else:
                            w = [("S", s_prev_th)]
                        for ec in range(8):
                            pb.emit(TEN, lambda e, ec=ec, e2c=e2c, g2=g2, th=th,
                                    bank=bank: e.matmul(
                                bank[:],
                                wsa_sb[:, ec * 1024 + e2c * 128:ec * 1024 + (e2c + 1) * 128],
                                bbv(OFF_R2 + (g2 * 8 + ec) * 1024 + th * 512, 512),
                                start=(ec == 0), stop=(ec == 7)),
                                w if ec == 0 else (), inc="T" if ec == 7 else None)
                        t_a = pb.cnt["T"]
                        zslot = bbv(OFF_R3 + zc * 1024 + th * 512, 512)
                        s_prev_th = pb.emit(SCA, lambda e, e2c=e2c, bank=bank,
                                            zs=zslot: e.activation(
                            zs, bank[:], ACT.Silu,
                            bias=bsa_sb[:, e2c:e2c + 1]), [("T", t_a)], inc="S")
                        z_done[zc] = pb.emit(
                            VEC, lambda e, zc=zc, th=th, zs=zslot: e.tensor_mul(
                                zs, zs,
                                bbv(OFF_R2 + zc * 1024 + th * 512, 512)),
                            [("S", s_prev_th)], inc="V")
                    z_sil[zc] = s_prev_th
            V_Z_END = pb.cnt["V"]
            if dump and j == 0:
                ZSO = dma_sync("SO", z_dump, bbv(OFF_R3, 16384), [("V", V_Z_END)])
                dump_barrier()
            sd_evac = {}
            lg_sd = {}

            def load_sd(hq, w):
                lg_sd[hq] = dma_cast(
                    f"LWd{hq % 2}",
                    bbv(OFF_R2 + (hq % 2) * 8192, 8192), di["wsd_r2"][hq], w)

            w0 = [("V", V_Z_END), ("S", S_SH_END)]
            if DSO:
                w0.append(("SO", DSO))
            load_sd(0, w0)
            load_sd(1, [])
            for hq in range(4):
                wsdq = bbv(OFF_R2 + (hq % 2) * 8192, 8192)
                for quad in range(2):
                    for zc in range(16):
                        for tl in range(4):
                            tile = quad * 4 + tl
                            w = []
                            if zc == 0 and tl == 0:
                                w = [(f"LWd{hq % 2}", lg_sd[hq]), ("V", V_Z_END)]
                                pk = (hq, 0, 3) if quad == 1 else (hq - 1, 1, 3)
                                if pk in sd_evac:
                                    w.append(("V", sd_evac[pk]))
                            pb.emit(TEN, lambda e, zc=zc, tile=tile, tl=tl,
                                    wq=wsdq: e.matmul(
                                P[tl][:],
                                bbv(OFF_R3 + zc * 1024 + tile * 128, 128),
                                wq[:, zc * 512:(zc + 1) * 512],
                                start=(zc == 0), stop=(zc == 15)),
                                w, inc="T" if zc == 15 else None)
                    t_sd = pb.cnt["T"]
                    for tl in range(4):
                        tile = quad * 4 + tl
                        sd_evac[(hq, quad, tl)] = pb.emit(
                            VEC, lambda e, tile=tile, hq=hq, tl=tl: e.tensor_add(
                                fbv(OFF_ACC + tile * 2048 + hq * 512, 512),
                                fbv(OFF_ACC + tile * 2048 + hq * 512, 512),
                                P[tl][:]), [("T", t_sd)], inc="V")
                if hq + 2 < 4:
                    load_sd(hq + 2, [("T", pb.cnt["T"])])
            V_SD_END = pb.cnt["V"]
            DSO2 = None
            if dump and j == 0:
                DSO2 = dma_sync("SO", accd[1], fbv(OFF_ACC, 16384),
                                [("V", V_SD_END)])
                dump_barrier()

            ex_evac_last = V_SD_END
            prev_h_readers_T = pb.cnt["T"]
            for n in range(NE):
                h_done = {}
                lg_up = {}

                def load_up(ec, w, n=n):
                    lg_up[ec] = dma_cast(
                        f"LWa{ec % 2}",
                        bbv(OFF_R4 + (ec % 2) * 2048, 2048),
                        di["wup_r"][n][ec], w)

                w0 = [("T", prev_h_readers_T), ("V", ex_evac_last)]
                if n == 0 and DSO2:
                    w0.append(("SO", DSO2))
                load_up(0, w0)
                load_up(1, [])
                for ec in range(8):
                    bank = P[ec % 2]
                    slab = bbv(OFF_R4 + (ec % 2) * 2048, 2048)
                    s_prev_th = None
                    for th in range(2):
                        w = []
                        if th == 0:
                            w = [(f"LWa{ec % 2}", lg_up[ec])]
                            if ec == 0:
                                w.append(("V", ex_evac_last))
                            if ec >= 2:
                                w.append(("S", h_done[ec - 2]))
                        else:
                            w = [("S", s_prev_th)]
                        for fc in range(16):
                            pb.emit(TEN, lambda e, fc=fc, th=th, bank=bank,
                                    sl=slab: e.matmul(
                                bank[:], sl[:, fc * 128:(fc + 1) * 128],
                                xtt_rhs(fc, th), start=(fc == 0), stop=(fc == 15)),
                                w if fc == 0 else (), inc="T" if fc == 15 else None)
                        t_h = pb.cnt["T"]
                        s_prev_th = pb.emit(SCA, lambda e, n=n, ec=ec, th=th,
                                            bank=bank: e.activation(
                            bbv(OFF_R2 + ec * 1024 + th * 512, 512), bank[:],
                            ACT.Identity, bias=bup_sb[:, n * 8 + ec:n * 8 + ec + 1]),
                            [("T", t_h)], inc="S")
                    h_done[ec] = s_prev_th
                    if ec + 2 < 8:
                        load_up(ec + 2, [("T", pb.cnt["T"])])
                S_H_END = pb.cnt["S"]
                prev_h_readers_T = pb.cnt["T"]
                u_done = {}
                u_sil = {}
                lg_sw = {}

                def load_sw(e2c, w, n=n):
                    lg_sw[e2c] = dma_cast(
                        f"LWb{e2c % 2}",
                        bbv(OFF_R4 + 4096 + (e2c % 2) * 1024, 1024),
                        di["wsw_r"][n][e2c], w)

                load_sw(0, [("V", ex_evac_last)])
                load_sw(1, [])
                for e2c in range(8):
                    bank = P[2 + (e2c % 2)]
                    slab = bbv(OFF_R4 + 4096 + (e2c % 2) * 1024, 1024)
                    s_prev_th = None
                    for th in range(2):
                        w = []
                        if th == 0:
                            w = [(f"LWb{e2c % 2}", lg_sw[e2c]), ("S", S_H_END)]
                            if e2c >= 2:
                                w.append(("S", u_sil[e2c - 2]))
                        else:
                            w = [("S", s_prev_th)]
                        for ec in range(8):
                            pb.emit(TEN, lambda e, ec=ec, th=th, bank=bank,
                                    sl=slab: e.matmul(
                                bank[:], sl[:, ec * 128:(ec + 1) * 128],
                                bbv(OFF_R2 + ec * 1024 + th * 512, 512),
                                start=(ec == 0), stop=(ec == 7)),
                                w if ec == 0 else (), inc="T" if ec == 7 else None)
                        t_u = pb.cnt["T"]
                        uslot = bbv(OFF_R3 + e2c * 1024 + th * 512, 512)
                        s_prev_th = pb.emit(SCA, lambda e, n=n, e2c=e2c, bank=bank,
                                            us=uslot: e.activation(
                            us, bank[:], ACT.Silu,
                            bias=bsw_sb[:, n * 8 + e2c:n * 8 + e2c + 1]),
                            [("T", t_u)], inc="S")
                        u_done[e2c] = pb.emit(VEC, lambda e, e2c=e2c, th=th,
                                              us=uslot: e.tensor_mul(
                            us, us,
                            bbv(OFF_R2 + e2c * 1024 + th * 512, 512)),
                            [("S", s_prev_th)], inc="V")
                    u_sil[e2c] = s_prev_th
                    if e2c + 2 < 8:
                        load_sw(e2c + 2, [("T", pb.cnt["T"])])
                V_U_END = pb.cnt["V"]
                dn_evac = {}
                lg_dn = {}

                def load_dn(hq, w, n=n):
                    lg_dn[hq] = dma_cast(
                        f"LWc{hq % 2}",
                        bbv(OFF_R2 + (hq % 2) * 4096, 4096),
                        di["wdn_r"][n][hq], w)

                load_dn(0, [("V", V_U_END)])
                load_dn(1, [("V", V_U_END)])
                for hq in range(4):
                    wdnq = bbv(OFF_R2 + (hq % 2) * 4096, 4096)
                    for quad in range(2):
                        for ec in range(8):
                            for tl in range(4):
                                tile = quad * 4 + tl
                                w = []
                                if ec == 0 and tl == 0:
                                    w = [(f"LWc{hq % 2}", lg_dn[hq]), ("V", V_U_END)]
                                    pk = (hq, 0, 3) if quad == 1 else (hq - 1, 1, 3)
                                    if pk in dn_evac:
                                        w.append(("V", dn_evac[pk]))
                                pb.emit(TEN, lambda e, ec=ec, tile=tile, tl=tl,
                                        wq=wdnq: e.matmul(
                                    P[tl][:],
                                    bbv(OFF_R3 + ec * 1024 + tile * 128, 128),
                                    wq[:, ec * 512:(ec + 1) * 512],
                                    start=(ec == 0), stop=(ec == 7)),
                                    w, inc="T" if ec == 7 else None)
                        t_dn = pb.cnt["T"]
                        for tl in range(4):
                            tile = quad * 4 + tl
                            dn_evac[(hq, quad, tl)] = pb.emit(
                                VEC, lambda e, tile=tile, hq=hq, n=n, tl=tl:
                                e.scalar_tensor_tensor(
                                    fbv(OFF_ACC + tile * 2048 + hq * 512, 512),
                                    P[tl][:],
                                    w_all[:, tile * 8 + n:tile * 8 + n + 1],
                                    fbv(OFF_ACC + tile * 2048 + hq * 512, 512),
                                    ALU.mult, ALU.add), [("T", t_dn)], inc="V")
                    if hq + 2 < 4:
                        load_dn(hq + 2, [("T", pb.cnt["T"])])
                ex_evac_last = pb.cnt["V"]
                prev_h_readers_T = pb.cnt["T"]

            DSO3 = None
            if dump and j == 0:
                DSO3 = dma_sync("SO", accd[2], fbv(OFF_ACC, 16384),
                                [("V", ex_evac_last)])
                dump_barrier()
            comb_v = {}
            comb_trans = {}
            comb_st = {}
            for tile in range(NT_OWN):
                xslot = fbv(OFF_XF + (tile % 2) * 2048, 2048)
                w = [("V", ex_evac_last), ("SO", D_SO_END)]
                if DSO3:
                    w.append(("SO", DSO3))
                if tile >= 2:
                    w.append(("V", comb_v[tile - 2]))
                    w.append(("SO", pb.cnt["SO"]))
                lh_res = dma_sync(f"LX{tile % 2}", xslot, res_scr.rearrange(
                    "(t p) h -> t p h", p=128)[tile], w)
                accap = fbv(OFF_ACC + tile * 2048, 2048)
                v_add = pb.emit(VEC, lambda e, a=accap, xs=xslot: e.tensor_add(
                    a, a, xs), [(f"LX{tile % 2}", lh_res)], inc="V")
                if j == 0:
                    xlslot = bbv(OFF_XLN_DE + (tile % 2) * 2048, 2048)
                    w = [("V", v_add)]
                    if tile >= 2:
                        w.append(("T", comb_trans[tile - 2]))
                    s_c = pb.emit(SCA, lambda e, a=accap, xl=xlslot:
                                  e.activation(xl, a, ACT.Copy), w, inc="S")
                    comb_v[tile] = emit_transpose16(
                        lambda fc, xl=xlslot: xl[:, fc * 128:(fc + 1) * 128],
                        OFF_XTT + tile * 2048, [("S", s_c)])
                    comb_trans[tile] = pb.cnt["T"]
                else:
                    lh_x2 = dma_sync(f"LX{tile % 2}", xslot, x2_scr.rearrange(
                        "(t p) h -> t p h", p=128)[tile], [("V", v_add)])
                    v_f = pb.emit(VEC, lambda e, a=accap, xs=xslot: e.tensor_add(
                        a, a, xs), [(f"LX{tile % 2}", lh_x2)], inc="V")
                    comb_v[tile] = v_f
                    comb_st[tile] = dma_sync(
                        "SO", out_own.rearrange("(t p) h -> t p h", p=128)[tile],
                        accap, [("V", v_f)])
            prev_iter_T = pb.cnt["T"]
            prev_iter_V = pb.cnt["V"]
            prev_iter_SO = pb.cnt["SO"]

    pb.wait("sync", [(s, pb.cnt[s]) for s in SYNC_SEMS if pb.cnt[s]])
    pb.wait("gpsimd", [(s, pb.cnt[s]) for s in GPS_SEMS if pb.cnt[s]])
    pb.run()
    return nc


def _own_tiles(i):
    return [4 * s + i if s < 4 else 4 * s + 3 - i for s in range(8)]


def _make_cmask(i):
    m = np.zeros((2, 128, 512), np.float32)
    tri = (np.arange(128)[:, None] <= np.arange(128)[None, :]).astype(np.float32)
    for side, r in enumerate((i, 3 - i)):
        for jj in range(4):
            blk = m[side, :, jj * 128:(jj + 1) * 128]
            if jj < r:
                blk[:] = 1.0
            elif jj == r:
                blk[:] = tri
    return m


def make_in_maps(x, w_qkv, w_o, Wg, bg, Wup, bup, Wsw, bsw, Wdn, bdn,
                 Wsu, bsu, Wsa, bsa, Wsd, bsd):
    f = lambda a: np.ascontiguousarray(np.asarray(a, dtype=np.float32))
    x = f(x); w_qkv = f(w_qkv)
    wqT = np.ascontiguousarray(w_qkv[0:H].T)
    wkT = np.ascontiguousarray(w_qkv[H:2 * H].T)
    wvT = np.ascontiguousarray(w_qkv[2 * H:3 * H].T)
    woT = np.ascontiguousarray(f(w_o).T)
    c = np.ascontiguousarray
    shared = dict(
        wq_r=c(wqT.reshape(16, 128, 16, 128).transpose(2, 1, 0, 3).reshape(16, 128, 2048)),
        wk_r=c(wkT.reshape(16, 128, 2048).transpose(1, 0, 2).reshape(128, 32768)),
        wv_r=c(wvT.reshape(16, 128, 2048).transpose(1, 0, 2).reshape(128, 32768)),
        wo_r=c(woT.reshape(16, 128, 4, 512).transpose(2, 1, 0, 3).reshape(4, 128, 8192)),
        wg_in=c(f(Wg).reshape(2, 16, 128, 8).transpose(2, 0, 1, 3).reshape(128, 256)),
        bg_in=f(bg).reshape(1, 16),
        wup_r=c(f(Wup).reshape(6, 16, 128, 8, 128).transpose(0, 3, 2, 1, 4).reshape(6, 8, 128, 2048)),
        wsw_r=c(f(Wsw).reshape(6, 8, 128, 8, 128).transpose(0, 3, 2, 1, 4).reshape(6, 8, 128, 1024)),
        wdn_r=c(f(Wdn).reshape(6, 8, 128, 4, 512).transpose(0, 3, 2, 1, 4).reshape(6, 4, 128, 4096)),
        wsu_r2=c(f(Wsu).reshape(16, 128, 16, 128).transpose(2, 1, 0, 3).reshape(16, 128, 2048)),
        wsa_r2=c(f(Wsa).reshape(8, 128, 1024).transpose(1, 0, 2).reshape(128, 8192)),
        wsd_r2=c(f(Wsd).reshape(16, 128, 4, 512).transpose(2, 1, 0, 3).reshape(4, 128, 8192)),
        bup_r=np.ascontiguousarray(
            f(bup).reshape(6, 8, 128).transpose(2, 0, 1).reshape(128, 48)),
        bsw_r=np.ascontiguousarray(
            f(bsw).reshape(6, 8, 128).transpose(2, 0, 1).reshape(128, 48)),
        bsu_r=np.ascontiguousarray(f(bsu).reshape(16, 128).T),
        bsa_r=np.ascontiguousarray(f(bsa).reshape(8, 128).T),
        bdnsd=np.ascontiguousarray(
            np.concatenate([f(bdn), f(bsd)[None]], 0)),
    )
    in_maps = []
    for c in range(8):
        b, i = c // 4, c % 4
        m = dict(shared)
        m["x_own"] = np.ascontiguousarray(
            x[b].reshape(NT_SEQ, 128, H)[_own_tiles(i)].reshape(1024, H))
        m["x_seq"] = x[b]
        m["cmask"] = _make_cmask(i)
        in_maps.append(m)
    return in_maps


def kernel(L0_attn_hidden_states, L0_attn_w_qkv, L0_attn_w_o,
           L0_ffn_Wg, L0_ffn_bg, L0_ffn_Wup, L0_ffn_bup, L0_ffn_Wsw, L0_ffn_bsw,
           L0_ffn_Wdn, L0_ffn_bdn, L0_ffn_Wsu, L0_ffn_bsu, L0_ffn_Wsa,
           L0_ffn_bsa, L0_ffn_Wsd, L0_ffn_bsd):
    in_maps = make_in_maps(L0_attn_hidden_states, L0_attn_w_qkv, L0_attn_w_o,
                           L0_ffn_Wg, L0_ffn_bg, L0_ffn_Wup, L0_ffn_bup,
                           L0_ffn_Wsw, L0_ffn_bsw, L0_ffn_Wdn, L0_ffn_bdn,
                           L0_ffn_Wsu, L0_ffn_bsu, L0_ffn_Wsa, L0_ffn_bsa,
                           L0_ffn_Wsd, L0_ffn_bsd)
    nc = build()
    res = run_bass_kernel_spmd(nc, in_maps, core_ids=list(range(8)))
    out = np.empty((2, SEQ, H), np.float32)
    for c in range(8):
        b, i = c // 4, c % 4
        oc = np.asarray(res.results[c]["out_own"]).reshape(NT_OWN, 128, H)
        for s, t in enumerate(_own_tiles(i)):
            out[b, t * 128:(t + 1) * 128, :] = oc[s]
    return out
